# version 07
# baseline (speedup 1.0000x reference)
import numpy as np
import ml_dtypes
from contextlib import ExitStack

import concourse.bass as bass
import concourse.mybir as mybir
from concourse.bass_utils import run_bass_kernel_spmd

F32 = mybir.dt.float32
BF16 = mybir.dt.bfloat16
AF = mybir.ActivationFunctionType
ALU = mybir.AluOpType
AX = mybir.AxisListType

P = 128
C = 1024
NG = 8
NH = 16
HD = 64
T = 2048
HALO = 4
TP = T + 2 * HALO
R0, R1 = HALO, HALO + T
HID = 4096
NGH = 32

SUBS = [(0, 512), (512, 512), (1024, 512), (1536, 512), (2048, 8)]
RSUBS = [(4, 512), (516, 512), (1028, 512), (1540, 512)]

BF = ml_dtypes.bfloat16


def _rope_lhst():
    k_max = C // 2
    theta = 1.0 / 10000.0 ** (np.arange(k_max, dtype=np.float64) / k_max)
    cs, sn = np.cos(theta), np.sin(theta)
    out = np.zeros((NG, P, P), np.float32)
    for g in range(NG):
        for jl in range(64):
            j = 64 * g + jl
            out[g, 2 * jl, 2 * jl] = cs[j]
            out[g, 2 * jl + 1, 2 * jl] = -sn[j]
            out[g, 2 * jl, 2 * jl + 1] = sn[j]
            out[g, 2 * jl + 1, 2 * jl + 1] = cs[j]
    return out.astype(BF)


def _zmask():
    out = np.zeros((NG, P, NH), np.float32)
    for g in range(NG):
        for p in range(P):
            out[g, p, (128 * g + p) // HD] = 1.0 / 8192.0
    return out.astype(BF)


def _zsel():
    out = np.zeros((NG, NH, P), np.float32)
    for g in range(NG):
        for p in range(P):
            out[g, (128 * g + p) // HD, p] = 1.0
    return out



ENGS = ("pe", "act", "dve", "pool", "sp")


class Op:
    __slots__ = ("idx", "eng", "kind", "emit", "reads", "writes", "token",
                 "need_inc", "deps", "tag")

    def __init__(self, idx, eng, kind, emit, reads, writes, tag=""):
        self.idx = idx
        self.eng = eng
        self.kind = kind
        self.emit = emit
        self.reads = reads
        self.writes = writes
        self.token = None
        self.need_inc = False
        self.deps = []
        self.tag = tag


class Stream:

    def __init__(self, bld, name, eng, n_sems=4):
        self.bld = bld
        self.name = name
        self.eng = eng
        self.n_sems = n_sems
        self.sems = [f"{name}{i}" for i in range(n_sems)]
        self.counts = [0] * n_sems
        self.wave_i = 0
        self.prev_wave_final = {}

    def wave(self, dmas):
        si = self.wave_i % self.n_sems
        sem = self.sems[si]
        if self.wave_i >= self.n_sems:
            old = self.prev_wave_final[si]
            self.bld.raw_wait(self.eng, sem, old)
        for emit, reads, writes in dmas:
            self.counts[si] += 16
            val = self.counts[si]
            op = self.bld.op(self.eng, emit, reads, writes, kind="d",
                             tag=f"{self.name}w{self.wave_i}")
            op.token = (sem, val)
        self.prev_wave_final[si] = self.counts[si]
        tok = (sem, self.counts[si])
        self.wave_i += 1
        return tok


class Builder:
    def __init__(self, nc, sim_sync=False, hw_same_engine_sync=True):
        self.nc = nc
        self.sim_sync = sim_sync
        self.hw_same_engine_sync = hw_same_engine_sync
        self.ops = []
        self.writers = []
        self.readers = []
        self.sem_names = set()
        self.streams = {}
        self.raw_waits = {}
        self.ar_base = 16416
        self.ar_top = 229344
        self.free_list = [(self.ar_base, self.ar_top)]
        self.tensors = {}
        self.sb_handles = {}

    def alloc(self, name, shape, dtype, align=64, top=None):
        size = int(np.prod(shape[1:])) * mybir.dt.size(dtype)
        size_al = -(-size // align) * align
        from_top = (size_al < 20480) if top is None else top
        blocks = list(enumerate(self.free_list))
        if from_top:
            blocks = reversed(blocks)
        for i, (a, b) in blocks:
            a_al = -(-a // align) * align
            if b - a_al >= size_al:
                if from_top:
                    addr = (b - size_al) // align * align
                    if addr < a:
                        continue
                else:
                    addr = a_al
                pieces = []
                if a < addr:
                    pieces.append((a, addr))
                if addr + size_al < b:
                    pieces.append((addr + size_al, b))
                self.free_list[i:i + 1] = pieces
                h = self.nc.alloc_sbuf_tensor_at(name, list(shape), dtype, offset=addr)
                self.tensors[h.name] = (h, addr, size_al)
                return h
        live = sorted((a, a + s, n) for n, (_, a, s) in self.tensors.items())
        raise MemoryError(f"SBUF arena full allocating {name} ({size_al}B/part); "
                          f"free={self.free_list}; live={live}")

    def free(self, h):
        name = h.name
        _, addr, size = self.tensors.pop(name)
        self.free_list.append((addr, addr + size))
        self.free_list.sort()
        merged = []
        for a, b in self.free_list:
            if merged and merged[-1][1] >= a:
                merged[-1] = (merged[-1][0], max(merged[-1][1], b))
            else:
                merged.append((a, b))
        self.free_list = [tuple(x) for x in merged]

    def rs(self, h, c0=None, c1=None, p0=0, p1=None):
        name = h.name if hasattr(h, "name") else h
        hh, addr, _ = self.tensors[name]
        sz = mybir.dt.size(hh.dtype)
        ncols = hh.shape[1]
        if c0 is None:
            c0, c1 = 0, ncols
        if p1 is None:
            p1 = hh.shape[0]
        return ("S", p0, p1, addr + c0 * sz, addr + c1 * sz)

    @staticmethod
    def rp(bank_name):
        return ("P", bank_name)

    @staticmethod
    def rd(dram_handle):
        return ("D", dram_handle.name if hasattr(dram_handle, "name") else dram_handle)

    @staticmethod
    def _overlap(r1, r2):
        if r1[0] != r2[0]:
            return False
        if r1[0] == "S":
            return not (r1[2] <= r2[1] or r2[2] <= r1[1] or
                        r1[4] <= r2[3] or r2[4] <= r1[3])
        return r1[1] == r2[1]

    def op(self, eng, emit, reads=(), writes=(), kind="c", tag=""):
        o = Op(len(self.ops), eng, kind, emit, list(reads), list(writes), tag)
        deps = {}
        for r in o.reads:
            for res, w in self.writers:
                if self._overlap(res, r):
                    deps[w.idx] = w
        for w_ in o.writes:
            for res, w in self.writers:
                if self._overlap(res, w_):
                    deps[w.idx] = w
            for res, rdr in self.readers:
                if self._overlap(res, w_):
                    deps[rdr.idx] = rdr
        for d in deps.values():
            if d is o:
                continue
            if d.eng == o.eng and d.kind == "c" and o.kind == "c":
                if not self.sim_sync:
                    if o.eng == "pe" or not self.hw_same_engine_sync:
                        continue
            o.deps.append(d)
        for w_ in o.writes:
            self.writers = [(res, w) for (res, w) in self.writers
                            if not self._covered(res, w_)]
            self.readers = [(res, r) for (res, r) in self.readers
                            if not self._covered(res, w_)]
            self.writers.append((w_, o))
        for r in o.reads:
            self.readers.append((r, o))
        self.ops.append(o)
        return o

    @staticmethod
    def _covered(res, w):
        if res[0] != w[0]:
            return False
        if res[0] == "S":
            return (w[1] <= res[1] and res[2] <= w[2] and
                    w[3] <= res[3] and res[4] <= w[4])
        return res[1] == w[1]

    def raw_wait(self, eng, sem, val):
        o = Op(len(self.ops), eng, "w", None, [], [], "gate")
        o.token = (sem, val)
        self.ops.append(o)

    def stream(self, name, eng, n_sems=4):
        s = Stream(self, name, eng, n_sems)
        self.streams[name] = s
        return s

    def finalize(self, eng_objs, final_eng="sp"):
        nc = self.nc
        for o in self.ops:
            for d in o.deps:
                if d.token is None:
                    d.need_inc = True
        eng_counts = {e: 0 for e in ENGS}
        for o in self.ops:
            if o.kind == "c" and o.need_inc:
                eng_counts[o.eng] += 1
                o.token = (f"eng_{o.eng}", eng_counts[o.eng])
        sems = {}
        for o in self.ops:
            if o.token is not None:
                sems.setdefault(o.token[0], 0)
                sems[o.token[0]] = max(sems[o.token[0]], o.token[1])
        sched = {e: [] for e in ENGS}
        waited = {e: {} for e in ENGS}
        for o in self.ops:
            if o.kind == "w":
                sem, val = o.token
                if waited[o.eng].get(sem, 0) < val:
                    sched[o.eng].append(("wait", sem, val))
                    waited[o.eng][sem] = val
                continue
            for d in o.deps:
                sem, val = d.token
                if waited[o.eng].get(sem, 0) < val:
                    sched[o.eng].append(("wait", sem, val))
                    waited[o.eng][sem] = val
            sched[o.eng].append(("inst", o))
        final_waits = []
        for sem, top in sems.items():
            if waited[final_eng].get(sem, 0) < top:
                final_waits.append((sem, top))
        sem_handles = {}
        with ExitStack() as ctx:
            for sem in sems:
                sem_handles[sem] = ctx.enter_context(nc.semaphore(sem))
            blk = ctx.enter_context(nc.Block())

            def run_engine(ename):
                def body(eobj):
                    for item in sched[ename]:
                        if item[0] == "wait":
                            eobj.wait_ge(sem_handles[item[1]], item[2])
                        else:
                            o = item[1]
                            inst = o.emit(eobj)
                            if o.token is not None:
                                sem, val = o.token
                                prev = getattr(o, "_prev_val", None)
                                amt = 16 if o.kind == "d" else 1
                                inst.then_inc(sem_handles[sem], amt)
                    if ename == final_eng:
                        for sem, top in final_waits:
                            eobj.wait_ge(sem_handles[sem], top)
                return body

            blk.tensor(run_engine("pe"))
            blk.scalar(run_engine("act"))
            blk.vector(run_engine("dve"))
            blk.gpsimd(run_engine("pool"))
            blk.sync(run_engine("sp"))
        return dict(n_ops=len(self.ops), sems=len(sems))



SMALLS = {}


def _smalls_layout():
    cols = 0
    spec = {}
    def add(name, n):
        nonlocal cols
        spec[name] = (cols, n)
        cols += n
    add("g1", NG); add("b1", NG); add("g2", NG); add("b2", NG)
    for cv in ("cpe1", "dwc", "lepe", "cpe2"):
        add(cv + "_w", 3 * NG)
        add(cv + "_b", NG)
    add("in_b", NG); add("actp_b", NG); add("qk_b", 2 * NG); add("out_b", NG)
    add("fc1_b", NGH); add("fc2_b", NG)
    add("w1_s", NGH); add("w2_s", NGH); add("h1_s", NGH); add("h2_s", NGH)
    add("w3_s", NG)
    add("lmask", HALO); add("rmask", HALO)
    add("bsel", 2)
    return spec, cols


SMALLS, SMALLS_COLS = _smalls_layout()


def build_program(upto="B4", fake_cc=False, sim_sync=False, hw_same_engine_sync=True):
    nc = bass.Bass("TRN2", target_bir_lowering=False)
    bld = Builder(nc, sim_sync=sim_sync, hw_same_engine_sync=hw_same_engine_sync)
    phases_all = ["PH1", "PH2", "PH3", "PH4", "PH5", "PH6", "PH7", "PH8",
                  "PH9", "F1", "F2", "F3", "F4", "B0", "B1", "B2", "B3", "B4"]
    phases = phases_all[:phases_all.index(upto) + 1]

    dt_in = {}
    def din(name, shape, dtype=F32):
        dt_in[name] = nc.dram_tensor(name, list(shape), dtype, kind="ExternalInput")
        return dt_in[name]

    u_t = din("u_t", [P, NG * TP])
    smalls = din("smalls", [P, SMALLS_COLS])
    w_in = din("w_in", [P, NG * C])
    w_actp = din("w_actp", [P, NG * C])
    w_qk = din("w_qk", [P, NG * 2 * C])
    w_out = din("w_out", [P, NG * C])
    w_fc1 = din("w_fc1", [NGH, P, NG * P])
    w_fc2 = din("w_fc2", [NG, P, NGH * P])
    w_w1 = din("w_w1", [NGH, P, NG * P])
    w_w2 = din("w_w2", [NGH, P, NG * P])
    w_h1 = din("w_h1", [NGH, P, NGH * P])
    w_h2 = din("w_h2", [NGH, P, NGH * P])
    w_w3 = din("w_w3", [NG, P, NGH * P])

    out_t = nc.dram_tensor("out_t", [P, NG * T], F32, kind="ExternalOutput")

    a_sp = nc.dram_tensor("a_sp", [P, NG * TP], F32, kind="Internal")
    q_sp = nc.dram_tensor("q_sp", [P, NG * TP], BF16, kind="Internal")
    ar_sp = nc.dram_tensor("ar_sp", [P, NG * TP], BF16, kind="Internal")
    lp_sp = nc.dram_tensor("lp_sp", [P, NG * TP], BF16, kind="Internal")
    x2_sp = nc.dram_tensor("x2_sp", [P, NG * T], F32, kind="Internal")
    h0_sp = nc.dram_tensor("h0_sp", [P, NGH * T], BF16, kind="Internal")
    h1_sp = nc.dram_tensor("h1_sp", [P, NGH * T], BF16, kind="Internal")
    h2_sp = nc.dram_tensor("h2_sp", [P, NGH * T], BF16, kind="Internal")
    CCN = NG * P + NG
    cc_in = nc.dram_tensor("cc_in", [2, P * CCN], F32, kind="Internal")
    cc_out = nc.dram_tensor("cc_out", [2, P * CCN], F32, kind="Internal")

    rope_d = nc.inline_tensor(np.ascontiguousarray(_rope_lhst().reshape(NG * P, P)), "rope_d")
    zmask_d = nc.inline_tensor(np.ascontiguousarray(_zmask().reshape(NG * P, NH)), "zmask_d")
    zsel_d = nc.inline_tensor(np.ascontiguousarray(_zsel().reshape(NG * NH, P)), "zsel_d")
    ident_d = nc.inline_tensor(np.eye(P, dtype=BF), "ident_d")
    onesc_d = nc.inline_tensor(np.ones((P, 1), np.float32), "onesc_d")
    onesr_d = nc.inline_tensor(np.ones((1, P), np.float32), "onesr_d")

    psum = {}
    with ExitStack() as psctx:
        for nm, dt in [("pmm0", F32), ("pmm1", F32), ("pmm2", F32), ("pst", F32),
                       ("pkv0", F32), ("pkv1", F32), ("ptra", BF16), ("ptrb", BF16)]:
            psum[nm] = psctx.enter_context(nc.psum_tensor(nm, [P, 512], dt))
        _build_body(nc, bld, phases, fake_cc, locals())
        meta = bld.finalize(None)
    return nc, meta


def _build_body(nc, bld, phases, fake_cc, env):
    psum = env["psum"]
    u_t, smalls = env["u_t"], env["smalls"]
    out_t = env["out_t"]
    a_sp, q_sp, ar_sp, lp_sp = env["a_sp"], env["q_sp"], env["ar_sp"], env["lp_sp"]
    x2_sp, h1_sp, h2_sp = env["x2_sp"], env["h1_sp"], env["h2_sp"]
    cc_in, cc_out = env["cc_in"], env["cc_out"]
    CCN = env["CCN"]

    st_w = bld.stream("sw", "pool", 4)
    st_io = bld.stream("sio", "sp", 4)
    st_h = bld.stream("sh", "act", 4)
    st_cc = bld.stream("scc", "pool", 2)

    R = bld.rs
    RP = bld.rp
    RD = bld.rd

    cst = {}
    cst["ones_c"] = bld.alloc("ones_c", [P, 1], F32)
    cst["ones_r"] = bld.alloc("ones_r", [1, P], F32)
    cst["ident"] = bld.alloc("ident", [P, P], BF16)
    cst["rope"] = bld.alloc("rope", [P, NG * P], BF16)
    cst["zmask"] = bld.alloc("zmask", [P, NG * NH], BF16)
    cst["zsel"] = bld.alloc("zsel", [NH, NG * P], F32)
    cst["sm"] = bld.alloc("sm", [P, SMALLS_COLS], F32)

    init_dmas = []
    for dname, key, hsl in [("onesc_d", "ones_c", None), ("onesr_d", "ones_r", None),
                            ("ident_d", "ident", None), ("rope_d", "rope", "g"),
                            ("zmask_d", "zmask", "g"), ("zsel_d", "zsel", "g16")]:
        dh = env[dname] if dname in env else None
    def dma(stream, out_ap, in_ap, reads, writes, tag=""):
        return stream.wave([(lambda e, o=out_ap, i=in_ap: e.dma_start(out=o, in_=i),
                             reads, writes)])

    rope_d, zmask_d, zsel_d = env["rope_d"], env["zmask_d"], env["zsel_d"]
    ident_d, onesc_d, onesr_d = env["ident_d"], env["onesc_d"], env["onesr_d"]

    dma(st_io, cst["ones_c"][:], onesc_d.ap(), [RD(onesc_d)], [R(cst["ones_c"])])
    dma(st_io, cst["ones_r"][:], onesr_d.ap(), [RD(onesr_d)], [R(cst["ones_r"])])
    dma(st_io, cst["ident"][:], ident_d.ap(), [RD(ident_d)], [R(cst["ident"])])
    dma(st_io, cst["rope"][:].rearrange("p (g f) -> p g f", g=NG),
        rope_d.ap().rearrange("(g p) f -> p g f", g=NG),
        [RD(rope_d)], [R(cst["rope"])])
    dma(st_io, cst["zmask"][:].rearrange("p (g f) -> p g f", g=NG),
        zmask_d.ap().rearrange("(g p) f -> p g f", g=NG),
        [RD(zmask_d)], [R(cst["zmask"])])
    dma(st_io, cst["zsel"][:].rearrange("p (g f) -> p g f", g=NG),
        zsel_d.ap().rearrange("(g p) f -> p g f", g=NG),
        [RD(zsel_d)], [R(cst["zsel"])])
    dma(st_io, cst["sm"][:], smalls.ap(), [RD(smalls)], [R(cst["sm"])])

    def smcol(name, i=0):
        off, n = SMALLS[name]
        assert i < n
        return cst["sm"][:, off + i:off + i + 1]

    def smres(name):
        off, n = SMALLS[name]
        return R(cst["sm"], off, off + n)

    sm = cst["sm"]

    def mm(bank, bc0, bc1, lhsT_ap, rhs_ap, reads, start, stop, tag=""):
        ps = psum[bank]
        bld.op("pe",
               lambda e, o=ps[:, bc0:bc1], l=lhsT_ap, r=rhs_ap, st=start, sp=stop:
                   e.matmul(o, l, r, start=st, stop=sp),
               reads=reads + [RP(bank)] if not start else reads,
               writes=[RP(bank)], tag=tag)

    def act(out_ap, in_ap, func, reads, writes, bias=None, scale=None, tag=""):
        kw = {}
        if bias is not None:
            kw["bias"] = bias
        if scale is not None:
            kw["scale"] = scale
        bld.op("act", lambda e, o=out_ap, i=in_ap, f=func, k=kw: e.activation(o, i, f, **k),
               reads=reads, writes=writes, tag=tag)

    def dve(meth, args, reads, writes, tag=""):
        bld.op("dve", lambda e, m=meth, a=args: getattr(e, m)(*a),
               reads=reads, writes=writes, tag=tag)

    rows2 = [bld.alloc("rows0", [1, 4 * 512], F32)] * 2
    sq2 = [bld.alloc("sqs0", [P, NG * 512], F32)] * 2
    rbmb2 = [bld.alloc(f"rbmb{i}", [P, 2 * 512], F32) for i in range(2)]
    _ln_pi = [0]

    def ln_pass(src_h, src_base_fn, subs, dst_write, gamma=None, beta=None, tag="ln",
                banks=("pst", "pmm2", "pmm0", "pmm1")):
        bk_sum, bk_sq, bk_rb, bk_mb = banks
        for (c0, w) in subs:
            pi = _ln_pi[0] % 2
            _ln_pi[0] += 1
            rows = rows2[pi]
            sq_sub = sq2[pi]
            rbmb = rbmb2[pi]
            for g in range(NG):
                sap, sres = src_base_fn(g, c0, w)
                act(sq_sub[:, g * 512:g * 512 + w], sap, AF.Square,
                    [sres], [R(sq_sub, g * 512, g * 512 + w)], tag=f"{tag}sq")
            for g in range(NG):
                sap, sres = src_base_fn(g, c0, w)
                ps = psum[bk_sum]
                bld.op("pe", lambda e, o=ps[0:1, 0:w], l=cst["ones_c"][:], r=sap,
                       st=(g == 0), sp=(g == NG - 1): e.matmul(o, l, r, start=st, stop=sp),
                       reads=[sres, R(cst["ones_c"])] + ([] if g == 0 else [RP(bk_sum)]),
                       writes=[RP(bk_sum)], tag=f"{tag}st1")
            for g in range(NG):
                ps = psum[bk_sq]
                bld.op("pe", lambda e, o=ps[0:1, 0:w], l=cst["ones_c"][:],
                       r=sq_sub[:, g * 512:g * 512 + w],
                       st=(g == 0), sp=(g == NG - 1): e.matmul(o, l, r, start=st, stop=sp),
                       reads=[R(sq_sub, g * 512, g * 512 + w), R(cst["ones_c"])] +
                             ([] if g == 0 else [RP(bk_sq)]),
                       writes=[RP(bk_sq)], tag=f"{tag}st2")
            srow = rows[0:1, 0:w]
            vrow = rows[0:1, 512:512 + w]
            rrow = rows[0:1, 1024:1024 + w]
            nmrow = rows[0:1, 1536:1536 + w]
            dve("tensor_scalar", (srow, psum[bk_sum][0:1, 0:w], 1.0 / C, None, ALU.mult),
                [RP(bk_sum)], [R(rows, 0, w, 0, 1)], tag=f"{tag}r1")
            dve("tensor_scalar", (vrow, psum[bk_sq][0:1, 0:w], 1.0 / C, None, ALU.mult),
                [RP(bk_sq)], [R(rows, 512, 512 + w, 0, 1)], tag=f"{tag}r2")
            dve("tensor_tensor", (nmrow, srow, srow, ALU.mult),
                [R(rows, 0, w, 0, 1)], [R(rows, 1536, 1536 + w, 0, 1)], tag=f"{tag}r3")
            dve("tensor_tensor", (vrow, vrow, nmrow, ALU.subtract),
                [R(rows, 512, 512 + w, 0, 1), R(rows, 1536, 1536 + w, 0, 1)],
                [R(rows, 512, 512 + w, 0, 1)], tag=f"{tag}r4")
            dve("tensor_scalar", (vrow, vrow, 1e-5, None, ALU.add),
                [R(rows, 512, 512 + w, 0, 1)], [R(rows, 512, 512 + w, 0, 1)], tag=f"{tag}r5")
            act(vrow, vrow, AF.Sqrt, [R(rows, 512, 512 + w, 0, 1)],
                [R(rows, 512, 512 + w, 0, 1)], tag=f"{tag}r6")
            dve("reciprocal", (rrow, vrow),
                [R(rows, 512, 512 + w, 0, 1)], [R(rows, 1024, 1024 + w, 0, 1)], tag=f"{tag}r7")
            dve("scalar_tensor_tensor", (nmrow, srow, -1.0, rrow, ALU.mult, ALU.mult),
                [R(rows, 0, w, 0, 1), R(rows, 1024, 1024 + w, 0, 1)],
                [R(rows, 1536, 1536 + w, 0, 1)], tag=f"{tag}r8")
            bld.op("pe", lambda e, o=psum[bk_rb][:, 0:w], l=cst["ones_r"][:], r=rrow:
                   e.matmul(o, l, r, start=True, stop=True),
                   reads=[R(rows, 1024, 1024 + w, 0, 1), R(cst["ones_r"])],
                   writes=[RP(bk_rb)], tag=f"{tag}b1")
            bld.op("pe", lambda e, o=psum[bk_mb][:, 0:w], l=cst["ones_r"][:], r=nmrow:
                   e.matmul(o, l, r, start=True, stop=True),
                   reads=[R(rows, 1536, 1536 + w, 0, 1), R(cst["ones_r"])],
                   writes=[RP(bk_mb)], tag=f"{tag}b2")
            dve("tensor_copy", (rbmb[:, 0:w], psum[bk_rb][:, 0:w]),
                [RP(bk_rb)], [R(rbmb, 0, w)], tag=f"{tag}be1")
            dve("tensor_copy", (rbmb[:, 512:512 + w], psum[bk_mb][:, 0:w]),
                [RP(bk_mb)], [R(rbmb, 512, 512 + w)], tag=f"{tag}be2")
            rb_ap = rbmb[:, 0:w]
            mb_ap = rbmb[:, 512:512 + w]
            for g in range(NG):
                dst_write(g, c0, w, rb_ap, mb_ap, [R(rbmb, 0, w), R(rbmb, 512, 512 + w)])

    def conv_pass(src_h, dst_emit, taps, bias, subs, tmp_h, tag="cv"):
        for (c0, w) in subs:
            for g in range(NG):
                base = g * TP
                s_rs = lambda a0, a1: R(src_h, base + a0, base + a1)
                tc = (g % 2) * 512
                t_ap = tmp_h[:, tc:tc + w]
                t_rs = R(tmp_h, tc, tc + w)
                off, _ = SMALLS[taps]
                w0 = sm[:, off + g * 3 + 0:off + g * 3 + 1]
                w1 = sm[:, off + g * 3 + 1:off + g * 3 + 2]
                w2 = sm[:, off + g * 3 + 2:off + g * 3 + 3]
                boff, _ = SMALLS[bias]
                bcol = sm[:, boff + g:boff + g + 1]
                dve("tensor_scalar", (t_ap, src_h[:, base + c0:base + c0 + w],
                                      w1, bcol, ALU.mult, ALU.add),
                    [s_rs(c0, c0 + w), smres(taps), smres(bias)], [t_rs], tag=f"{tag}c")
                lo = max(c0, 1)
                if lo < c0 + w:
                    dve("scalar_tensor_tensor",
                        (tmp_h[:, tc + lo - c0:tc + w], src_h[:, base + lo - 1:base + c0 + w - 1],
                         w0, tmp_h[:, tc + lo - c0:tc + w], ALU.mult, ALU.add),
                        [s_rs(lo - 1, c0 + w - 1), t_rs, smres(taps)], [t_rs], tag=f"{tag}l")
                hi = min(c0 + w, TP - 1)
                if hi > c0:
                    dve("scalar_tensor_tensor",
                        (tmp_h[:, tc:tc + hi - c0], src_h[:, base + c0 + 1:base + hi + 1],
                         w2, tmp_h[:, tc:tc + hi - c0], ALU.mult, ALU.add),
                        [s_rs(c0 + 1, hi + 1), t_rs, smres(taps)], [t_rs], tag=f"{tag}r")
                dst_emit(g, c0, w, t_ap, [t_rs])

    def mask_halo(h, tag=""):
        for g in range(NG):
            base = g * TP
            dve("tensor_tensor", (h[:, base:base + HALO], h[:, base:base + HALO],
                                  sm[:, SMALLS["lmask"][0]:SMALLS["lmask"][0] + HALO],
                                  ALU.mult),
                [R(h, base, base + HALO), smres("lmask")], [R(h, base, base + HALO)],
                tag=f"mskl{tag}")
            dve("tensor_tensor", (h[:, base + TP - HALO:base + TP],
                                  h[:, base + TP - HALO:base + TP],
                                  sm[:, SMALLS["rmask"][0]:SMALLS["rmask"][0] + HALO],
                                  ALU.mult),
                [R(h, base + TP - HALO, base + TP), smres("rmask")],
                [R(h, base + TP - HALO, base + TP)], tag=f"mskr{tag}")

    def wload(dram_h, sb_h, tag=""):
        return dma(st_w, sb_h[:], dram_h.ap(), [RD(dram_h)], [R(sb_h)], tag)

    def proj(w_sb, F_out, x_h, x_base_fn, subs, evac, banks=("pmm0", "pmm1", "pmm2"),
             n_k=NG, tag="pj"):
        bi = 0
        for (c0, w) in subs:
            for oc in range(F_out // P):
                bank = banks[bi % len(banks)]
                bi += 1
                for k in range(n_k):
                    l_ap = w_sb[:, k * F_out + oc * P:k * F_out + (oc + 1) * P]
                    x_ap, x_res = x_base_fn(k, c0, w)
                    ps = psum[bank]
                    bld.op("pe",
                           lambda e, o=ps[:, 0:w], l=l_ap, r=x_ap,
                           st=(k == 0), sp=(k == n_k - 1): e.matmul(o, l, r, start=st, stop=sp),
                           reads=[R(w_sb, k * F_out + oc * P, k * F_out + (oc + 1) * P),
                                  x_res] + ([] if k == 0 else [RP(bank)]),
                           writes=[RP(bank)], tag=f"{tag}mm")
                evac(oc, c0, w, bank, [RP(bank)])

    xn0b = bld.alloc("xn0b", [P, NG * TP], BF16)
    bufs = {}

    def load_u_sub(c0, w, tag="u"):
        u_sub = bufs["u_sub"]
        tok = dma(st_io, u_sub[:, 0:NG * 512].rearrange("p (g m) -> p g m", g=NG)[:, :, 0:w],
                  u_t.ap().rearrange("p (g m) -> p g m", g=NG)[:, :, c0:c0 + w],
                  [RD(u_t)], [R(u_sub)], tag)
        return tok

    cvt = bld.alloc("cvt", [P, 2 * 512], F32)
    if "PH1" in phases:
        bufs["u_sub"] = bld.alloc("u_sub", [P, NG * 512], F32)
        u_sub = bufs["u_sub"]
        for (c0, w) in SUBS:
            load_u_sub(c0, w)
            def srcfn(g, cc0, ww, _c0=c0):
                return (u_sub[:, g * 512:g * 512 + ww], R(u_sub, g * 512, g * 512 + ww))
            def wr(g, cc0, ww, rb, mb, deps, _c0=c0):
                t = cvt[:, 0:ww]
                tres = R(cvt, 0, ww)
                dve("tensor_tensor", (t, u_sub[:, g * 512:g * 512 + ww], rb, ALU.mult),
                    [R(u_sub, g * 512, g * 512 + ww)] + deps, [tres], tag="ln0a")
                o = xn0b[:, g * TP + _c0:g * TP + _c0 + ww]
                dve("tensor_tensor", (o, t, mb, ALU.add), [tres] + deps,
                    [R(xn0b, g * TP + _c0, g * TP + _c0 + ww)], tag="ln0b")
            ln_pass(u_sub, srcfn, [(c0, w)], wr, tag="ln0")
        bld.free(u_sub)

    thunks = []
    if "PH1" in phases:
        a_h = bld.alloc("a_h", [P, NG * TP], F32)
        def a_emit(g, c0, w, t_ap, deps):
            o = a_h[:, g * TP + c0:g * TP + c0 + w]
            ores = R(a_h, g * TP + c0, g * TP + c0 + w)
            dve("tensor_tensor", (o, t_ap, xn0b[:, g * TP + c0:g * TP + c0 + w], ALU.add),
                deps + [R(xn0b, g * TP + c0, g * TP + c0 + w)], [ores], tag="a")
        for (c0, w) in SUBS:
            thunks.append(lambda c0=c0, w=w: conv_pass(
                xn0b, a_emit, "cpe1_w", "cpe1_b", [(c0, w)], cvt, tag="cv1"))
        thunks.append(lambda: dma(st_io, a_sp.ap(), a_h[:], [R(a_h)], [RD(a_sp)],
                                  "aspill"))

    if "PH2" in phases:
        xn1 = bld.alloc("xn1", [P, NG * TP], BF16)
        def srcfn2(g, c0, w):
            return (a_h[:, g * TP + c0:g * TP + c0 + w], R(a_h, g * TP + c0, g * TP + c0 + w))
        def wr2(g, c0, w, rb, mb, deps):
            t = cvt[:, 0:w]
            tres = R(cvt, 0, w)
            dve("tensor_tensor", (t, a_h[:, g * TP + c0:g * TP + c0 + w], rb, ALU.mult),
                [R(a_h, g * TP + c0, g * TP + c0 + w)] + deps, [tres], tag="ln1a")
            dve("tensor_tensor", (t, t, mb, ALU.add), [tres] + deps, [tres], tag="ln1b")
            o = xn1[:, g * TP + c0:g * TP + c0 + w]
            dve("tensor_scalar", (o, t, smcol("g1", g), smcol("b1", g), ALU.mult, ALU.add),
                [tres, smres("g1"), smres("b1")],
                [R(xn1, g * TP + c0, g * TP + c0 + w)], tag="ln1c")
        for (c0, w) in SUBS:
            thunks.append(lambda c0=c0, w=w: ln_pass(
                a_h, srcfn2, [(c0, w)], wr2, tag="ln1",
                banks=("pkv0", "pkv1", "pkv0", "pkv1")))

    f1_units = []
    if "F1" in phases:
        wslr = [bld.alloc(f"w_sl{i}", [P, 2 * NG * P], BF16) for i in range(4)]
        t1s = bld.alloc("t1s", [P, 4 * 512], F32)
        h0ev = bld.alloc("h0ev", [P, 2 * 512], BF16)
        h0_sp = env["h0_sp"]
        w_w1, w_w2 = env["w_w1"], env["w_w2"]

        def f1_unit(oc):
            slb = wslr[oc % 4]
            dma(st_w, slb[:, 0:NG * P], w_w1.ap()[oc], [RD(w_w1)], [R(slb, 0, NG * P)],
                f"w1s{oc}")
            dma(st_w, slb[:, NG * P:2 * NG * P], w_w2.ap()[oc], [RD(w_w2)],
                [R(slb, NG * P, 2 * NG * P)], f"w2s{oc}")
            banks4w = ("pmm0", "pmm1", "pmm2", "pst")
            for k in range(NG):
                for si, (c0, w) in enumerate(RSUBS):
                    b1 = banks4w[si]
                    bld.op("pe", lambda e, o=psum[b1][:, 0:w],
                           l=slb[:, k * P:(k + 1) * P],
                           r=xn0b[:, k * TP + c0:k * TP + c0 + w],
                           st=(k == 0), sp=(k == NG - 1): e.matmul(o, l, r, start=st, stop=sp),
                           reads=[R(slb, k * P, (k + 1) * P),
                                  R(xn0b, k * TP + c0, k * TP + c0 + w)] +
                                 ([] if k == 0 else [RP(b1)]),
                           writes=[RP(b1)], tag="w1mm")
            for si, (c0, w) in enumerate(RSUBS):
                b1 = banks4w[si]
                act(t1s[:, si * 512:si * 512 + w], psum[b1][:, 0:w], AF.Silu,
                    [RP(b1), smres("w1_s")], [R(t1s, si * 512, si * 512 + w)],
                    scale=smcol("w1_s", oc), tag="f1silu")
            for k in range(NG):
                for si, (c0, w) in enumerate(RSUBS):
                    b2 = banks4w[si]
                    bld.op("pe", lambda e, o=psum[b2][:, 0:w],
                           l=slb[:, NG * P + k * P:NG * P + (k + 1) * P],
                           r=xn0b[:, k * TP + c0:k * TP + c0 + w],
                           st=(k == 0), sp=(k == NG - 1): e.matmul(o, l, r, start=st, stop=sp),
                           reads=[R(slb, NG * P + k * P, NG * P + (k + 1) * P),
                                  R(xn0b, k * TP + c0, k * TP + c0 + w)] +
                                 ([] if k == 0 else [RP(b2)]),
                           writes=[RP(b2)], tag="w2mm")
            for si, (c0, w) in enumerate(RSUBS):
                b2 = banks4w[si]
                par = si % 2
                dve("scalar_tensor_tensor",
                    (h0ev[:, par * 512:par * 512 + w], psum[b2][:, 0:w],
                     smcol("w2_s", oc), t1s[:, si * 512:si * 512 + w], ALU.mult, ALU.mult),
                    [RP(b2), R(t1s, si * 512, si * 512 + w), smres("w2_s")],
                    [R(h0ev, par * 512, par * 512 + w)], tag="f1mul")
                hcol = oc * T + (c0 - R0)
                dma(st_h, h0_sp.ap()[:, hcol:hcol + w], h0ev[:, par * 512:par * 512 + w],
                    [R(h0ev, par * 512, par * 512 + w)], [RD(h0_sp)], "h0sp")

        f1_units = [lambda oc=oc: f1_unit(oc) for oc in range(NGH)]

    na, nb = len(f1_units), len(thunks)
    ia = ib = 0
    while ia < na or ib < nb:
        if ib >= nb or (ia < na and ia * (nb + 1) <= ib * (na + 1)):
            f1_units[ia]()
            ia += 1
        else:
            thunks[ib]()
            ib += 1

    if "F1" in phases:
        for _h in wslr:
            bld.free(_h)
        bld.free(t1s)
        bld.free(h0ev)
    if "PH2" in phases:
        bld.free(a_h)
    bld.free(xn0b)
    bld.free(sq2[0])
    bld.free(rows2[0])
    bld.free(rbmb2[0])
    bld.free(rbmb2[1])

    if "PH3" in phases:
        in_wb = bld.alloc("in_wb", [P, NG * C], BF16)
        wload(env["w_in"], in_wb, "w_in")
        xi0 = bld.alloc("xi0", [P, NG * TP], BF16)
        def xfn(k, c0, w):
            return (xn1[:, k * TP + c0:k * TP + c0 + w],
                    R(xn1, k * TP + c0, k * TP + c0 + w))
        def ev3(oc, c0, w, bank, deps):
            act(xi0[:, oc * TP + c0:oc * TP + c0 + w], psum[bank][:, 0:w], AF.Identity,
                deps + [smres("in_b")], [R(xi0, oc * TP + c0, oc * TP + c0 + w)],
                bias=smcol("in_b", oc), tag="ev3")
        proj(in_wb, C, xn1, xfn, SUBS, ev3, tag="inw")
        bld.free(in_wb)

    if "PH4" in phases:
        mask_halo(xi0, "xi0")
        xi = bld.alloc("xi", [P, NG * TP], BF16)
        def xi_emit(g, c0, w, t_ap, deps):
            act(xi[:, g * TP + c0:g * TP + c0 + w], t_ap, AF.Silu,
                deps, [R(xi, g * TP + c0, g * TP + c0 + w)], tag="silu_xi")
        conv_pass(xi0, xi_emit, "dwc_w", "dwc_b", SUBS, cvt, tag="cv2")
        mask_halo(xi, "xi")
        bld.free(xi0)

    if "PH5" in phases:
        qk_wb = bld.alloc("qk_wb", [P, NG * 2 * C], BF16)
        wload(env["w_qk"], qk_wb, "w_qk")
        q_h = bld.alloc("q_h", [P, NG * TP], BF16)
        krope = bld.alloc("krope", [P, NG * TP], BF16)
        k_sub = bld.alloc("k_sub", [P, NG * 512], BF16)
        elu_e = bld.alloc("elu_e", [P, 512], F32)
        elu_p = bld.alloc("elu_p", [P, 512], F32)
        ksum = bld.alloc("ksum", [P, NG], F32)
        ksum_t = bld.alloc("ksum_t", [P, NG], F32)
        dve("memset", (ksum[:], 0.0), [], [R(ksum)], tag="ks0")

        def xfn5(k, c0, w):
            return (xi[:, k * TP + c0:k * TP + c0 + w],
                    R(xi, k * TP + c0, k * TP + c0 + w))

        def elu_evac(oc, c0, w, bank, deps, out_ap, out_res):
            bcol = smcol("qk_b", oc)
            dve("tensor_scalar", (elu_e[:, 0:w], psum[bank][:, 0:w], bcol, 0.0,
                                  ALU.add, ALU.min),
                deps + [smres("qk_b")], [R(elu_e, 0, w)], tag="elmin")
            dve("tensor_scalar", (elu_p[:, 0:w], psum[bank][:, 0:w], bcol, 0.0,
                                  ALU.add, ALU.max),
                deps + [smres("qk_b")], [R(elu_p, 0, w)], tag="elmax")
            act(elu_e[:, 0:w], elu_e[:, 0:w], AF.Exp, [R(elu_e, 0, w)],
                [R(elu_e, 0, w)], tag="elexp")
            dve("tensor_tensor", (out_ap, elu_e[:, 0:w], elu_p[:, 0:w], ALU.add),
                [R(elu_e, 0, w), R(elu_p, 0, w)], [out_res], tag="eladd")

        for (c0, w) in SUBS:
            for oc in range(NG):
                bank = ("pmm0", "pmm1", "pmm2")[oc % 3]
                for k in range(NG):
                    l_ap = qk_wb[:, k * 2 * C + oc * P:k * 2 * C + (oc + 1) * P]
                    x_ap, x_res = xfn5(k, c0, w)
                    bld.op("pe", lambda e, o=psum[bank][:, 0:w], l=l_ap, r=x_ap,
                           st=(k == 0), sp=(k == NG - 1): e.matmul(o, l, r, start=st, stop=sp),
                           reads=[R(qk_wb, k * 2 * C + oc * P, k * 2 * C + (oc + 1) * P),
                                  x_res] + ([] if k == 0 else [RP(bank)]),
                           writes=[RP(bank)], tag="qmm")
                elu_evac(oc, c0, w, bank, [RP(bank)],
                         q_h[:, oc * TP + c0:oc * TP + c0 + w],
                         R(q_h, oc * TP + c0, oc * TP + c0 + w))
            for oc in range(NG):
                bank = ("pmm0", "pmm1", "pmm2")[oc % 3]
                for k in range(NG):
                    l_ap = qk_wb[:, k * 2 * C + (NG + oc) * P:k * 2 * C + (NG + oc + 1) * P]
                    x_ap, x_res = xfn5(k, c0, w)
                    bld.op("pe", lambda e, o=psum[bank][:, 0:w], l=l_ap, r=x_ap,
                           st=(k == 0), sp=(k == NG - 1): e.matmul(o, l, r, start=st, stop=sp),
                           reads=[R(qk_wb, k * 2 * C + (NG + oc) * P,
                                    k * 2 * C + (NG + oc + 1) * P),
                                  x_res] + ([] if k == 0 else [RP(bank)]),
                           writes=[RP(bank)], tag="kmm")
                elu_evac(NG + oc, c0, w, bank, [RP(bank)],
                         k_sub[:, oc * 512:oc * 512 + w],
                         R(k_sub, oc * 512, oc * 512 + w))
            r0 = max(c0, R0)
            r1_ = min(c0 + w, R1)
            if r0 < r1_:
                for g in range(NG):
                    dve("reduce_sum", (ksum_t[:, g:g + 1],
                                       k_sub[:, g * 512 + r0 - c0:g * 512 + r1_ - c0], AX.X),
                        [R(k_sub, g * 512 + r0 - c0, g * 512 + r1_ - c0)],
                        [R(ksum_t, g, g + 1)], tag="ksr")
                dve("tensor_tensor", (ksum[:], ksum[:], ksum_t[:], ALU.add),
                    [R(ksum), R(ksum_t)], [R(ksum)], tag="ksa")
            for g in range(NG):
                bank = ("pmm0", "pmm1")[g % 2]
                bld.op("pe", lambda e, o=psum[bank][:, 0:w],
                       l=cst["rope"][:, g * P:(g + 1) * P], r=k_sub[:, g * 512:g * 512 + w]:
                       e.matmul(o, l, r, start=True, stop=True),
                       reads=[R(cst["rope"], g * P, (g + 1) * P),
                              R(k_sub, g * 512, g * 512 + w)],
                       writes=[RP(bank)], tag="krmm")
                dve("tensor_copy", (krope[:, g * TP + c0:g * TP + c0 + w],
                                    psum[bank][:, 0:w]),
                    [RP(bank)], [R(krope, g * TP + c0, g * TP + c0 + w)], tag="krev")
        bld.free(qk_wb)
        bld.free(k_sub)
        bld.free(elu_e)
        bld.free(elu_p)
        bld.free(ksum_t)

    if "PH6" in phases:
        krt = bld.alloc("krt", [P, 2 * P], BF16)
        vht = bld.alloc("vht", [P, 2 * P], BF16)
        def kv_bank(g):
            return ("pkv0", g % 4) if g < 4 else ("pkv1", g % 4)

        def emit_tr(g, ti):
            tc0 = R0 + ti * P
            par = ti % 2
            tb = ("ptra", "ptrb")[par]
            bld.op("pe", lambda e, o=psum[tb][:, 0:P],
                   l=krope[:, g * TP + tc0:g * TP + tc0 + P], i=cst["ident"][:]:
                   e.transpose(o, l, i),
                   reads=[R(krope, g * TP + tc0, g * TP + tc0 + P), R(cst["ident"]),
                          RP(tb)],
                   writes=[RP(tb)], tag="trk")
            dve("tensor_copy", (krt[:, par * P:(par + 1) * P], psum[tb][:, 0:P]),
                [RP(tb)], [R(krt, par * P, (par + 1) * P)], tag="trke")
            bld.op("pe", lambda e, o=psum[tb][:, P:2 * P],
                   l=xi[:, g * TP + tc0:g * TP + tc0 + P], i=cst["ident"][:]:
                   e.transpose(o, l, i),
                   reads=[R(xi, g * TP + tc0, g * TP + tc0 + P), R(cst["ident"]),
                          RP(tb)],
                   writes=[RP(tb)], tag="trv")
            dve("tensor_copy", (vht[:, par * P:(par + 1) * P], psum[tb][:, P:2 * P]),
                [RP(tb)], [R(vht, par * P, (par + 1) * P)], tag="trve")

        def emit_kvmm(g, ti):
            par = ti % 2
            kb, slot = kv_bank(g)
            bld.op("pe", lambda e, o=psum[kb][:, slot * P:(slot + 1) * P],
                   l=krt[:, par * P:(par + 1) * P], r=vht[:, par * P:(par + 1) * P],
                   st=(ti == 0), sp=(ti == 15): e.matmul(o, l, r, start=st, stop=sp),
                   reads=[R(krt, par * P, (par + 1) * P), R(vht, par * P, (par + 1) * P)]
                         + ([] if ti == 0 else [RP(kb)]),
                   writes=[RP(kb)], tag="kvmm")

        for g in range(NG):
            emit_tr(g, 0)
            for ti in range(16):
                if ti < 15:
                    emit_tr(g, ti + 1)
                emit_kvmm(g, ti)
        ccs = bld.alloc("ccs", [P, 2 * CCN], F32)
        kv_sb = bld.alloc("kv_sb", [P, NG * P], F32)
        for g in range(NG):
            kb, slot = kv_bank(g)
            dve("tensor_scalar", (kv_sb[:, g * P:(g + 1) * P],
                                  psum[kb][:, slot * P:(slot + 1) * P],
                                  1.0 / 8192.0, None, ALU.mult),
                [RP(kb)], [R(kv_sb, g * P, (g + 1) * P)], tag="kvev")
        for b in range(2):
            dve("tensor_scalar", (ccs[:, b * CCN:b * CCN + NG * P], kv_sb[:],
                                  smcol("bsel", b), None, ALU.mult),
                [R(kv_sb), smres("bsel")], [R(ccs, b * CCN, b * CCN + NG * P)], tag="cckv")
            dve("tensor_scalar", (ccs[:, b * CCN + NG * P:(b + 1) * CCN], ksum[:],
                                  smcol("bsel", b), None, ALU.mult),
                [R(ksum), smres("bsel")],
                [R(ccs, b * CCN + NG * P, (b + 1) * CCN)], tag="ccks")
        dma(st_cc, cc_in.ap().rearrange("b (p m) -> p b m", p=P),
            ccs[:].rearrange("p (b m) -> p b m", b=2),
            [R(ccs)], [RD(cc_in)], "ccst")
        if fake_cc:
            dma(st_cc, cc_out.ap(), cc_in.ap(), [RD(cc_in)], [RD(cc_out)], "fakecc")
        else:
            o = bld.op("pool", lambda e: e.collective_compute(
                "AllReduce", ALU.add,
                replica_groups=[[0, 1, 2, 3, 4, 5, 6, 7]],
                ins=[cc_in.ap().opt()], outs=[cc_out.ap().opt()]),
                reads=[RD(cc_in)], writes=[RD(cc_out)], kind="cc", tag="cc")
            o.token = ("cc_sem", 1)
        bld.free(krt)
        bld.free(vht)
        bld.free(kv_sb)
        bld.free(ccs)
        bld.free(krope)
        bld.free(ksum)

    if "PH7" in phases:
        lepe = bld.alloc("lepe", [P, NG * TP], BF16)
        def lp_emit(g, c0, w, t_ap, deps):
            dve("tensor_copy", (lepe[:, g * TP + c0:g * TP + c0 + w], t_ap),
                deps, [R(lepe, g * TP + c0, g * TP + c0 + w)], tag="lpcp")
        conv_pass(xi, lp_emit, "lepe_w", "lepe_b", SUBS, cvt, tag="cv3")
        bld.free(xi)
        bld.free(cvt)

    if "PH8" in phases:
        actp_wb = bld.alloc("actp_wb", [P, NG * C], BF16)
        wload(env["w_actp"], actp_wb, "w_actp")
        ar_h = bld.alloc("ar_h", [P, NG * TP], BF16)
        def xfn8(k, c0, w):
            return (xn1[:, k * TP + c0:k * TP + c0 + w],
                    R(xn1, k * TP + c0, k * TP + c0 + w))
        def ev8(oc, c0, w, bank, deps):
            act(ar_h[:, oc * TP + c0:oc * TP + c0 + w], psum[bank][:, 0:w], AF.Silu,
                deps + [smres("actp_b")], [R(ar_h, oc * TP + c0, oc * TP + c0 + w)],
                bias=smcol("actp_b", oc), tag="ev8")
        proj(actp_wb, C, xn1, xfn8, SUBS, ev8, tag="actp")
        bld.free(actp_wb)
        bld.free(xn1)

    if "PH9" in phases:
        dma(st_io, q_sp.ap(), q_h[:], [R(q_h)], [RD(q_sp)], "qsp")
        dma(st_io, ar_sp.ap(), ar_h[:], [R(ar_h)], [RD(ar_sp)], "arsp")
        dma(st_io, lp_sp.ap(), lepe[:], [R(lepe)], [RD(lp_sp)], "lpsp")
        bld.free(q_h)
        bld.free(ar_h)
        bld.free(lepe)

    if "F2" in phases:
        h_full = bld.alloc("h_full", [P, NGH * T], BF16)
        dma(st_h, h_full[:], env["h0_sp"].ap(), [RD(env["h0_sp"])], [R(h_full)], "h0rl")

    def h_pass(w_dram, s_name, dst_sp, tag):
        slr = [bld.alloc(f"hsl{i}{tag}", [P, NGH * P], BF16) for i in range(4)]
        ev = bld.alloc(f"hev{tag}", [P, 2 * 512], BF16)
        banks4 = ("pmm0", "pmm1", "pmm2", "pst")
        for oc in range(NGH):
            slb = slr[oc % 4]
            dma(st_w, slb[:], w_dram.ap()[oc], [RD(w_dram)], [R(slb)], f"{tag}s{oc}")
            for k in range(NGH):
                for si, (c0, w) in enumerate(RSUBS):
                    bank = banks4[si]
                    bld.op("pe", lambda e, o=psum[bank][:, 0:w],
                           l=slb[:, k * P:(k + 1) * P],
                           r=h_full[:, k * T + (c0 - R0):k * T + (c0 - R0) + w],
                           st=(k == 0), sp=(k == NGH - 1): e.matmul(o, l, r, start=st, stop=sp),
                           reads=[R(slb, k * P, (k + 1) * P),
                                  R(h_full, k * T + (c0 - R0), k * T + (c0 - R0) + w)] +
                                 ([] if k == 0 else [RP(bank)]),
                           writes=[RP(bank)], tag=f"{tag}mm")
            for si, (c0, w) in enumerate(RSUBS):
                bank = banks4[si]
                epar = si % 2
                act(ev[:, epar * 512:epar * 512 + w], psum[bank][:, 0:w], AF.Gelu,
                    [RP(bank), smres(s_name)], [R(ev, epar * 512, epar * 512 + w)],
                    scale=smcol(s_name, oc), tag=f"{tag}g")
                dma(st_h, dst_sp.ap()[:, oc * T + (c0 - R0):oc * T + (c0 - R0) + w],
                    ev[:, epar * 512:epar * 512 + w],
                    [R(ev, epar * 512, epar * 512 + w)], [RD(dst_sp)], f"{tag}sp")
        for _h in slr:
            bld.free(_h)
        bld.free(ev)

    if "F2" in phases:
        h_pass(env["w_h1"], "h1_s", h1_sp, "f2")
        dma(st_h, h_full[:], h1_sp.ap(), [RD(h1_sp)], [R(h_full)], "h1rl")
    if "F3" in phases:
        h_pass(env["w_h2"], "h2_s", h2_sp, "f3")
        dma(st_h, h_full[:], h2_sp.ap(), [RD(h2_sp)], [R(h_full)], "h2rl")

    if "F4" in phases:
        w3r = [bld.alloc(f"w3sl{i}", [P, NGH * P], BF16) for i in range(4)]
        uo_sub = bld.alloc("uo_sub", [P, 2 * 512], F32)
        fo_sub = bld.alloc("fo_sub", [P, 2 * 512], F32)
        w_w3 = env["w_w3"]
        banks4f = ("pmm0", "pmm1", "pmm2", "pst")
        for oc in range(NG):
            slb = w3r[oc % 4]
            dma(st_w, slb[:], w_w3.ap()[oc], [RD(w_w3)], [R(slb)], f"w3s{oc}")
            for k in range(NGH):
                for si, (c0, w) in enumerate(RSUBS):
                    bank = banks4f[si]
                    bld.op("pe", lambda e, o=psum[bank][:, 0:w],
                           l=slb[:, k * P:(k + 1) * P],
                           r=h_full[:, k * T + (c0 - R0):k * T + (c0 - R0) + w],
                           st=(k == 0), sp=(k == NGH - 1): e.matmul(o, l, r, start=st, stop=sp),
                           reads=[R(slb, k * P, (k + 1) * P),
                                  R(h_full, k * T + (c0 - R0), k * T + (c0 - R0) + w)] +
                                 ([] if k == 0 else [RP(bank)]),
                           writes=[RP(bank)], tag="w3mm")
            for si, (c0, w) in enumerate(RSUBS):
                bank = banks4f[si]
                po = (si % 2) * 512
                dma(st_io, uo_sub[:, po:po + w],
                    u_t.ap()[:, oc * TP + c0:oc * TP + c0 + w],
                    [RD(u_t)], [R(uo_sub, po, po + w)], "u4")
                act(fo_sub[:, po:po + w], psum[bank][:, 0:w], AF.Identity,
                    [RP(bank), smres("w3_s")], [R(fo_sub, po, po + w)],
                    scale=smcol("w3_s", oc), tag="f4ev")
                dve("tensor_tensor", (fo_sub[:, po:po + w], fo_sub[:, po:po + w],
                                      uo_sub[:, po:po + w], ALU.add),
                    [R(fo_sub, po, po + w), R(uo_sub, po, po + w)],
                    [R(fo_sub, po, po + w)], tag="f4add")
                dma(st_io, out_t.ap()[:, oc * T + (c0 - R0):oc * T + (c0 - R0) + w],
                    fo_sub[:, po:po + w], [R(fo_sub, po, po + w)], [RD(out_t)], "f4st")
        for _h in w3r:
            bld.free(_h)
        bld.free(h_full)
        bld.free(uo_sub)
        bld.free(fo_sub)

    if "B0" in phases:
        ccr = bld.alloc("ccr", [P, 2 * CCN], F32)
        kvb = bld.alloc("kvb", [P, NG * P], BF16)
        kml = bld.alloc("kml", [P, NG * NH], BF16)
        kvf = bld.alloc("kvf", [P, NG * P], F32)
        ksg = bld.alloc("ksg", [P, NG], F32)
        dma(st_cc, ccr[:].rearrange("p (b m) -> p b m", b=2),
            cc_out.ap().rearrange("b (p m) -> p b m", p=P),
            [RD(cc_out)], [R(ccr)], "ccrd")
        dve("tensor_scalar", (kvf[:], ccr[:, 0:NG * P], smcol("bsel", 0), None, ALU.mult),
            [R(ccr, 0, NG * P), smres("bsel")], [R(kvf)], tag="b0a")
        dve("scalar_tensor_tensor", (kvf[:], ccr[:, CCN:CCN + NG * P], smcol("bsel", 1),
                                     kvf[:], ALU.mult, ALU.add),
            [R(ccr, CCN, CCN + NG * P), R(kvf), smres("bsel")], [R(kvf)], tag="b0b")
        dve("tensor_scalar", (ksg[:], ccr[:, NG * P:CCN], smcol("bsel", 0), None, ALU.mult),
            [R(ccr, NG * P, CCN), smres("bsel")], [R(ksg)], tag="b0c")
        dve("scalar_tensor_tensor", (ksg[:], ccr[:, CCN + NG * P:2 * CCN], smcol("bsel", 1),
                                     ksg[:], ALU.mult, ALU.add),
            [R(ccr, CCN + NG * P, 2 * CCN), R(ksg), smres("bsel")], [R(ksg)], tag="b0d")
        dve("memset", (kvb[:], 0.0), [], [R(kvb)], tag="b0z")
        for g in range(NG):
            dve("tensor_copy", (kvb[0:64, g * P:g * P + 64], kvf[0:64, g * P:g * P + 64]),
                [R(kvf, g * P, g * P + 64, 0, 64)],
                [R(kvb, g * P, g * P + 64, 0, 64)], tag="b0d1")
            dve("tensor_copy", (kvb[64:128, g * P + 64:(g + 1) * P],
                                kvf[64:128, g * P + 64:(g + 1) * P]),
                [R(kvf, g * P + 64, (g + 1) * P, 64, 128)],
                [R(kvb, g * P + 64, (g + 1) * P, 64, 128)], tag="b0d2")
        for g in range(NG):
            dve("tensor_scalar", (kml[:, g * NH:(g + 1) * NH],
                                  cst["zmask"][:, g * NH:(g + 1) * NH],
                                  ksg[:, g:g + 1], None, ALU.mult),
                [R(cst["zmask"], g * NH, (g + 1) * NH), R(ksg, g, g + 1)],
                [R(kml, g * NH, (g + 1) * NH)], tag="b0k")
        bld.free(ccr)
        bld.free(kvf)

        outw_b = bld.alloc("outw_b", [P, NG * C], BF16)
        wload(env["w_out"], outw_b, "w_out")

    if "B1" in phases:
        y_h = bld.alloc("y_h", [P, NG * TP], F32)
        qs2 = [bld.alloc(f"qs_sub{i}", [P, NG * 512], BF16) for i in range(2)]
        ars2 = [bld.alloc(f"ars_sub{i}", [P, NG * 512], BF16) for i in range(2)]
        lps2 = [bld.alloc(f"lps_sub{i}", [P, NG * 512], BF16) for i in range(2)]
        as_sub = bld.alloc("as_sub", [P, NG * 512], F32)
        qr2 = [bld.alloc(f"qr_sub{i}", [P, NG * 512], BF16) for i in range(2)]
        at2 = [bld.alloc(f"at_sub{i}", [P, NG * 512], BF16) for i in range(2)]
        zsb = bld.alloc("zsb", [NH, 512], F32)
        zbb = bld.alloc("zbb", [P, 2 * 512], F32)
        t_f = bld.alloc("t_f", [P, 2 * 512], F32)

        for sidx, (c0, w) in enumerate(SUBS):
            qs_sub = qs2[sidx % 2]
            ars_sub = ars2[sidx % 2]
            lps_sub = lps2[sidx % 2]
            qr_sub = qr2[sidx % 2]
            at_sub = at2[sidx % 2]
            dma(st_io, qs_sub[:].rearrange("p (g m) -> p g m", g=NG)[:, :, 0:w],
                q_sp.ap().rearrange("p (g m) -> p g m", g=NG)[:, :, c0:c0 + w],
                [RD(q_sp)], [R(qs_sub)], "qld")
            dma(st_io, ars_sub[:].rearrange("p (g m) -> p g m", g=NG)[:, :, 0:w],
                ar_sp.ap().rearrange("p (g m) -> p g m", g=NG)[:, :, c0:c0 + w],
                [RD(ar_sp)], [R(ars_sub)], "arld")
            dma(st_io, lps_sub[:].rearrange("p (g m) -> p g m", g=NG)[:, :, 0:w],
                lp_sp.ap().rearrange("p (g m) -> p g m", g=NG)[:, :, c0:c0 + w],
                [RD(lp_sp)], [R(lps_sub)], "lpld")
            dma(st_io, as_sub[:].rearrange("p (g m) -> p g m", g=NG)[:, :, 0:w],
                a_sp.ap().rearrange("p (g m) -> p g m", g=NG)[:, :, c0:c0 + w],
                [RD(a_sp)], [R(as_sub)], "ald")
            for g in range(NG):
                bank = ("pmm0", "pmm1")[g % 2]
                bld.op("pe", lambda e, o=psum[bank][:, 0:w],
                       l=cst["rope"][:, g * P:(g + 1) * P],
                       r=qs_sub[:, g * 512:g * 512 + w]:
                       e.matmul(o, l, r, start=True, stop=True),
                       reads=[R(cst["rope"], g * P, (g + 1) * P),
                              R(qs_sub, g * 512, g * 512 + w)],
                       writes=[RP(bank)], tag="qrmm")
                dve("tensor_copy", (qr_sub[:, g * 512:g * 512 + w], psum[bank][:, 0:w]),
                    [RP(bank)], [R(qr_sub, g * 512, g * 512 + w)], tag="qrev")
            for g in range(NG):
                bld.op("pe", lambda e, o=psum["pst"][0:NH, 0:w],
                       l=kml[:, g * NH:(g + 1) * NH], r=qs_sub[:, g * 512:g * 512 + w],
                       st=(g == 0), sp=(g == NG - 1): e.matmul(o, l, r, start=st, stop=sp),
                       reads=[R(kml, g * NH, (g + 1) * NH),
                              R(qs_sub, g * 512, g * 512 + w)] +
                             ([] if g == 0 else [RP("pst")]),
                       writes=[RP("pst")], tag="zmm")
            dve("tensor_scalar", (zsb[:, 0:w], psum["pst"][0:NH, 0:w], 1e-6, None, ALU.add),
                [RP("pst")], [R(zsb, 0, w)], tag="zadd")
            dve("reciprocal", (zsb[:, 0:w], zsb[:, 0:w]),
                [R(zsb, 0, w)], [R(zsb, 0, w)], tag="zrec")
            for g in range(NG):
                zb_bank = ("pkv0", "pkv1")[g % 2]
                at_bank = ("pmm2", "pst")[g % 2]
                bld.op("pe", lambda e, o=psum[zb_bank][:, 0:w],
                       l=cst["zsel"][:, g * P:(g + 1) * P], r=zsb[0:NH, 0:w]:
                       e.matmul(o, l, r, start=True, stop=True),
                       reads=[R(cst["zsel"], g * P, (g + 1) * P), R(zsb, 0, w)],
                       writes=[RP(zb_bank)], tag="zbmm")
                zo = (g % 2) * 512
                dve("tensor_copy", (zbb[:, zo:zo + w], psum[zb_bank][:, 0:w]),
                    [RP(zb_bank)], [R(zbb, zo, zo + w)], tag="zbev")
                bld.op("pe", lambda e, o=psum[at_bank][:, 0:w],
                       l=kvb[:, g * P:(g + 1) * P], r=qr_sub[:, g * 512:g * 512 + w]:
                       e.matmul(o, l, r, start=True, stop=True),
                       reads=[R(kvb, g * P, (g + 1) * P),
                              R(qr_sub, g * 512, g * 512 + w)],
                       writes=[RP(at_bank)], tag="atmm")
                dve("tensor_tensor", (t_f[:, zo:zo + w], psum[at_bank][:, 0:w],
                                      zbb[:, zo:zo + w], ALU.mult),
                    [RP(at_bank), R(zbb, zo, zo + w)], [R(t_f, zo, zo + w)], tag="atz")
                dve("tensor_tensor", (t_f[:, zo:zo + w], t_f[:, zo:zo + w],
                                      lps_sub[:, g * 512:g * 512 + w], ALU.add),
                    [R(t_f, zo, zo + w), R(lps_sub, g * 512, g * 512 + w)],
                    [R(t_f, zo, zo + w)], tag="atl")
                dve("tensor_tensor", (at_sub[:, g * 512:g * 512 + w], t_f[:, zo:zo + w],
                                      ars_sub[:, g * 512:g * 512 + w], ALU.mult),
                    [R(t_f, zo, zo + w), R(ars_sub, g * 512, g * 512 + w)],
                    [R(at_sub, g * 512, g * 512 + w)], tag="atar")
            for oc in range(NG):
                bank = ("pmm0", "pmm1")[oc % 2]
                for k in range(NG):
                    bld.op("pe", lambda e, o=psum[bank][:, 0:w],
                           l=outw_b[:, k * C + oc * P:k * C + (oc + 1) * P],
                           r=at_sub[:, k * 512:k * 512 + w],
                           st=(k == 0), sp=(k == NG - 1): e.matmul(o, l, r, start=st, stop=sp),
                           reads=[R(outw_b, k * C + oc * P, k * C + (oc + 1) * P),
                                  R(at_sub, k * 512, k * 512 + w)] +
                                 ([] if k == 0 else [RP(bank)]),
                           writes=[RP(bank)], tag="ywmm")
                yo = (oc % 2) * 512
                act(t_f[:, yo:yo + w], psum[bank][:, 0:w], AF.Identity,
                    [RP(bank), smres("out_b")], [R(t_f, yo, yo + w)],
                    bias=smcol("out_b", oc), tag="yev")
                dve("tensor_tensor", (y_h[:, oc * TP + c0:oc * TP + c0 + w],
                                      t_f[:, yo:yo + w],
                                      as_sub[:, oc * 512:oc * 512 + w], ALU.add),
                    [R(t_f, yo, yo + w), R(as_sub, oc * 512, oc * 512 + w)],
                    [R(y_h, oc * TP + c0, oc * TP + c0 + w)], tag="ya")
        for _h in qs2 + ars2 + lps2 + qr2 + at2 + [as_sub, outw_b, kvb, kml, ksg,
                                                    zsb, zbb, t_f]:
            bld.free(_h)

    if "B2" in phases:
        sq2[0] = sq2[1] = bld.alloc("sqs0b", [P, NG * 512], F32)
        rows2[0] = rows2[1] = bld.alloc("rows0b", [1, 4 * 512], F32)
        rbmb2[0] = bld.alloc("rbmb0b", [P, 2 * 512], F32)
        rbmb2[1] = bld.alloc("rbmb1b", [P, 2 * 512], F32)
        cvt = bld.alloc("cvtb", [P, 2 * 512], F32)
        mask_halo(y_h, "y")
        xn2b = bld.alloc("xn2b", [P, NG * T], BF16, top=True)
        x2_sub = bld.alloc("x2_sub", [P, NG * 512], F32)
        def x2_emit(g, c0, w, t_ap, deps):
            o = x2_sub[:, g * 512:g * 512 + w]
            dve("tensor_tensor", (o, t_ap, y_h[:, g * TP + c0:g * TP + c0 + w], ALU.add),
                deps + [R(y_h, g * TP + c0, g * TP + c0 + w)],
                [R(x2_sub, g * 512, g * 512 + w)], tag="x2")
        for (c0, w) in RSUBS:
            conv_pass(y_h, x2_emit, "cpe2_w", "cpe2_b", [(c0, w)], cvt, tag="cv4")
            dma(st_io, x2_sp.ap().rearrange("p (g m) -> p g m", g=NG)[:, :, c0 - R0:c0 - R0 + w],
                x2_sub[:].rearrange("p (g m) -> p g m", g=NG)[:, :, 0:w],
                [R(x2_sub)], [RD(x2_sp)], "x2sp")
            def srcb(g, cc0, ww):
                return (x2_sub[:, g * 512:g * 512 + ww], R(x2_sub, g * 512, g * 512 + ww))
            def wrb(g, cc0, ww, rb, mb, deps, _c0=c0):
                t = cvt[:, 0:ww]
                tres = R(cvt, 0, ww)
                dve("tensor_tensor", (t, x2_sub[:, g * 512:g * 512 + ww], rb, ALU.mult),
                    [R(x2_sub, g * 512, g * 512 + ww)] + deps, [tres], tag="ln2a")
                dve("tensor_tensor", (t, t, mb, ALU.add), [tres] + deps, [tres], tag="ln2b")
                o = xn2b[:, g * T + _c0 - R0:g * T + _c0 - R0 + ww]
                dve("tensor_scalar", (o, t, smcol("g2", g), smcol("b2", g), ALU.mult, ALU.add),
                    [tres, smres("g2"), smres("b2")],
                    [R(xn2b, g * T + _c0 - R0, g * T + _c0 - R0 + ww)], tag="ln2c")
            ln_pass(x2_sub, srcb, [(c0, w)], wrb, tag="ln2")
        bld.free(y_h)
        bld.free(x2_sub)
        bld.free(cvt)
        for _h in {h.name: h for h in sq2 + rows2 + rbmb2}.values():
            bld.free(_h)

    if "B3" in phases:
        g_full = bld.alloc("g_full", [P, NGH * T], BF16)
        f1r = [bld.alloc(f"f1sl{i}", [P, NG * P], BF16) for i in range(4)]
        w_fc1 = env["w_fc1"]
        banks4b = ("pmm0", "pmm1", "pmm2", "pst")
        for oc in range(NGH):
            slb = f1r[oc % 4]
            dma(st_w, slb[:], w_fc1.ap()[oc], [RD(w_fc1)], [R(slb)], f"fc1s{oc}")
            for k in range(NG):
                for si, (c0, w) in enumerate(RSUBS):
                    bank = banks4b[si]
                    bld.op("pe", lambda e, o=psum[bank][:, 0:w],
                           l=slb[:, k * P:(k + 1) * P],
                           r=xn2b[:, k * T + (c0 - R0):k * T + (c0 - R0) + w],
                           st=(k == 0), sp=(k == NG - 1): e.matmul(o, l, r, start=st, stop=sp),
                           reads=[R(slb, k * P, (k + 1) * P),
                                  R(xn2b, k * T + (c0 - R0), k * T + (c0 - R0) + w)] +
                                 ([] if k == 0 else [RP(bank)]),
                           writes=[RP(bank)], tag="fc1mm")
            for si, (c0, w) in enumerate(RSUBS):
                bank = banks4b[si]
                act(g_full[:, oc * T + (c0 - R0):oc * T + (c0 - R0) + w],
                    psum[bank][:, 0:w], AF.Gelu,
                    [RP(bank), smres("fc1_b")],
                    [R(g_full, oc * T + (c0 - R0), oc * T + (c0 - R0) + w)],
                    bias=smcol("fc1_b", oc), tag="fc1g")
        for _h in f1r:
            bld.free(_h)
        bld.free(xn2b)

    if "B4" in phases:
        f2r = [bld.alloc(f"f2sl{i}", [P, NGH * P], BF16) for i in range(4)]
        x2r = bld.alloc("x2r", [P, 2 * 512], F32)
        po_r = bld.alloc("po_r", [P, 2 * 512], F32)
        o_sub = bld.alloc("o_sub", [P, 2 * 512], F32)
        w_fc2 = env["w_fc2"]
        banks4c = ("pmm0", "pmm1", "pmm2", "pst")
        for oc in range(NG):
            slb = f2r[oc % 4]
            dma(st_w, slb[:], w_fc2.ap()[oc], [RD(w_fc2)], [R(slb)], f"fc2s{oc}")
            for k in range(NGH):
                for si, (c0, w) in enumerate(RSUBS):
                    bank = banks4c[si]
                    bld.op("pe", lambda e, o=psum[bank][:, 0:w],
                           l=slb[:, k * P:(k + 1) * P],
                           r=g_full[:, k * T + (c0 - R0):k * T + (c0 - R0) + w],
                           st=(k == 0), sp=(k == NGH - 1): e.matmul(o, l, r, start=st, stop=sp),
                           reads=[R(slb, k * P, (k + 1) * P),
                                  R(g_full, k * T + (c0 - R0), k * T + (c0 - R0) + w)] +
                                 ([] if k == 0 else [RP(bank)]),
                           writes=[RP(bank)], tag="fc2mm")
            for si, (c0, w) in enumerate(RSUBS):
                bank = banks4c[si]
                rc = oc * T + (c0 - R0)
                po = ((oc * 4 + si) % 2) * 512
                dma(st_io, x2r[:, po:po + w], x2_sp.ap()[:, rc:rc + w],
                    [RD(x2_sp)], [R(x2r, po, po + w)], "x2rl")
                dma(st_io, po_r[:, po:po + w], out_t.ap()[:, rc:rc + w],
                    [RD(out_t)], [R(po_r, po, po + w)], "porl")
                act(o_sub[:, po:po + w], psum[bank][:, 0:w], AF.Identity,
                    [RP(bank), smres("fc2_b")], [R(o_sub, po, po + w)],
                    bias=smcol("fc2_b", oc), tag="f2ev")
                dve("tensor_tensor", (o_sub[:, po:po + w], o_sub[:, po:po + w],
                                      x2r[:, po:po + w], ALU.add),
                    [R(o_sub, po, po + w), R(x2r, po, po + w)], [R(o_sub, po, po + w)],
                    tag="f2a1")
                dve("tensor_tensor", (o_sub[:, po:po + w], o_sub[:, po:po + w],
                                      po_r[:, po:po + w], ALU.add),
                    [R(o_sub, po, po + w), R(po_r, po, po + w)], [R(o_sub, po, po + w)],
                    tag="f2a2")
                dma(st_io, out_t.ap()[:, rc:rc + w], o_sub[:, po:po + w],
                    [R(o_sub, po, po + w)], [RD(out_t)], "ost")



def _tile_rows(a, ng):
    f = a.shape[1]
    return np.ascontiguousarray(a.reshape(ng, P, f).transpose(1, 0, 2).reshape(P, ng * f))


def _oc_slices(w_oi, n_oc, n_k):
    O, I = w_oi.shape
    assert O == n_oc * P and I == n_k * P
    a = w_oi.reshape(n_oc, P, n_k, P).transpose(0, 3, 2, 1)
    return np.ascontiguousarray(a.reshape(n_oc, P, n_k * P))


def _oc_slices_kf(w_kf, n_oc, n_k):
    K, F = w_kf.shape
    assert K == n_k * P and F == n_oc * P
    a = w_kf.reshape(n_k, P, n_oc, P).transpose(2, 1, 0, 3)
    return np.ascontiguousarray(a.reshape(n_oc, P, n_k * P))


def make_smalls(params, core):
    chunk = core % 4
    b = core // 4
    sm = np.zeros((P, SMALLS_COLS), np.float32)

    def put(name, arr):
        off, n = SMALLS[name]
        sm[:, off:off + n] = arr

    def chunks_vec(v):
        return np.ascontiguousarray(v.reshape(NG, P).T)

    put("g1", chunks_vec(params["norm1_g"]))
    put("b1", chunks_vec(params["norm1_b"]))
    put("g2", chunks_vec(params["norm2_g"]))
    put("b2", chunks_vec(params["norm2_b"]))
    for cv, wname, bname in [("cpe1", "cpe1_w", "cpe1_b"), ("dwc", "dwc_w", "dwc_b"),
                             ("lepe", "lepe_w", "lepe_b"), ("cpe2", "cpe2_w", "cpe2_b")]:
        w3 = params[wname].reshape(C, 3).reshape(NG, P, 3).transpose(1, 0, 2).reshape(P, NG * 3)
        put(cv + "_w", w3)
        put(cv + "_b", chunks_vec(params[bname]))
    put("in_b", chunks_vec(params["in_b"]))
    put("actp_b", chunks_vec(params["actp_b"]))
    put("qk_b", np.ascontiguousarray(params["qk_b"].reshape(2 * NG, P).T))
    put("out_b", chunks_vec(params["out_b"]))
    put("fc1_b", np.ascontiguousarray(params["fc1_b"].reshape(NGH, P).T))
    put("fc2_b", chunks_vec(params["fc2_b"]))
    put("w1_s", np.ascontiguousarray(params["w1_s"].reshape(NGH, P).T))
    put("w2_s", np.ascontiguousarray(params["w2_s"].reshape(NGH, P).T))
    put("h1_s", np.ascontiguousarray(params["h1_s"].reshape(NGH, P).T))
    put("h2_s", np.ascontiguousarray(params["h2_s"].reshape(NGH, P).T))
    put("w3_s", chunks_vec(params["w3_s"]))
    put("lmask", np.full((P, HALO), 0.0 if chunk == 0 else 1.0, np.float32))
    put("rmask", np.full((P, HALO), 0.0 if chunk == 3 else 1.0, np.float32))
    bs = np.zeros((P, 2), np.float32)
    bs[:, b] = 1.0
    put("bsel", bs)
    return sm


def make_in_maps(inputs):
    x = inputs["L0_attn_x"]
    pr = {k[len("L0_attn_"):]: np.asarray(v) for k, v in inputs.items()
          if k.startswith("L0_attn_")}
    fr = {k[len("L0_ffn_"):]: np.asarray(v) for k, v in inputs.items()
          if k.startswith("L0_ffn_")}
    pr.update({k: v for k, v in fr.items()})
    params = pr

    w_shared = dict(
        w_in=_tile_rows(params["in_w"], NG),
        w_actp=_tile_rows(params["actp_w"], NG),
        w_qk=_tile_rows(params["qk_w"], NG),
        w_out=_tile_rows(params["out_w"], NG),
        w_fc1=_oc_slices_kf(params["fc1_w"], NGH, NG),
        w_fc2=_oc_slices_kf(params["fc2_w"], NG, NGH),
        w_w1=_oc_slices(params["w1_w"], NGH, NG),
        w_w2=_oc_slices(params["w2_w"], NGH, NG),
        w_h1=_oc_slices(params["h1_w"], NGH, NGH),
        w_h2=_oc_slices(params["h2_w"], NGH, NGH),
        w_w3=_oc_slices(params["w3_w"], NG, NGH),
    )

    in_maps = []
    for core in range(8):
        b, chunk = core // 4, core % 4
        t0 = chunk * T
        lo, hi = t0 - HALO, t0 + T + HALO
        seg = np.zeros((TP, C), np.float32)
        s0, s1 = max(lo, 0), min(hi, 8192)
        seg[s0 - lo:s1 - lo, :] = x[b, s0:s1, :]
        u_fm = np.ascontiguousarray(seg.T)
        u_t = _tile_rows(u_fm, NG)
        m = dict(u_t=u_t, smalls=make_smalls(params, core), **w_shared)
        in_maps.append(m)
    return in_maps


_CACHE = {}


def kernel(**inputs):
    key = "full"
    if key not in _CACHE:
        _CACHE[key] = build_program()
    nc, meta = _CACHE[key]
    in_maps = make_in_maps(inputs)
    res = run_bass_kernel_spmd(nc, in_maps, core_ids=list(range(8)))
    out = np.empty((2, 8192, 1024), np.float32)
    for core in range(8):
        b, chunk = core // 4, core % 4
        o = res.results[core]["out_t"]
        o = o.reshape(P, NG, T).transpose(1, 0, 2).reshape(C, T)
        out[b, chunk * T:(chunk + 1) * T, :] = o.T
    return out


# version 08
# speedup vs baseline: 1.0224x; 1.0224x over previous
import numpy as np
import ml_dtypes
from contextlib import ExitStack

import concourse.bass as bass
import concourse.mybir as mybir
from concourse.bass_utils import run_bass_kernel_spmd

F32 = mybir.dt.float32
BF16 = mybir.dt.bfloat16
AF = mybir.ActivationFunctionType
ALU = mybir.AluOpType
AX = mybir.AxisListType

P = 128
C = 1024
NG = 8
NH = 16
HD = 64
T = 2048
HALO = 4
TP = T + 2 * HALO
R0, R1 = HALO, HALO + T
HID = 4096
NGH = 32

SUBS = [(0, 512), (512, 512), (1024, 512), (1536, 512), (2048, 8)]
RSUBS = [(4, 512), (516, 512), (1028, 512), (1540, 512)]

BF = ml_dtypes.bfloat16


def _rope_lhst():
    k_max = C // 2
    theta = 1.0 / 10000.0 ** (np.arange(k_max, dtype=np.float64) / k_max)
    cs, sn = np.cos(theta), np.sin(theta)
    out = np.zeros((NG, P, P), np.float32)
    for g in range(NG):
        for jl in range(64):
            j = 64 * g + jl
            out[g, 2 * jl, 2 * jl] = cs[j]
            out[g, 2 * jl + 1, 2 * jl] = -sn[j]
            out[g, 2 * jl, 2 * jl + 1] = sn[j]
            out[g, 2 * jl + 1, 2 * jl + 1] = cs[j]
    return out.astype(BF)


def _zmask():
    out = np.zeros((NG, P, NH), np.float32)
    for g in range(NG):
        for p in range(P):
            out[g, p, (128 * g + p) // HD] = 1.0 / 8192.0
    return out.astype(BF)


def _zsel():
    out = np.zeros((NG, NH, P), np.float32)
    for g in range(NG):
        for p in range(P):
            out[g, (128 * g + p) // HD, p] = 1.0
    return out



ENGS = ("pe", "act", "dve", "pool", "sp")


class Op:
    __slots__ = ("idx", "eng", "kind", "emit", "reads", "writes", "token",
                 "need_inc", "deps", "tag")

    def __init__(self, idx, eng, kind, emit, reads, writes, tag=""):
        self.idx = idx
        self.eng = eng
        self.kind = kind
        self.emit = emit
        self.reads = reads
        self.writes = writes
        self.token = None
        self.need_inc = False
        self.deps = []
        self.tag = tag


class Stream:

    def __init__(self, bld, name, eng, n_sems=4):
        self.bld = bld
        self.name = name
        self.eng = eng
        self.n_sems = n_sems
        self.sems = [f"{name}{i}" for i in range(n_sems)]
        self.counts = [0] * n_sems
        self.wave_i = 0
        self.prev_wave_final = {}

    def wave(self, dmas):
        si = self.wave_i % self.n_sems
        sem = self.sems[si]
        if self.wave_i >= self.n_sems:
            old = self.prev_wave_final[si]
            self.bld.raw_wait(self.eng, sem, old)
        for emit, reads, writes in dmas:
            self.counts[si] += 16
            val = self.counts[si]
            op = self.bld.op(self.eng, emit, reads, writes, kind="d",
                             tag=f"{self.name}w{self.wave_i}")
            op.token = (sem, val)
        self.prev_wave_final[si] = self.counts[si]
        tok = (sem, self.counts[si])
        self.wave_i += 1
        return tok


class Builder:
    def __init__(self, nc, sim_sync=False, hw_same_engine_sync=True):
        self.nc = nc
        self.sim_sync = sim_sync
        self.hw_same_engine_sync = hw_same_engine_sync
        self.ops = []
        self.writers = []
        self.readers = []
        self.sem_names = set()
        self.streams = {}
        self.raw_waits = {}
        self.ar_base = 16416
        self.ar_top = 229344
        self.free_list = [(self.ar_base, self.ar_top)]
        self.tensors = {}
        self.sb_handles = {}

    def alloc(self, name, shape, dtype, align=64, top=None):
        size = int(np.prod(shape[1:])) * mybir.dt.size(dtype)
        size_al = -(-size // align) * align
        from_top = (size_al < 20480) if top is None else top
        blocks = list(enumerate(self.free_list))
        if from_top:
            blocks = reversed(blocks)
        for i, (a, b) in blocks:
            a_al = -(-a // align) * align
            if b - a_al >= size_al:
                if from_top:
                    addr = (b - size_al) // align * align
                    if addr < a:
                        continue
                else:
                    addr = a_al
                pieces = []
                if a < addr:
                    pieces.append((a, addr))
                if addr + size_al < b:
                    pieces.append((addr + size_al, b))
                self.free_list[i:i + 1] = pieces
                h = self.nc.alloc_sbuf_tensor_at(name, list(shape), dtype, offset=addr)
                self.tensors[h.name] = (h, addr, size_al)
                return h
        live = sorted((a, a + s, n) for n, (_, a, s) in self.tensors.items())
        raise MemoryError(f"SBUF arena full allocating {name} ({size_al}B/part); "
                          f"free={self.free_list}; live={live}")

    def free(self, h):
        name = h.name
        _, addr, size = self.tensors.pop(name)
        self.free_list.append((addr, addr + size))
        self.free_list.sort()
        merged = []
        for a, b in self.free_list:
            if merged and merged[-1][1] >= a:
                merged[-1] = (merged[-1][0], max(merged[-1][1], b))
            else:
                merged.append((a, b))
        self.free_list = [tuple(x) for x in merged]

    def rs(self, h, c0=None, c1=None, p0=0, p1=None):
        name = h.name if hasattr(h, "name") else h
        hh, addr, _ = self.tensors[name]
        sz = mybir.dt.size(hh.dtype)
        ncols = hh.shape[1]
        if c0 is None:
            c0, c1 = 0, ncols
        if p1 is None:
            p1 = hh.shape[0]
        return ("S", p0, p1, addr + c0 * sz, addr + c1 * sz)

    @staticmethod
    def rp(bank_name):
        return ("P", bank_name)

    @staticmethod
    def rd(dram_handle):
        return ("D", dram_handle.name if hasattr(dram_handle, "name") else dram_handle)

    @staticmethod
    def _overlap(r1, r2):
        if r1[0] != r2[0]:
            return False
        if r1[0] == "S":
            return not (r1[2] <= r2[1] or r2[2] <= r1[1] or
                        r1[4] <= r2[3] or r2[4] <= r1[3])
        return r1[1] == r2[1]

    def op(self, eng, emit, reads=(), writes=(), kind="c", tag=""):
        o = Op(len(self.ops), eng, kind, emit, list(reads), list(writes), tag)
        deps = {}
        for r in o.reads:
            for res, w in self.writers:
                if self._overlap(res, r):
                    deps[w.idx] = w
        for w_ in o.writes:
            for res, w in self.writers:
                if self._overlap(res, w_):
                    deps[w.idx] = w
            for res, rdr in self.readers:
                if self._overlap(res, w_):
                    deps[rdr.idx] = rdr
        for d in deps.values():
            if d is o:
                continue
            if d.eng == o.eng and d.kind == "c" and o.kind == "c":
                if not self.sim_sync:
                    if o.eng == "pe" or not self.hw_same_engine_sync:
                        continue
            o.deps.append(d)
        for w_ in o.writes:
            self.writers = [(res, w) for (res, w) in self.writers
                            if not self._covered(res, w_)]
            self.readers = [(res, r) for (res, r) in self.readers
                            if not self._covered(res, w_)]
            self.writers.append((w_, o))
        for r in o.reads:
            self.readers.append((r, o))
        self.ops.append(o)
        return o

    @staticmethod
    def _covered(res, w):
        if res[0] != w[0]:
            return False
        if res[0] == "S":
            return (w[1] <= res[1] and res[2] <= w[2] and
                    w[3] <= res[3] and res[4] <= w[4])
        return res[1] == w[1]

    def raw_wait(self, eng, sem, val):
        o = Op(len(self.ops), eng, "w", None, [], [], "gate")
        o.token = (sem, val)
        self.ops.append(o)

    def stream(self, name, eng, n_sems=4):
        s = Stream(self, name, eng, n_sems)
        self.streams[name] = s
        return s

    def finalize(self, eng_objs, final_eng="sp"):
        nc = self.nc
        for o in self.ops:
            for d in o.deps:
                if d.token is None:
                    d.need_inc = True
        eng_counts = {e: 0 for e in ENGS}
        for o in self.ops:
            if o.kind == "c" and o.need_inc:
                eng_counts[o.eng] += 1
                o.token = (f"eng_{o.eng}", eng_counts[o.eng])
        sems = {}
        for o in self.ops:
            if o.token is not None:
                sems.setdefault(o.token[0], 0)
                sems[o.token[0]] = max(sems[o.token[0]], o.token[1])
        sched = {e: [] for e in ENGS}
        waited = {e: {} for e in ENGS}
        for o in self.ops:
            if o.kind == "w":
                sem, val = o.token
                if waited[o.eng].get(sem, 0) < val:
                    sched[o.eng].append(("wait", sem, val))
                    waited[o.eng][sem] = val
                continue
            for d in o.deps:
                sem, val = d.token
                if waited[o.eng].get(sem, 0) < val:
                    sched[o.eng].append(("wait", sem, val))
                    waited[o.eng][sem] = val
            sched[o.eng].append(("inst", o))
        final_waits = []
        for sem, top in sems.items():
            if waited[final_eng].get(sem, 0) < top:
                final_waits.append((sem, top))
        sem_handles = {}
        with ExitStack() as ctx:
            for sem in sems:
                sem_handles[sem] = ctx.enter_context(nc.semaphore(sem))
            blk = ctx.enter_context(nc.Block())

            def run_engine(ename):
                def body(eobj):
                    for item in sched[ename]:
                        if item[0] == "wait":
                            eobj.wait_ge(sem_handles[item[1]], item[2])
                        else:
                            o = item[1]
                            inst = o.emit(eobj)
                            if o.token is not None:
                                sem, val = o.token
                                prev = getattr(o, "_prev_val", None)
                                amt = 16 if o.kind == "d" else 1
                                inst.then_inc(sem_handles[sem], amt)
                    if ename == final_eng:
                        for sem, top in final_waits:
                            eobj.wait_ge(sem_handles[sem], top)
                return body

            blk.tensor(run_engine("pe"))
            blk.scalar(run_engine("act"))
            blk.vector(run_engine("dve"))
            blk.gpsimd(run_engine("pool"))
            blk.sync(run_engine("sp"))
        return dict(n_ops=len(self.ops), sems=len(sems))



SMALLS = {}


def _smalls_layout():
    cols = 0
    spec = {}
    def add(name, n):
        nonlocal cols
        spec[name] = (cols, n)
        cols += n
    add("g1", NG); add("b1", NG); add("g2", NG); add("b2", NG)
    for cv in ("cpe1", "dwc", "lepe", "cpe2"):
        add(cv + "_w", 3 * NG)
        add(cv + "_b", NG)
    add("in_b", NG); add("actp_b", NG); add("qk_b", 2 * NG); add("out_b", NG)
    add("fc1_b", NGH); add("fc2_b", NG)
    add("w1_s", NGH); add("w2_s", NGH); add("h1_s", NGH); add("h2_s", NGH)
    add("w3_s", NG)
    add("lmask", HALO); add("rmask", HALO)
    add("bsel", 2)
    return spec, cols


SMALLS, SMALLS_COLS = _smalls_layout()


def build_program(upto="B4", fake_cc=False, sim_sync=False, hw_same_engine_sync=True):
    nc = bass.Bass("TRN2", target_bir_lowering=False)
    bld = Builder(nc, sim_sync=sim_sync, hw_same_engine_sync=hw_same_engine_sync)
    phases_all = ["PH1", "PH2", "PH3", "PH4", "PH5", "PH6", "PH7", "PH8",
                  "PH9", "F1", "F2", "F3", "F4", "B0", "B1", "B2", "B3", "B4"]
    phases = phases_all[:phases_all.index(upto) + 1]

    dt_in = {}
    def din(name, shape, dtype=F32):
        dt_in[name] = nc.dram_tensor(name, list(shape), dtype, kind="ExternalInput")
        return dt_in[name]

    u_t = din("u_t", [P, NG * TP])
    smalls = din("smalls", [P, SMALLS_COLS])
    w_in = din("w_in", [P, NG * C])
    w_actp = din("w_actp", [P, NG * C])
    w_qk = din("w_qk", [P, NG * 2 * C])
    w_out = din("w_out", [P, NG * C])
    w_fc1 = din("w_fc1", [NGH, P, NG * P])
    w_fc2 = din("w_fc2", [NG, P, NGH * P])
    w_w1 = din("w_w1", [NGH, P, NG * P])
    w_w2 = din("w_w2", [NGH, P, NG * P])
    w_h1 = din("w_h1", [NGH, P, NGH * P])
    w_h2 = din("w_h2", [NGH, P, NGH * P])
    w_w3 = din("w_w3", [NG, P, NGH * P])

    out_t = nc.dram_tensor("out_t", [P, NG * T], F32, kind="ExternalOutput")

    a_sp = nc.dram_tensor("a_sp", [P, NG * TP], F32, kind="Internal")
    q_sp = nc.dram_tensor("q_sp", [P, NG * TP], BF16, kind="Internal")
    ar_sp = nc.dram_tensor("ar_sp", [P, NG * TP], BF16, kind="Internal")
    lp_sp = nc.dram_tensor("lp_sp", [P, NG * TP], BF16, kind="Internal")
    x2_sp = nc.dram_tensor("x2_sp", [P, NG * T], F32, kind="Internal")
    h0_sp = nc.dram_tensor("h0_sp", [P, NGH * T], BF16, kind="Internal")
    h1_sp = nc.dram_tensor("h1_sp", [P, NGH * T], BF16, kind="Internal")
    h2_sp = nc.dram_tensor("h2_sp", [P, NGH * T], BF16, kind="Internal")
    CCN = NG * P + NG
    cc_in = nc.dram_tensor("cc_in", [2, P * CCN], F32, kind="Internal")
    cc_out = nc.dram_tensor("cc_out", [2, P * CCN], F32, kind="Internal")

    rope_d = nc.inline_tensor(np.ascontiguousarray(_rope_lhst().reshape(NG * P, P)), "rope_d")
    zmask_d = nc.inline_tensor(np.ascontiguousarray(_zmask().reshape(NG * P, NH)), "zmask_d")
    zsel_d = nc.inline_tensor(np.ascontiguousarray(_zsel().reshape(NG * NH, P)), "zsel_d")
    ident_d = nc.inline_tensor(np.eye(P, dtype=BF), "ident_d")
    onesc_d = nc.inline_tensor(np.ones((P, 1), np.float32), "onesc_d")
    onesr_d = nc.inline_tensor(np.ones((1, P), np.float32), "onesr_d")

    psum = {}
    with ExitStack() as psctx:
        for nm, dt in [("pmm0", F32), ("pmm1", F32), ("pmm2", F32), ("pst", F32),
                       ("pkv0", F32), ("pkv1", F32), ("ptra", BF16), ("ptrb", BF16)]:
            psum[nm] = psctx.enter_context(nc.psum_tensor(nm, [P, 512], dt))
        _build_body(nc, bld, phases, fake_cc, locals())
        meta = bld.finalize(None)
    return nc, meta


def _build_body(nc, bld, phases, fake_cc, env):
    psum = env["psum"]
    u_t, smalls = env["u_t"], env["smalls"]
    out_t = env["out_t"]
    a_sp, q_sp, ar_sp, lp_sp = env["a_sp"], env["q_sp"], env["ar_sp"], env["lp_sp"]
    x2_sp, h1_sp, h2_sp = env["x2_sp"], env["h1_sp"], env["h2_sp"]
    cc_in, cc_out = env["cc_in"], env["cc_out"]
    CCN = env["CCN"]

    st_w = bld.stream("sw", "pool", 8)
    st_io = bld.stream("sio", "sp", 12)
    st_h = bld.stream("sh", "act", 8)
    st_cc = bld.stream("scc", "pool", 2)

    R = bld.rs
    RP = bld.rp
    RD = bld.rd

    cst = {}
    cst["ones_c"] = bld.alloc("ones_c", [P, 1], F32)
    cst["ones_r"] = bld.alloc("ones_r", [1, P], F32)
    cst["ident"] = bld.alloc("ident", [P, P], BF16)
    cst["rope"] = bld.alloc("rope", [P, NG * P], BF16)
    cst["zmask"] = bld.alloc("zmask", [P, NG * NH], BF16)
    cst["zsel"] = bld.alloc("zsel", [NH, NG * P], F32)
    cst["sm"] = bld.alloc("sm", [P, SMALLS_COLS], F32)

    init_dmas = []
    for dname, key, hsl in [("onesc_d", "ones_c", None), ("onesr_d", "ones_r", None),
                            ("ident_d", "ident", None), ("rope_d", "rope", "g"),
                            ("zmask_d", "zmask", "g"), ("zsel_d", "zsel", "g16")]:
        dh = env[dname] if dname in env else None
    def dma(stream, out_ap, in_ap, reads, writes, tag=""):
        return stream.wave([(lambda e, o=out_ap, i=in_ap: e.dma_start(out=o, in_=i),
                             reads, writes)])

    rope_d, zmask_d, zsel_d = env["rope_d"], env["zmask_d"], env["zsel_d"]
    ident_d, onesc_d, onesr_d = env["ident_d"], env["onesc_d"], env["onesr_d"]

    dma(st_io, cst["ones_c"][:], onesc_d.ap(), [RD(onesc_d)], [R(cst["ones_c"])])
    dma(st_io, cst["ones_r"][:], onesr_d.ap(), [RD(onesr_d)], [R(cst["ones_r"])])
    dma(st_io, cst["ident"][:], ident_d.ap(), [RD(ident_d)], [R(cst["ident"])])
    dma(st_io, cst["rope"][:].rearrange("p (g f) -> p g f", g=NG),
        rope_d.ap().rearrange("(g p) f -> p g f", g=NG),
        [RD(rope_d)], [R(cst["rope"])])
    dma(st_io, cst["zmask"][:].rearrange("p (g f) -> p g f", g=NG),
        zmask_d.ap().rearrange("(g p) f -> p g f", g=NG),
        [RD(zmask_d)], [R(cst["zmask"])])
    dma(st_io, cst["zsel"][:].rearrange("p (g f) -> p g f", g=NG),
        zsel_d.ap().rearrange("(g p) f -> p g f", g=NG),
        [RD(zsel_d)], [R(cst["zsel"])])
    dma(st_io, cst["sm"][:], smalls.ap(), [RD(smalls)], [R(cst["sm"])])

    def smcol(name, i=0):
        off, n = SMALLS[name]
        assert i < n
        return cst["sm"][:, off + i:off + i + 1]

    def smres(name):
        off, n = SMALLS[name]
        return R(cst["sm"], off, off + n)

    sm = cst["sm"]

    def mm(bank, bc0, bc1, lhsT_ap, rhs_ap, reads, start, stop, tag=""):
        ps = psum[bank]
        bld.op("pe",
               lambda e, o=ps[:, bc0:bc1], l=lhsT_ap, r=rhs_ap, st=start, sp=stop:
                   e.matmul(o, l, r, start=st, stop=sp),
               reads=reads + [RP(bank)] if not start else reads,
               writes=[RP(bank)], tag=tag)

    def act(out_ap, in_ap, func, reads, writes, bias=None, scale=None, tag=""):
        kw = {}
        if bias is not None:
            kw["bias"] = bias
        if scale is not None:
            kw["scale"] = scale
        bld.op("act", lambda e, o=out_ap, i=in_ap, f=func, k=kw: e.activation(o, i, f, **k),
               reads=reads, writes=writes, tag=tag)

    def dve(meth, args, reads, writes, tag=""):
        bld.op("dve", lambda e, m=meth, a=args: getattr(e, m)(*a),
               reads=reads, writes=writes, tag=tag)

    rows2 = [bld.alloc("rows0", [1, 4 * 512], F32)] * 2
    sq2 = [bld.alloc("sqs0", [P, NG * 512], F32)] * 2
    rbmb2 = [bld.alloc(f"rbmb{i}", [P, 2 * 512], F32) for i in range(2)]
    _ln_pi = [0]

    def ln_pass(src_h, src_base_fn, subs, dst_write, gamma=None, beta=None, tag="ln",
                banks=("pst", "pmm2", "pmm0", "pmm1")):
        bk_sum, bk_sq, bk_rb, bk_mb = banks
        for (c0, w) in subs:
            pi = _ln_pi[0] % 2
            _ln_pi[0] += 1
            rows = rows2[pi]
            sq_sub = sq2[pi]
            rbmb = rbmb2[pi]
            for g in range(NG):
                sap, sres = src_base_fn(g, c0, w)
                act(sq_sub[:, g * 512:g * 512 + w], sap, AF.Square,
                    [sres], [R(sq_sub, g * 512, g * 512 + w)], tag=f"{tag}sq")
            for g in range(NG):
                sap, sres = src_base_fn(g, c0, w)
                ps = psum[bk_sum]
                bld.op("pe", lambda e, o=ps[0:1, 0:w], l=cst["ones_c"][:], r=sap,
                       st=(g == 0), sp=(g == NG - 1): e.matmul(o, l, r, start=st, stop=sp),
                       reads=[sres, R(cst["ones_c"])] + ([] if g == 0 else [RP(bk_sum)]),
                       writes=[RP(bk_sum)], tag=f"{tag}st1")
            for g in range(NG):
                ps = psum[bk_sq]
                bld.op("pe", lambda e, o=ps[0:1, 0:w], l=cst["ones_c"][:],
                       r=sq_sub[:, g * 512:g * 512 + w],
                       st=(g == 0), sp=(g == NG - 1): e.matmul(o, l, r, start=st, stop=sp),
                       reads=[R(sq_sub, g * 512, g * 512 + w), R(cst["ones_c"])] +
                             ([] if g == 0 else [RP(bk_sq)]),
                       writes=[RP(bk_sq)], tag=f"{tag}st2")
            srow = rows[0:1, 0:w]
            vrow = rows[0:1, 512:512 + w]
            rrow = rows[0:1, 1024:1024 + w]
            nmrow = rows[0:1, 1536:1536 + w]
            dve("tensor_scalar", (srow, psum[bk_sum][0:1, 0:w], 1.0 / C, None, ALU.mult),
                [RP(bk_sum)], [R(rows, 0, w, 0, 1)], tag=f"{tag}r1")
            dve("tensor_scalar", (vrow, psum[bk_sq][0:1, 0:w], 1.0 / C, None, ALU.mult),
                [RP(bk_sq)], [R(rows, 512, 512 + w, 0, 1)], tag=f"{tag}r2")
            dve("tensor_tensor", (nmrow, srow, srow, ALU.mult),
                [R(rows, 0, w, 0, 1)], [R(rows, 1536, 1536 + w, 0, 1)], tag=f"{tag}r3")
            dve("tensor_tensor", (vrow, vrow, nmrow, ALU.subtract),
                [R(rows, 512, 512 + w, 0, 1), R(rows, 1536, 1536 + w, 0, 1)],
                [R(rows, 512, 512 + w, 0, 1)], tag=f"{tag}r4")
            dve("tensor_scalar", (vrow, vrow, 1e-5, None, ALU.add),
                [R(rows, 512, 512 + w, 0, 1)], [R(rows, 512, 512 + w, 0, 1)], tag=f"{tag}r5")
            act(vrow, vrow, AF.Sqrt, [R(rows, 512, 512 + w, 0, 1)],
                [R(rows, 512, 512 + w, 0, 1)], tag=f"{tag}r6")
            dve("reciprocal", (rrow, vrow),
                [R(rows, 512, 512 + w, 0, 1)], [R(rows, 1024, 1024 + w, 0, 1)], tag=f"{tag}r7")
            dve("scalar_tensor_tensor", (nmrow, srow, -1.0, rrow, ALU.mult, ALU.mult),
                [R(rows, 0, w, 0, 1), R(rows, 1024, 1024 + w, 0, 1)],
                [R(rows, 1536, 1536 + w, 0, 1)], tag=f"{tag}r8")
            bld.op("pe", lambda e, o=psum[bk_rb][:, 0:w], l=cst["ones_r"][:], r=rrow:
                   e.matmul(o, l, r, start=True, stop=True),
                   reads=[R(rows, 1024, 1024 + w, 0, 1), R(cst["ones_r"])],
                   writes=[RP(bk_rb)], tag=f"{tag}b1")
            bld.op("pe", lambda e, o=psum[bk_mb][:, 0:w], l=cst["ones_r"][:], r=nmrow:
                   e.matmul(o, l, r, start=True, stop=True),
                   reads=[R(rows, 1536, 1536 + w, 0, 1), R(cst["ones_r"])],
                   writes=[RP(bk_mb)], tag=f"{tag}b2")
            dve("tensor_copy", (rbmb[:, 0:w], psum[bk_rb][:, 0:w]),
                [RP(bk_rb)], [R(rbmb, 0, w)], tag=f"{tag}be1")
            dve("tensor_copy", (rbmb[:, 512:512 + w], psum[bk_mb][:, 0:w]),
                [RP(bk_mb)], [R(rbmb, 512, 512 + w)], tag=f"{tag}be2")
            rb_ap = rbmb[:, 0:w]
            mb_ap = rbmb[:, 512:512 + w]
            for g in range(NG):
                dst_write(g, c0, w, rb_ap, mb_ap, [R(rbmb, 0, w), R(rbmb, 512, 512 + w)])

    def conv_pass(src_h, dst_emit, taps, bias, subs, tmp_h, tag="cv"):
        for (c0, w) in subs:
            for g in range(NG):
                base = g * TP
                s_rs = lambda a0, a1: R(src_h, base + a0, base + a1)
                tc = (g % 2) * 512
                t_ap = tmp_h[:, tc:tc + w]
                t_rs = R(tmp_h, tc, tc + w)
                off, _ = SMALLS[taps]
                w0 = sm[:, off + g * 3 + 0:off + g * 3 + 1]
                w1 = sm[:, off + g * 3 + 1:off + g * 3 + 2]
                w2 = sm[:, off + g * 3 + 2:off + g * 3 + 3]
                boff, _ = SMALLS[bias]
                bcol = sm[:, boff + g:boff + g + 1]
                dve("tensor_scalar", (t_ap, src_h[:, base + c0:base + c0 + w],
                                      w1, bcol, ALU.mult, ALU.add),
                    [s_rs(c0, c0 + w), smres(taps), smres(bias)], [t_rs], tag=f"{tag}c")
                lo = max(c0, 1)
                if lo < c0 + w:
                    dve("scalar_tensor_tensor",
                        (tmp_h[:, tc + lo - c0:tc + w], src_h[:, base + lo - 1:base + c0 + w - 1],
                         w0, tmp_h[:, tc + lo - c0:tc + w], ALU.mult, ALU.add),
                        [s_rs(lo - 1, c0 + w - 1), t_rs, smres(taps)], [t_rs], tag=f"{tag}l")
                hi = min(c0 + w, TP - 1)
                if hi > c0:
                    dve("scalar_tensor_tensor",
                        (tmp_h[:, tc:tc + hi - c0], src_h[:, base + c0 + 1:base + hi + 1],
                         w2, tmp_h[:, tc:tc + hi - c0], ALU.mult, ALU.add),
                        [s_rs(c0 + 1, hi + 1), t_rs, smres(taps)], [t_rs], tag=f"{tag}r")
                dst_emit(g, c0, w, t_ap, [t_rs])

    def mask_halo(h, tag=""):
        for g in range(NG):
            base = g * TP
            dve("tensor_tensor", (h[:, base:base + HALO], h[:, base:base + HALO],
                                  sm[:, SMALLS["lmask"][0]:SMALLS["lmask"][0] + HALO],
                                  ALU.mult),
                [R(h, base, base + HALO), smres("lmask")], [R(h, base, base + HALO)],
                tag=f"mskl{tag}")
            dve("tensor_tensor", (h[:, base + TP - HALO:base + TP],
                                  h[:, base + TP - HALO:base + TP],
                                  sm[:, SMALLS["rmask"][0]:SMALLS["rmask"][0] + HALO],
                                  ALU.mult),
                [R(h, base + TP - HALO, base + TP), smres("rmask")],
                [R(h, base + TP - HALO, base + TP)], tag=f"mskr{tag}")

    def wload(dram_h, sb_h, tag=""):
        return dma(st_w, sb_h[:], dram_h.ap(), [RD(dram_h)], [R(sb_h)], tag)

    def proj(w_sb, F_out, x_h, x_base_fn, subs, evac, banks=("pmm0", "pmm1", "pmm2"),
             n_k=NG, tag="pj"):
        bi = 0
        for (c0, w) in subs:
            for oc in range(F_out // P):
                bank = banks[bi % len(banks)]
                bi += 1
                for k in range(n_k):
                    l_ap = w_sb[:, k * F_out + oc * P:k * F_out + (oc + 1) * P]
                    x_ap, x_res = x_base_fn(k, c0, w)
                    ps = psum[bank]
                    bld.op("pe",
                           lambda e, o=ps[:, 0:w], l=l_ap, r=x_ap,
                           st=(k == 0), sp=(k == n_k - 1): e.matmul(o, l, r, start=st, stop=sp),
                           reads=[R(w_sb, k * F_out + oc * P, k * F_out + (oc + 1) * P),
                                  x_res] + ([] if k == 0 else [RP(bank)]),
                           writes=[RP(bank)], tag=f"{tag}mm")
                evac(oc, c0, w, bank, [RP(bank)])

    xn0b = bld.alloc("xn0b", [P, NG * TP], BF16)
    bufs = {}

    def load_u_sub(c0, w, tag="u"):
        u_sub = bufs["u_sub"]
        tok = dma(st_io, u_sub[:, 0:NG * 512].rearrange("p (g m) -> p g m", g=NG)[:, :, 0:w],
                  u_t.ap().rearrange("p (g m) -> p g m", g=NG)[:, :, c0:c0 + w],
                  [RD(u_t)], [R(u_sub)], tag)
        return tok

    cvt = bld.alloc("cvt", [P, 2 * 512], F32)
    if "PH1" in phases:
        bufs["u_sub"] = bld.alloc("u_sub", [P, NG * 512], F32)
        u_sub = bufs["u_sub"]
        for (c0, w) in SUBS:
            load_u_sub(c0, w)
            def srcfn(g, cc0, ww, _c0=c0):
                return (u_sub[:, g * 512:g * 512 + ww], R(u_sub, g * 512, g * 512 + ww))
            def wr(g, cc0, ww, rb, mb, deps, _c0=c0):
                t = cvt[:, 0:ww]
                tres = R(cvt, 0, ww)
                dve("tensor_tensor", (t, u_sub[:, g * 512:g * 512 + ww], rb, ALU.mult),
                    [R(u_sub, g * 512, g * 512 + ww)] + deps, [tres], tag="ln0a")
                o = xn0b[:, g * TP + _c0:g * TP + _c0 + ww]
                dve("tensor_tensor", (o, t, mb, ALU.add), [tres] + deps,
                    [R(xn0b, g * TP + _c0, g * TP + _c0 + ww)], tag="ln0b")
            ln_pass(u_sub, srcfn, [(c0, w)], wr, tag="ln0")
        bld.free(u_sub)

    thunks = []
    if "PH1" in phases:
        a_h = bld.alloc("a_h", [P, NG * TP], F32)
        def a_emit(g, c0, w, t_ap, deps):
            o = a_h[:, g * TP + c0:g * TP + c0 + w]
            ores = R(a_h, g * TP + c0, g * TP + c0 + w)
            dve("tensor_tensor", (o, t_ap, xn0b[:, g * TP + c0:g * TP + c0 + w], ALU.add),
                deps + [R(xn0b, g * TP + c0, g * TP + c0 + w)], [ores], tag="a")
        for (c0, w) in SUBS:
            thunks.append(lambda c0=c0, w=w: conv_pass(
                xn0b, a_emit, "cpe1_w", "cpe1_b", [(c0, w)], cvt, tag="cv1"))
        thunks.append(lambda: dma(st_io, a_sp.ap(), a_h[:], [R(a_h)], [RD(a_sp)],
                                  "aspill"))

    if "PH2" in phases:
        xn1 = bld.alloc("xn1", [P, NG * TP], BF16)
        def srcfn2(g, c0, w):
            return (a_h[:, g * TP + c0:g * TP + c0 + w], R(a_h, g * TP + c0, g * TP + c0 + w))
        def wr2(g, c0, w, rb, mb, deps):
            t = cvt[:, 0:w]
            tres = R(cvt, 0, w)
            dve("tensor_tensor", (t, a_h[:, g * TP + c0:g * TP + c0 + w], rb, ALU.mult),
                [R(a_h, g * TP + c0, g * TP + c0 + w)] + deps, [tres], tag="ln1a")
            dve("tensor_tensor", (t, t, mb, ALU.add), [tres] + deps, [tres], tag="ln1b")
            o = xn1[:, g * TP + c0:g * TP + c0 + w]
            dve("tensor_scalar", (o, t, smcol("g1", g), smcol("b1", g), ALU.mult, ALU.add),
                [tres, smres("g1"), smres("b1")],
                [R(xn1, g * TP + c0, g * TP + c0 + w)], tag="ln1c")
        for (c0, w) in SUBS:
            thunks.append(lambda c0=c0, w=w: ln_pass(
                a_h, srcfn2, [(c0, w)], wr2, tag="ln1",
                banks=("pkv0", "pkv1", "pkv0", "pkv1")))

    f1_units = []
    if "F1" in phases:
        wslr = [bld.alloc(f"w_sl{i}", [P, 2 * NG * P], BF16) for i in range(4)]
        t1s = bld.alloc("t1s", [P, 4 * 512], F32)
        h0ev = bld.alloc("h0ev", [P, 2 * 512], BF16)
        h0_sp = env["h0_sp"]
        w_w1, w_w2 = env["w_w1"], env["w_w2"]

        def f1_unit(oc):
            slb = wslr[oc % 4]
            dma(st_w, slb[:, 0:NG * P], w_w1.ap()[oc], [RD(w_w1)], [R(slb, 0, NG * P)],
                f"w1s{oc}")
            dma(st_w, slb[:, NG * P:2 * NG * P], w_w2.ap()[oc], [RD(w_w2)],
                [R(slb, NG * P, 2 * NG * P)], f"w2s{oc}")
            banks4w = ("pmm0", "pmm1", "pmm2", "pst")
            for k in range(NG):
                for si, (c0, w) in enumerate(RSUBS):
                    b1 = banks4w[si]
                    bld.op("pe", lambda e, o=psum[b1][:, 0:w],
                           l=slb[:, k * P:(k + 1) * P],
                           r=xn0b[:, k * TP + c0:k * TP + c0 + w],
                           st=(k == 0), sp=(k == NG - 1): e.matmul(o, l, r, start=st, stop=sp),
                           reads=[R(slb, k * P, (k + 1) * P),
                                  R(xn0b, k * TP + c0, k * TP + c0 + w)] +
                                 ([] if k == 0 else [RP(b1)]),
                           writes=[RP(b1)], tag="w1mm")
            for si, (c0, w) in enumerate(RSUBS):
                b1 = banks4w[si]
                act(t1s[:, si * 512:si * 512 + w], psum[b1][:, 0:w], AF.Silu,
                    [RP(b1), smres("w1_s")], [R(t1s, si * 512, si * 512 + w)],
                    scale=smcol("w1_s", oc), tag="f1silu")
            for k in range(NG):
                for si, (c0, w) in enumerate(RSUBS):
                    b2 = banks4w[si]
                    bld.op("pe", lambda e, o=psum[b2][:, 0:w],
                           l=slb[:, NG * P + k * P:NG * P + (k + 1) * P],
                           r=xn0b[:, k * TP + c0:k * TP + c0 + w],
                           st=(k == 0), sp=(k == NG - 1): e.matmul(o, l, r, start=st, stop=sp),
                           reads=[R(slb, NG * P + k * P, NG * P + (k + 1) * P),
                                  R(xn0b, k * TP + c0, k * TP + c0 + w)] +
                                 ([] if k == 0 else [RP(b2)]),
                           writes=[RP(b2)], tag="w2mm")
            for si, (c0, w) in enumerate(RSUBS):
                b2 = banks4w[si]
                par = si % 2
                dve("scalar_tensor_tensor",
                    (h0ev[:, par * 512:par * 512 + w], psum[b2][:, 0:w],
                     smcol("w2_s", oc), t1s[:, si * 512:si * 512 + w], ALU.mult, ALU.mult),
                    [RP(b2), R(t1s, si * 512, si * 512 + w), smres("w2_s")],
                    [R(h0ev, par * 512, par * 512 + w)], tag="f1mul")
                hcol = oc * T + (c0 - R0)
                dma(st_h, h0_sp.ap()[:, hcol:hcol + w], h0ev[:, par * 512:par * 512 + w],
                    [R(h0ev, par * 512, par * 512 + w)], [RD(h0_sp)], "h0sp")

        f1_units = [lambda oc=oc: f1_unit(oc) for oc in range(NGH)]

    na, nb = len(f1_units), len(thunks)
    ia = ib = 0
    while ia < na or ib < nb:
        if ib >= nb or (ia < na and ia * (nb + 1) <= ib * (na + 1)):
            f1_units[ia]()
            ia += 1
        else:
            thunks[ib]()
            ib += 1

    if "F1" in phases:
        for _h in wslr:
            bld.free(_h)
        bld.free(t1s)
        bld.free(h0ev)
    if "PH2" in phases:
        bld.free(a_h)
    bld.free(xn0b)
    bld.free(sq2[0])
    bld.free(rows2[0])
    bld.free(rbmb2[0])
    bld.free(rbmb2[1])

    if "PH3" in phases:
        in_wb = bld.alloc("in_wb", [P, NG * C], BF16)
        wload(env["w_in"], in_wb, "w_in")
        xi0 = bld.alloc("xi0", [P, NG * TP], BF16)
        def xfn(k, c0, w):
            return (xn1[:, k * TP + c0:k * TP + c0 + w],
                    R(xn1, k * TP + c0, k * TP + c0 + w))
        def ev3(oc, c0, w, bank, deps):
            act(xi0[:, oc * TP + c0:oc * TP + c0 + w], psum[bank][:, 0:w], AF.Identity,
                deps + [smres("in_b")], [R(xi0, oc * TP + c0, oc * TP + c0 + w)],
                bias=smcol("in_b", oc), tag="ev3")
        proj(in_wb, C, xn1, xfn, SUBS, ev3, tag="inw")
        bld.free(in_wb)

    if "PH4" in phases:
        mask_halo(xi0, "xi0")
        xi = bld.alloc("xi", [P, NG * TP], BF16)
        def xi_emit(g, c0, w, t_ap, deps):
            act(xi[:, g * TP + c0:g * TP + c0 + w], t_ap, AF.Silu,
                deps, [R(xi, g * TP + c0, g * TP + c0 + w)], tag="silu_xi")
        conv_pass(xi0, xi_emit, "dwc_w", "dwc_b", SUBS, cvt, tag="cv2")
        mask_halo(xi, "xi")
        bld.free(xi0)

    if "PH5" in phases:
        qk_wb = bld.alloc("qk_wb", [P, NG * 2 * C], BF16)
        wload(env["w_qk"], qk_wb, "w_qk")
        q_h = bld.alloc("q_h", [P, NG * TP], BF16)
        krope = bld.alloc("krope", [P, NG * TP], BF16)
        k_sub = bld.alloc("k_sub", [P, NG * 512], BF16)
        elu_e = bld.alloc("elu_e", [P, 512], F32)
        elu_p = bld.alloc("elu_p", [P, 512], F32)
        ksum = bld.alloc("ksum", [P, NG], F32)
        ksum_t = bld.alloc("ksum_t", [P, NG], F32)
        dve("memset", (ksum[:], 0.0), [], [R(ksum)], tag="ks0")

        def xfn5(k, c0, w):
            return (xi[:, k * TP + c0:k * TP + c0 + w],
                    R(xi, k * TP + c0, k * TP + c0 + w))

        def elu_evac(oc, c0, w, bank, deps, out_ap, out_res):
            bcol = smcol("qk_b", oc)
            dve("tensor_scalar", (elu_e[:, 0:w], psum[bank][:, 0:w], bcol, 0.0,
                                  ALU.add, ALU.min),
                deps + [smres("qk_b")], [R(elu_e, 0, w)], tag="elmin")
            dve("tensor_scalar", (elu_p[:, 0:w], psum[bank][:, 0:w], bcol, 0.0,
                                  ALU.add, ALU.max),
                deps + [smres("qk_b")], [R(elu_p, 0, w)], tag="elmax")
            act(elu_e[:, 0:w], elu_e[:, 0:w], AF.Exp, [R(elu_e, 0, w)],
                [R(elu_e, 0, w)], tag="elexp")
            dve("tensor_tensor", (out_ap, elu_e[:, 0:w], elu_p[:, 0:w], ALU.add),
                [R(elu_e, 0, w), R(elu_p, 0, w)], [out_res], tag="eladd")

        for (c0, w) in SUBS:
            for oc in range(NG):
                bank = ("pmm0", "pmm1", "pmm2")[oc % 3]
                for k in range(NG):
                    l_ap = qk_wb[:, k * 2 * C + oc * P:k * 2 * C + (oc + 1) * P]
                    x_ap, x_res = xfn5(k, c0, w)
                    bld.op("pe", lambda e, o=psum[bank][:, 0:w], l=l_ap, r=x_ap,
                           st=(k == 0), sp=(k == NG - 1): e.matmul(o, l, r, start=st, stop=sp),
                           reads=[R(qk_wb, k * 2 * C + oc * P, k * 2 * C + (oc + 1) * P),
                                  x_res] + ([] if k == 0 else [RP(bank)]),
                           writes=[RP(bank)], tag="qmm")
                elu_evac(oc, c0, w, bank, [RP(bank)],
                         q_h[:, oc * TP + c0:oc * TP + c0 + w],
                         R(q_h, oc * TP + c0, oc * TP + c0 + w))
            for oc in range(NG):
                bank = ("pmm0", "pmm1", "pmm2")[oc % 3]
                for k in range(NG):
                    l_ap = qk_wb[:, k * 2 * C + (NG + oc) * P:k * 2 * C + (NG + oc + 1) * P]
                    x_ap, x_res = xfn5(k, c0, w)
                    bld.op("pe", lambda e, o=psum[bank][:, 0:w], l=l_ap, r=x_ap,
                           st=(k == 0), sp=(k == NG - 1): e.matmul(o, l, r, start=st, stop=sp),
                           reads=[R(qk_wb, k * 2 * C + (NG + oc) * P,
                                    k * 2 * C + (NG + oc + 1) * P),
                                  x_res] + ([] if k == 0 else [RP(bank)]),
                           writes=[RP(bank)], tag="kmm")
                elu_evac(NG + oc, c0, w, bank, [RP(bank)],
                         k_sub[:, oc * 512:oc * 512 + w],
                         R(k_sub, oc * 512, oc * 512 + w))
            r0 = max(c0, R0)
            r1_ = min(c0 + w, R1)
            if r0 < r1_:
                for g in range(NG):
                    dve("reduce_sum", (ksum_t[:, g:g + 1],
                                       k_sub[:, g * 512 + r0 - c0:g * 512 + r1_ - c0], AX.X),
                        [R(k_sub, g * 512 + r0 - c0, g * 512 + r1_ - c0)],
                        [R(ksum_t, g, g + 1)], tag="ksr")
                dve("tensor_tensor", (ksum[:], ksum[:], ksum_t[:], ALU.add),
                    [R(ksum), R(ksum_t)], [R(ksum)], tag="ksa")
            for g in range(NG):
                bank = ("pmm0", "pmm1")[g % 2]
                bld.op("pe", lambda e, o=psum[bank][:, 0:w],
                       l=cst["rope"][:, g * P:(g + 1) * P], r=k_sub[:, g * 512:g * 512 + w]:
                       e.matmul(o, l, r, start=True, stop=True),
                       reads=[R(cst["rope"], g * P, (g + 1) * P),
                              R(k_sub, g * 512, g * 512 + w)],
                       writes=[RP(bank)], tag="krmm")
                dve("tensor_copy", (krope[:, g * TP + c0:g * TP + c0 + w],
                                    psum[bank][:, 0:w]),
                    [RP(bank)], [R(krope, g * TP + c0, g * TP + c0 + w)], tag="krev")
        bld.free(qk_wb)
        bld.free(k_sub)
        bld.free(elu_e)
        bld.free(elu_p)
        bld.free(ksum_t)

    if "PH6" in phases:
        krt = bld.alloc("krt", [P, 2 * P], BF16)
        vht = bld.alloc("vht", [P, 2 * P], BF16)
        def kv_bank(g):
            return ("pkv0", g % 4) if g < 4 else ("pkv1", g % 4)

        def emit_tr(g, ti):
            tc0 = R0 + ti * P
            par = ti % 2
            tb = ("ptra", "ptrb")[par]
            bld.op("pe", lambda e, o=psum[tb][:, 0:P],
                   l=krope[:, g * TP + tc0:g * TP + tc0 + P], i=cst["ident"][:]:
                   e.transpose(o, l, i),
                   reads=[R(krope, g * TP + tc0, g * TP + tc0 + P), R(cst["ident"]),
                          RP(tb)],
                   writes=[RP(tb)], tag="trk")
            dve("tensor_copy", (krt[:, par * P:(par + 1) * P], psum[tb][:, 0:P]),
                [RP(tb)], [R(krt, par * P, (par + 1) * P)], tag="trke")
            bld.op("pe", lambda e, o=psum[tb][:, P:2 * P],
                   l=xi[:, g * TP + tc0:g * TP + tc0 + P], i=cst["ident"][:]:
                   e.transpose(o, l, i),
                   reads=[R(xi, g * TP + tc0, g * TP + tc0 + P), R(cst["ident"]),
                          RP(tb)],
                   writes=[RP(tb)], tag="trv")
            dve("tensor_copy", (vht[:, par * P:(par + 1) * P], psum[tb][:, P:2 * P]),
                [RP(tb)], [R(vht, par * P, (par + 1) * P)], tag="trve")

        def emit_kvmm(g, ti):
            par = ti % 2
            kb, slot = kv_bank(g)
            bld.op("pe", lambda e, o=psum[kb][:, slot * P:(slot + 1) * P],
                   l=krt[:, par * P:(par + 1) * P], r=vht[:, par * P:(par + 1) * P],
                   st=(ti == 0), sp=(ti == 15): e.matmul(o, l, r, start=st, stop=sp),
                   reads=[R(krt, par * P, (par + 1) * P), R(vht, par * P, (par + 1) * P)]
                         + ([] if ti == 0 else [RP(kb)]),
                   writes=[RP(kb)], tag="kvmm")

        for g in range(NG):
            emit_tr(g, 0)
            for ti in range(16):
                if ti < 15:
                    emit_tr(g, ti + 1)
                emit_kvmm(g, ti)
        ccs = bld.alloc("ccs", [P, 2 * CCN], F32)
        kv_sb = bld.alloc("kv_sb", [P, NG * P], F32)
        for g in range(NG):
            kb, slot = kv_bank(g)
            dve("tensor_scalar", (kv_sb[:, g * P:(g + 1) * P],
                                  psum[kb][:, slot * P:(slot + 1) * P],
                                  1.0 / 8192.0, None, ALU.mult),
                [RP(kb)], [R(kv_sb, g * P, (g + 1) * P)], tag="kvev")
        for b in range(2):
            dve("tensor_scalar", (ccs[:, b * CCN:b * CCN + NG * P], kv_sb[:],
                                  smcol("bsel", b), None, ALU.mult),
                [R(kv_sb), smres("bsel")], [R(ccs, b * CCN, b * CCN + NG * P)], tag="cckv")
            dve("tensor_scalar", (ccs[:, b * CCN + NG * P:(b + 1) * CCN], ksum[:],
                                  smcol("bsel", b), None, ALU.mult),
                [R(ksum), smres("bsel")],
                [R(ccs, b * CCN + NG * P, (b + 1) * CCN)], tag="ccks")
        dma(st_cc, cc_in.ap().rearrange("b (p m) -> p b m", p=P),
            ccs[:].rearrange("p (b m) -> p b m", b=2),
            [R(ccs)], [RD(cc_in)], "ccst")
        if fake_cc:
            dma(st_cc, cc_out.ap(), cc_in.ap(), [RD(cc_in)], [RD(cc_out)], "fakecc")
        else:
            o = bld.op("pool", lambda e: e.collective_compute(
                "AllReduce", ALU.add,
                replica_groups=[[0, 1, 2, 3, 4, 5, 6, 7]],
                ins=[cc_in.ap().opt()], outs=[cc_out.ap().opt()]),
                reads=[RD(cc_in)], writes=[RD(cc_out)], kind="cc", tag="cc")
            o.token = ("cc_sem", 1)
        bld.free(krt)
        bld.free(vht)
        bld.free(kv_sb)
        bld.free(ccs)
        bld.free(krope)
        bld.free(ksum)

    if "PH7" in phases:
        lepe = bld.alloc("lepe", [P, NG * TP], BF16)
        def lp_emit(g, c0, w, t_ap, deps):
            dve("tensor_copy", (lepe[:, g * TP + c0:g * TP + c0 + w], t_ap),
                deps, [R(lepe, g * TP + c0, g * TP + c0 + w)], tag="lpcp")
        conv_pass(xi, lp_emit, "lepe_w", "lepe_b", SUBS, cvt, tag="cv3")
        bld.free(xi)
        bld.free(cvt)

    if "PH8" in phases:
        actp_wb = bld.alloc("actp_wb", [P, NG * C], BF16)
        wload(env["w_actp"], actp_wb, "w_actp")
        ar_h = bld.alloc("ar_h", [P, NG * TP], BF16)
        def xfn8(k, c0, w):
            return (xn1[:, k * TP + c0:k * TP + c0 + w],
                    R(xn1, k * TP + c0, k * TP + c0 + w))
        def ev8(oc, c0, w, bank, deps):
            act(ar_h[:, oc * TP + c0:oc * TP + c0 + w], psum[bank][:, 0:w], AF.Silu,
                deps + [smres("actp_b")], [R(ar_h, oc * TP + c0, oc * TP + c0 + w)],
                bias=smcol("actp_b", oc), tag="ev8")
        proj(actp_wb, C, xn1, xfn8, SUBS, ev8, tag="actp")
        bld.free(actp_wb)
        bld.free(xn1)

    if "PH9" in phases:
        dma(st_io, q_sp.ap(), q_h[:], [R(q_h)], [RD(q_sp)], "qsp")
        dma(st_io, ar_sp.ap(), ar_h[:], [R(ar_h)], [RD(ar_sp)], "arsp")
        dma(st_io, lp_sp.ap(), lepe[:], [R(lepe)], [RD(lp_sp)], "lpsp")
        bld.free(q_h)
        bld.free(ar_h)
        bld.free(lepe)

    if "F2" in phases:
        h_full = bld.alloc("h_full", [P, NGH * T], BF16)
        dma(st_h, h_full[:], env["h0_sp"].ap(), [RD(env["h0_sp"])], [R(h_full)], "h0rl")

    def h_pass(w_dram, s_name, dst_sp, tag):
        slr = [bld.alloc(f"hsl{i}{tag}", [P, NGH * P], BF16) for i in range(4)]
        ev = bld.alloc(f"hev{tag}", [P, 2 * 512], BF16)
        banks4 = ("pmm0", "pmm1", "pmm2", "pst")
        for oc in range(NGH):
            slb = slr[oc % 4]
            dma(st_w, slb[:], w_dram.ap()[oc], [RD(w_dram)], [R(slb)], f"{tag}s{oc}")
            for k in range(NGH):
                for si, (c0, w) in enumerate(RSUBS):
                    bank = banks4[si]
                    bld.op("pe", lambda e, o=psum[bank][:, 0:w],
                           l=slb[:, k * P:(k + 1) * P],
                           r=h_full[:, k * T + (c0 - R0):k * T + (c0 - R0) + w],
                           st=(k == 0), sp=(k == NGH - 1): e.matmul(o, l, r, start=st, stop=sp),
                           reads=[R(slb, k * P, (k + 1) * P),
                                  R(h_full, k * T + (c0 - R0), k * T + (c0 - R0) + w)] +
                                 ([] if k == 0 else [RP(bank)]),
                           writes=[RP(bank)], tag=f"{tag}mm")
            for si, (c0, w) in enumerate(RSUBS):
                bank = banks4[si]
                epar = si % 2
                act(ev[:, epar * 512:epar * 512 + w], psum[bank][:, 0:w], AF.Gelu,
                    [RP(bank), smres(s_name)], [R(ev, epar * 512, epar * 512 + w)],
                    scale=smcol(s_name, oc), tag=f"{tag}g")
                dma(st_h, dst_sp.ap()[:, oc * T + (c0 - R0):oc * T + (c0 - R0) + w],
                    ev[:, epar * 512:epar * 512 + w],
                    [R(ev, epar * 512, epar * 512 + w)], [RD(dst_sp)], f"{tag}sp")
        for _h in slr:
            bld.free(_h)
        bld.free(ev)

    if "F2" in phases:
        h_pass(env["w_h1"], "h1_s", h1_sp, "f2")
        dma(st_h, h_full[:], h1_sp.ap(), [RD(h1_sp)], [R(h_full)], "h1rl")
    if "F3" in phases:
        h_pass(env["w_h2"], "h2_s", h2_sp, "f3")
        dma(st_h, h_full[:], h2_sp.ap(), [RD(h2_sp)], [R(h_full)], "h2rl")

    if "F4" in phases:
        w3r = [bld.alloc(f"w3sl{i}", [P, NGH * P], BF16) for i in range(4)]
        uo_sub = bld.alloc("uo_sub", [P, 2 * 512], F32)
        fo_sub = bld.alloc("fo_sub", [P, 2 * 512], F32)
        w_w3 = env["w_w3"]
        banks4f = ("pmm0", "pmm1", "pmm2", "pst")
        for oc in range(NG):
            slb = w3r[oc % 4]
            dma(st_w, slb[:], w_w3.ap()[oc], [RD(w_w3)], [R(slb)], f"w3s{oc}")
            for k in range(NGH):
                for si, (c0, w) in enumerate(RSUBS):
                    bank = banks4f[si]
                    bld.op("pe", lambda e, o=psum[bank][:, 0:w],
                           l=slb[:, k * P:(k + 1) * P],
                           r=h_full[:, k * T + (c0 - R0):k * T + (c0 - R0) + w],
                           st=(k == 0), sp=(k == NGH - 1): e.matmul(o, l, r, start=st, stop=sp),
                           reads=[R(slb, k * P, (k + 1) * P),
                                  R(h_full, k * T + (c0 - R0), k * T + (c0 - R0) + w)] +
                                 ([] if k == 0 else [RP(bank)]),
                           writes=[RP(bank)], tag="w3mm")
            for si, (c0, w) in enumerate(RSUBS):
                bank = banks4f[si]
                po = (si % 2) * 512
                dma(st_io, uo_sub[:, po:po + w],
                    u_t.ap()[:, oc * TP + c0:oc * TP + c0 + w],
                    [RD(u_t)], [R(uo_sub, po, po + w)], "u4")
                act(fo_sub[:, po:po + w], psum[bank][:, 0:w], AF.Identity,
                    [RP(bank), smres("w3_s")], [R(fo_sub, po, po + w)],
                    scale=smcol("w3_s", oc), tag="f4ev")
                dve("tensor_tensor", (fo_sub[:, po:po + w], fo_sub[:, po:po + w],
                                      uo_sub[:, po:po + w], ALU.add),
                    [R(fo_sub, po, po + w), R(uo_sub, po, po + w)],
                    [R(fo_sub, po, po + w)], tag="f4add")
                dma(st_io, out_t.ap()[:, oc * T + (c0 - R0):oc * T + (c0 - R0) + w],
                    fo_sub[:, po:po + w], [R(fo_sub, po, po + w)], [RD(out_t)], "f4st")
        for _h in w3r:
            bld.free(_h)
        bld.free(h_full)
        bld.free(uo_sub)
        bld.free(fo_sub)

    if "B0" in phases:
        ccr = bld.alloc("ccr", [P, 2 * CCN], F32)
        kvb = bld.alloc("kvb", [P, NG * P], BF16)
        kml = bld.alloc("kml", [P, NG * NH], BF16)
        kvf = bld.alloc("kvf", [P, NG * P], F32)
        ksg = bld.alloc("ksg", [P, NG], F32)
        dma(st_cc, ccr[:].rearrange("p (b m) -> p b m", b=2),
            cc_out.ap().rearrange("b (p m) -> p b m", p=P),
            [RD(cc_out)], [R(ccr)], "ccrd")
        dve("tensor_scalar", (kvf[:], ccr[:, 0:NG * P], smcol("bsel", 0), None, ALU.mult),
            [R(ccr, 0, NG * P), smres("bsel")], [R(kvf)], tag="b0a")
        dve("scalar_tensor_tensor", (kvf[:], ccr[:, CCN:CCN + NG * P], smcol("bsel", 1),
                                     kvf[:], ALU.mult, ALU.add),
            [R(ccr, CCN, CCN + NG * P), R(kvf), smres("bsel")], [R(kvf)], tag="b0b")
        dve("tensor_scalar", (ksg[:], ccr[:, NG * P:CCN], smcol("bsel", 0), None, ALU.mult),
            [R(ccr, NG * P, CCN), smres("bsel")], [R(ksg)], tag="b0c")
        dve("scalar_tensor_tensor", (ksg[:], ccr[:, CCN + NG * P:2 * CCN], smcol("bsel", 1),
                                     ksg[:], ALU.mult, ALU.add),
            [R(ccr, CCN + NG * P, 2 * CCN), R(ksg), smres("bsel")], [R(ksg)], tag="b0d")
        dve("memset", (kvb[:], 0.0), [], [R(kvb)], tag="b0z")
        for g in range(NG):
            dve("tensor_copy", (kvb[0:64, g * P:g * P + 64], kvf[0:64, g * P:g * P + 64]),
                [R(kvf, g * P, g * P + 64, 0, 64)],
                [R(kvb, g * P, g * P + 64, 0, 64)], tag="b0d1")
            dve("tensor_copy", (kvb[64:128, g * P + 64:(g + 1) * P],
                                kvf[64:128, g * P + 64:(g + 1) * P]),
                [R(kvf, g * P + 64, (g + 1) * P, 64, 128)],
                [R(kvb, g * P + 64, (g + 1) * P, 64, 128)], tag="b0d2")
        for g in range(NG):
            dve("tensor_scalar", (kml[:, g * NH:(g + 1) * NH],
                                  cst["zmask"][:, g * NH:(g + 1) * NH],
                                  ksg[:, g:g + 1], None, ALU.mult),
                [R(cst["zmask"], g * NH, (g + 1) * NH), R(ksg, g, g + 1)],
                [R(kml, g * NH, (g + 1) * NH)], tag="b0k")
        bld.free(ccr)
        bld.free(kvf)

        outw_b = bld.alloc("outw_b", [P, NG * C], BF16)
        wload(env["w_out"], outw_b, "w_out")

    if "B1" in phases:
        y_h = bld.alloc("y_h", [P, NG * TP], F32)
        qs2 = [bld.alloc(f"qs_sub{i}", [P, NG * 512], BF16) for i in range(2)]
        ars2 = [bld.alloc(f"ars_sub{i}", [P, NG * 512], BF16) for i in range(2)]
        lps2 = [bld.alloc(f"lps_sub{i}", [P, NG * 512], BF16) for i in range(2)]
        as_sub = bld.alloc("as_sub", [P, NG * 512], F32)
        qr2 = [bld.alloc(f"qr_sub{i}", [P, NG * 512], BF16) for i in range(2)]
        at2 = [bld.alloc(f"at_sub{i}", [P, NG * 512], BF16) for i in range(2)]
        zsb = bld.alloc("zsb", [NH, 512], F32)
        zbb = bld.alloc("zbb", [P, 2 * 512], F32)
        t_f = bld.alloc("t_f", [P, 2 * 512], F32)

        for sidx, (c0, w) in enumerate(SUBS):
            qs_sub = qs2[sidx % 2]
            ars_sub = ars2[sidx % 2]
            lps_sub = lps2[sidx % 2]
            qr_sub = qr2[sidx % 2]
            at_sub = at2[sidx % 2]
            dma(st_io, qs_sub[:].rearrange("p (g m) -> p g m", g=NG)[:, :, 0:w],
                q_sp.ap().rearrange("p (g m) -> p g m", g=NG)[:, :, c0:c0 + w],
                [RD(q_sp)], [R(qs_sub)], "qld")
            dma(st_io, ars_sub[:].rearrange("p (g m) -> p g m", g=NG)[:, :, 0:w],
                ar_sp.ap().rearrange("p (g m) -> p g m", g=NG)[:, :, c0:c0 + w],
                [RD(ar_sp)], [R(ars_sub)], "arld")
            dma(st_io, lps_sub[:].rearrange("p (g m) -> p g m", g=NG)[:, :, 0:w],
                lp_sp.ap().rearrange("p (g m) -> p g m", g=NG)[:, :, c0:c0 + w],
                [RD(lp_sp)], [R(lps_sub)], "lpld")
            dma(st_io, as_sub[:].rearrange("p (g m) -> p g m", g=NG)[:, :, 0:w],
                a_sp.ap().rearrange("p (g m) -> p g m", g=NG)[:, :, c0:c0 + w],
                [RD(a_sp)], [R(as_sub)], "ald")
            for g in range(NG):
                bank = ("pmm0", "pmm1")[g % 2]
                bld.op("pe", lambda e, o=psum[bank][:, 0:w],
                       l=cst["rope"][:, g * P:(g + 1) * P],
                       r=qs_sub[:, g * 512:g * 512 + w]:
                       e.matmul(o, l, r, start=True, stop=True),
                       reads=[R(cst["rope"], g * P, (g + 1) * P),
                              R(qs_sub, g * 512, g * 512 + w)],
                       writes=[RP(bank)], tag="qrmm")
                dve("tensor_copy", (qr_sub[:, g * 512:g * 512 + w], psum[bank][:, 0:w]),
                    [RP(bank)], [R(qr_sub, g * 512, g * 512 + w)], tag="qrev")
            for g in range(NG):
                bld.op("pe", lambda e, o=psum["pst"][0:NH, 0:w],
                       l=kml[:, g * NH:(g + 1) * NH], r=qs_sub[:, g * 512:g * 512 + w],
                       st=(g == 0), sp=(g == NG - 1): e.matmul(o, l, r, start=st, stop=sp),
                       reads=[R(kml, g * NH, (g + 1) * NH),
                              R(qs_sub, g * 512, g * 512 + w)] +
                             ([] if g == 0 else [RP("pst")]),
                       writes=[RP("pst")], tag="zmm")
            dve("tensor_scalar", (zsb[:, 0:w], psum["pst"][0:NH, 0:w], 1e-6, None, ALU.add),
                [RP("pst")], [R(zsb, 0, w)], tag="zadd")
            dve("reciprocal", (zsb[:, 0:w], zsb[:, 0:w]),
                [R(zsb, 0, w)], [R(zsb, 0, w)], tag="zrec")
            for g in range(NG):
                zb_bank = ("pkv0", "pkv1")[g % 2]
                at_bank = ("pmm2", "pst")[g % 2]
                bld.op("pe", lambda e, o=psum[zb_bank][:, 0:w],
                       l=cst["zsel"][:, g * P:(g + 1) * P], r=zsb[0:NH, 0:w]:
                       e.matmul(o, l, r, start=True, stop=True),
                       reads=[R(cst["zsel"], g * P, (g + 1) * P), R(zsb, 0, w)],
                       writes=[RP(zb_bank)], tag="zbmm")
                zo = (g % 2) * 512
                dve("tensor_copy", (zbb[:, zo:zo + w], psum[zb_bank][:, 0:w]),
                    [RP(zb_bank)], [R(zbb, zo, zo + w)], tag="zbev")
                bld.op("pe", lambda e, o=psum[at_bank][:, 0:w],
                       l=kvb[:, g * P:(g + 1) * P], r=qr_sub[:, g * 512:g * 512 + w]:
                       e.matmul(o, l, r, start=True, stop=True),
                       reads=[R(kvb, g * P, (g + 1) * P),
                              R(qr_sub, g * 512, g * 512 + w)],
                       writes=[RP(at_bank)], tag="atmm")
                dve("tensor_tensor", (t_f[:, zo:zo + w], psum[at_bank][:, 0:w],
                                      zbb[:, zo:zo + w], ALU.mult),
                    [RP(at_bank), R(zbb, zo, zo + w)], [R(t_f, zo, zo + w)], tag="atz")
                dve("tensor_tensor", (t_f[:, zo:zo + w], t_f[:, zo:zo + w],
                                      lps_sub[:, g * 512:g * 512 + w], ALU.add),
                    [R(t_f, zo, zo + w), R(lps_sub, g * 512, g * 512 + w)],
                    [R(t_f, zo, zo + w)], tag="atl")
                dve("tensor_tensor", (at_sub[:, g * 512:g * 512 + w], t_f[:, zo:zo + w],
                                      ars_sub[:, g * 512:g * 512 + w], ALU.mult),
                    [R(t_f, zo, zo + w), R(ars_sub, g * 512, g * 512 + w)],
                    [R(at_sub, g * 512, g * 512 + w)], tag="atar")
            for oc in range(NG):
                bank = ("pmm0", "pmm1")[oc % 2]
                for k in range(NG):
                    bld.op("pe", lambda e, o=psum[bank][:, 0:w],
                           l=outw_b[:, k * C + oc * P:k * C + (oc + 1) * P],
                           r=at_sub[:, k * 512:k * 512 + w],
                           st=(k == 0), sp=(k == NG - 1): e.matmul(o, l, r, start=st, stop=sp),
                           reads=[R(outw_b, k * C + oc * P, k * C + (oc + 1) * P),
                                  R(at_sub, k * 512, k * 512 + w)] +
                                 ([] if k == 0 else [RP(bank)]),
                           writes=[RP(bank)], tag="ywmm")
                yo = (oc % 2) * 512
                act(t_f[:, yo:yo + w], psum[bank][:, 0:w], AF.Identity,
                    [RP(bank), smres("out_b")], [R(t_f, yo, yo + w)],
                    bias=smcol("out_b", oc), tag="yev")
                dve("tensor_tensor", (y_h[:, oc * TP + c0:oc * TP + c0 + w],
                                      t_f[:, yo:yo + w],
                                      as_sub[:, oc * 512:oc * 512 + w], ALU.add),
                    [R(t_f, yo, yo + w), R(as_sub, oc * 512, oc * 512 + w)],
                    [R(y_h, oc * TP + c0, oc * TP + c0 + w)], tag="ya")
        for _h in qs2 + ars2 + lps2 + qr2 + at2 + [as_sub, outw_b, kvb, kml, ksg,
                                                    zsb, zbb, t_f]:
            bld.free(_h)

    if "B2" in phases:
        sq2[0] = sq2[1] = bld.alloc("sqs0b", [P, NG * 512], F32)
        rows2[0] = rows2[1] = bld.alloc("rows0b", [1, 4 * 512], F32)
        rbmb2[0] = bld.alloc("rbmb0b", [P, 2 * 512], F32)
        rbmb2[1] = bld.alloc("rbmb1b", [P, 2 * 512], F32)
        cvt = bld.alloc("cvtb", [P, 2 * 512], F32)
        mask_halo(y_h, "y")
        xn2b = bld.alloc("xn2b", [P, NG * T], BF16, top=True)
        x2_sub = bld.alloc("x2_sub", [P, NG * 512], F32)
        def x2_emit(g, c0, w, t_ap, deps):
            o = x2_sub[:, g * 512:g * 512 + w]
            dve("tensor_tensor", (o, t_ap, y_h[:, g * TP + c0:g * TP + c0 + w], ALU.add),
                deps + [R(y_h, g * TP + c0, g * TP + c0 + w)],
                [R(x2_sub, g * 512, g * 512 + w)], tag="x2")
        for (c0, w) in RSUBS:
            conv_pass(y_h, x2_emit, "cpe2_w", "cpe2_b", [(c0, w)], cvt, tag="cv4")
            dma(st_io, x2_sp.ap().rearrange("p (g m) -> p g m", g=NG)[:, :, c0 - R0:c0 - R0 + w],
                x2_sub[:].rearrange("p (g m) -> p g m", g=NG)[:, :, 0:w],
                [R(x2_sub)], [RD(x2_sp)], "x2sp")
            def srcb(g, cc0, ww):
                return (x2_sub[:, g * 512:g * 512 + ww], R(x2_sub, g * 512, g * 512 + ww))
            def wrb(g, cc0, ww, rb, mb, deps, _c0=c0):
                t = cvt[:, 0:ww]
                tres = R(cvt, 0, ww)
                dve("tensor_tensor", (t, x2_sub[:, g * 512:g * 512 + ww], rb, ALU.mult),
                    [R(x2_sub, g * 512, g * 512 + ww)] + deps, [tres], tag="ln2a")
                dve("tensor_tensor", (t, t, mb, ALU.add), [tres] + deps, [tres], tag="ln2b")
                o = xn2b[:, g * T + _c0 - R0:g * T + _c0 - R0 + ww]
                dve("tensor_scalar", (o, t, smcol("g2", g), smcol("b2", g), ALU.mult, ALU.add),
                    [tres, smres("g2"), smres("b2")],
                    [R(xn2b, g * T + _c0 - R0, g * T + _c0 - R0 + ww)], tag="ln2c")
            ln_pass(x2_sub, srcb, [(c0, w)], wrb, tag="ln2")
        bld.free(y_h)
        bld.free(x2_sub)
        bld.free(cvt)
        for _h in {h.name: h for h in sq2 + rows2 + rbmb2}.values():
            bld.free(_h)

    if "B3" in phases:
        g_full = bld.alloc("g_full", [P, NGH * T], BF16)
        f1r = [bld.alloc(f"f1sl{i}", [P, NG * P], BF16) for i in range(4)]
        w_fc1 = env["w_fc1"]
        banks4b = ("pmm0", "pmm1", "pmm2", "pst")
        for oc in range(NGH):
            slb = f1r[oc % 4]
            dma(st_w, slb[:], w_fc1.ap()[oc], [RD(w_fc1)], [R(slb)], f"fc1s{oc}")
            for k in range(NG):
                for si, (c0, w) in enumerate(RSUBS):
                    bank = banks4b[si]
                    bld.op("pe", lambda e, o=psum[bank][:, 0:w],
                           l=slb[:, k * P:(k + 1) * P],
                           r=xn2b[:, k * T + (c0 - R0):k * T + (c0 - R0) + w],
                           st=(k == 0), sp=(k == NG - 1): e.matmul(o, l, r, start=st, stop=sp),
                           reads=[R(slb, k * P, (k + 1) * P),
                                  R(xn2b, k * T + (c0 - R0), k * T + (c0 - R0) + w)] +
                                 ([] if k == 0 else [RP(bank)]),
                           writes=[RP(bank)], tag="fc1mm")
            for si, (c0, w) in enumerate(RSUBS):
                bank = banks4b[si]
                act(g_full[:, oc * T + (c0 - R0):oc * T + (c0 - R0) + w],
                    psum[bank][:, 0:w], AF.Gelu,
                    [RP(bank), smres("fc1_b")],
                    [R(g_full, oc * T + (c0 - R0), oc * T + (c0 - R0) + w)],
                    bias=smcol("fc1_b", oc), tag="fc1g")
        for _h in f1r:
            bld.free(_h)
        bld.free(xn2b)

    if "B4" in phases:
        f2r = [bld.alloc(f"f2sl{i}", [P, NGH * P], BF16) for i in range(4)]
        x2r = bld.alloc("x2r", [P, 2 * 512], F32)
        po_r = bld.alloc("po_r", [P, 2 * 512], F32)
        o_sub = bld.alloc("o_sub", [P, 2 * 512], F32)
        w_fc2 = env["w_fc2"]
        banks4c = ("pmm0", "pmm1", "pmm2", "pst")
        for oc in range(NG):
            slb = f2r[oc % 4]
            dma(st_w, slb[:], w_fc2.ap()[oc], [RD(w_fc2)], [R(slb)], f"fc2s{oc}")
            for k in range(NGH):
                for si, (c0, w) in enumerate(RSUBS):
                    bank = banks4c[si]
                    bld.op("pe", lambda e, o=psum[bank][:, 0:w],
                           l=slb[:, k * P:(k + 1) * P],
                           r=g_full[:, k * T + (c0 - R0):k * T + (c0 - R0) + w],
                           st=(k == 0), sp=(k == NGH - 1): e.matmul(o, l, r, start=st, stop=sp),
                           reads=[R(slb, k * P, (k + 1) * P),
                                  R(g_full, k * T + (c0 - R0), k * T + (c0 - R0) + w)] +
                                 ([] if k == 0 else [RP(bank)]),
                           writes=[RP(bank)], tag="fc2mm")
            for si, (c0, w) in enumerate(RSUBS):
                bank = banks4c[si]
                rc = oc * T + (c0 - R0)
                po = ((oc * 4 + si) % 2) * 512
                dma(st_io, x2r[:, po:po + w], x2_sp.ap()[:, rc:rc + w],
                    [RD(x2_sp)], [R(x2r, po, po + w)], "x2rl")
                dma(st_io, po_r[:, po:po + w], out_t.ap()[:, rc:rc + w],
                    [RD(out_t)], [R(po_r, po, po + w)], "porl")
                act(o_sub[:, po:po + w], psum[bank][:, 0:w], AF.Identity,
                    [RP(bank), smres("fc2_b")], [R(o_sub, po, po + w)],
                    bias=smcol("fc2_b", oc), tag="f2ev")
                dve("tensor_tensor", (o_sub[:, po:po + w], o_sub[:, po:po + w],
                                      x2r[:, po:po + w], ALU.add),
                    [R(o_sub, po, po + w), R(x2r, po, po + w)], [R(o_sub, po, po + w)],
                    tag="f2a1")
                dve("tensor_tensor", (o_sub[:, po:po + w], o_sub[:, po:po + w],
                                      po_r[:, po:po + w], ALU.add),
                    [R(o_sub, po, po + w), R(po_r, po, po + w)], [R(o_sub, po, po + w)],
                    tag="f2a2")
                dma(st_io, out_t.ap()[:, rc:rc + w], o_sub[:, po:po + w],
                    [R(o_sub, po, po + w)], [RD(out_t)], "ost")



def _tile_rows(a, ng):
    f = a.shape[1]
    return np.ascontiguousarray(a.reshape(ng, P, f).transpose(1, 0, 2).reshape(P, ng * f))


def _oc_slices(w_oi, n_oc, n_k):
    O, I = w_oi.shape
    assert O == n_oc * P and I == n_k * P
    a = w_oi.reshape(n_oc, P, n_k, P).transpose(0, 3, 2, 1)
    return np.ascontiguousarray(a.reshape(n_oc, P, n_k * P))


def _oc_slices_kf(w_kf, n_oc, n_k):
    K, F = w_kf.shape
    assert K == n_k * P and F == n_oc * P
    a = w_kf.reshape(n_k, P, n_oc, P).transpose(2, 1, 0, 3)
    return np.ascontiguousarray(a.reshape(n_oc, P, n_k * P))


def make_smalls(params, core):
    chunk = core % 4
    b = core // 4
    sm = np.zeros((P, SMALLS_COLS), np.float32)

    def put(name, arr):
        off, n = SMALLS[name]
        sm[:, off:off + n] = arr

    def chunks_vec(v):
        return np.ascontiguousarray(v.reshape(NG, P).T)

    put("g1", chunks_vec(params["norm1_g"]))
    put("b1", chunks_vec(params["norm1_b"]))
    put("g2", chunks_vec(params["norm2_g"]))
    put("b2", chunks_vec(params["norm2_b"]))
    for cv, wname, bname in [("cpe1", "cpe1_w", "cpe1_b"), ("dwc", "dwc_w", "dwc_b"),
                             ("lepe", "lepe_w", "lepe_b"), ("cpe2", "cpe2_w", "cpe2_b")]:
        w3 = params[wname].reshape(C, 3).reshape(NG, P, 3).transpose(1, 0, 2).reshape(P, NG * 3)
        put(cv + "_w", w3)
        put(cv + "_b", chunks_vec(params[bname]))
    put("in_b", chunks_vec(params["in_b"]))
    put("actp_b", chunks_vec(params["actp_b"]))
    put("qk_b", np.ascontiguousarray(params["qk_b"].reshape(2 * NG, P).T))
    put("out_b", chunks_vec(params["out_b"]))
    put("fc1_b", np.ascontiguousarray(params["fc1_b"].reshape(NGH, P).T))
    put("fc2_b", chunks_vec(params["fc2_b"]))
    put("w1_s", np.ascontiguousarray(params["w1_s"].reshape(NGH, P).T))
    put("w2_s", np.ascontiguousarray(params["w2_s"].reshape(NGH, P).T))
    put("h1_s", np.ascontiguousarray(params["h1_s"].reshape(NGH, P).T))
    put("h2_s", np.ascontiguousarray(params["h2_s"].reshape(NGH, P).T))
    put("w3_s", chunks_vec(params["w3_s"]))
    put("lmask", np.full((P, HALO), 0.0 if chunk == 0 else 1.0, np.float32))
    put("rmask", np.full((P, HALO), 0.0 if chunk == 3 else 1.0, np.float32))
    bs = np.zeros((P, 2), np.float32)
    bs[:, b] = 1.0
    put("bsel", bs)
    return sm


def make_in_maps(inputs):
    x = inputs["L0_attn_x"]
    pr = {k[len("L0_attn_"):]: np.asarray(v) for k, v in inputs.items()
          if k.startswith("L0_attn_")}
    fr = {k[len("L0_ffn_"):]: np.asarray(v) for k, v in inputs.items()
          if k.startswith("L0_ffn_")}
    pr.update({k: v for k, v in fr.items()})
    params = pr

    w_shared = dict(
        w_in=_tile_rows(params["in_w"], NG),
        w_actp=_tile_rows(params["actp_w"], NG),
        w_qk=_tile_rows(params["qk_w"], NG),
        w_out=_tile_rows(params["out_w"], NG),
        w_fc1=_oc_slices_kf(params["fc1_w"], NGH, NG),
        w_fc2=_oc_slices_kf(params["fc2_w"], NG, NGH),
        w_w1=_oc_slices(params["w1_w"], NGH, NG),
        w_w2=_oc_slices(params["w2_w"], NGH, NG),
        w_h1=_oc_slices(params["h1_w"], NGH, NGH),
        w_h2=_oc_slices(params["h2_w"], NGH, NGH),
        w_w3=_oc_slices(params["w3_w"], NG, NGH),
    )

    in_maps = []
    for core in range(8):
        b, chunk = core // 4, core % 4
        t0 = chunk * T
        lo, hi = t0 - HALO, t0 + T + HALO
        seg = np.zeros((TP, C), np.float32)
        s0, s1 = max(lo, 0), min(hi, 8192)
        seg[s0 - lo:s1 - lo, :] = x[b, s0:s1, :]
        u_fm = np.ascontiguousarray(seg.T)
        u_t = _tile_rows(u_fm, NG)
        m = dict(u_t=u_t, smalls=make_smalls(params, core), **w_shared)
        in_maps.append(m)
    return in_maps


_CACHE = {}


def kernel(**inputs):
    key = "full"
    if key not in _CACHE:
        _CACHE[key] = build_program()
    nc, meta = _CACHE[key]
    in_maps = make_in_maps(inputs)
    res = run_bass_kernel_spmd(nc, in_maps, core_ids=list(range(8)))
    out = np.empty((2, 8192, 1024), np.float32)
    for core in range(8):
        b, chunk = core // 4, core % 4
        o = res.results[core]["out_t"]
        o = o.reshape(P, NG, T).transpose(1, 0, 2).reshape(C, T)
        out[b, chunk * T:(chunk + 1) * T, :] = o.T
    return out


# version 10
# speedup vs baseline: 1.1572x; 1.1318x over previous
import numpy as np
import ml_dtypes
from contextlib import ExitStack

import concourse.bass as bass
import concourse.mybir as mybir
from concourse.bass_utils import run_bass_kernel_spmd

F32 = mybir.dt.float32
BF16 = mybir.dt.bfloat16
AF = mybir.ActivationFunctionType
ALU = mybir.AluOpType
AX = mybir.AxisListType

P = 128
C = 1024
NG = 8
NH = 16
HD = 64
T = 2048
HALO = 4
TP = T + 2 * HALO
R0, R1 = HALO, HALO + T
HID = 4096
NGH = 32

SUBS = [(0, 512), (512, 512), (1024, 512), (1536, 512), (2048, 8)]
RSUBS = [(4, 512), (516, 512), (1028, 512), (1540, 512)]

BF = ml_dtypes.bfloat16


def _rope_lhst():
    k_max = C // 2
    theta = 1.0 / 10000.0 ** (np.arange(k_max, dtype=np.float64) / k_max)
    cs, sn = np.cos(theta), np.sin(theta)
    out = np.zeros((NG, P, P), np.float32)
    for g in range(NG):
        for jl in range(64):
            j = 64 * g + jl
            out[g, 2 * jl, 2 * jl] = cs[j]
            out[g, 2 * jl + 1, 2 * jl] = -sn[j]
            out[g, 2 * jl, 2 * jl + 1] = sn[j]
            out[g, 2 * jl + 1, 2 * jl + 1] = cs[j]
    return out.astype(BF)


def _zmask():
    out = np.zeros((NG, P, NH), np.float32)
    for g in range(NG):
        for p in range(P):
            out[g, p, (128 * g + p) // HD] = 1.0 / 8192.0
    return out.astype(BF)


def _zsel():
    out = np.zeros((NG, NH, P), np.float32)
    for g in range(NG):
        for p in range(P):
            out[g, (128 * g + p) // HD, p] = 1.0
    return out



ENGS = ("pe", "act", "dve", "pool", "sp")


class Op:
    __slots__ = ("idx", "eng", "kind", "emit", "reads", "writes", "token",
                 "need_inc", "deps", "tag")

    def __init__(self, idx, eng, kind, emit, reads, writes, tag=""):
        self.idx = idx
        self.eng = eng
        self.kind = kind
        self.emit = emit
        self.reads = reads
        self.writes = writes
        self.token = None
        self.need_inc = False
        self.deps = []
        self.tag = tag


class Stream:

    def __init__(self, bld, name, eng, n_sems=4):
        self.bld = bld
        self.name = name
        self.eng = eng
        self.n_sems = n_sems
        self.sems = [f"{name}{i}" for i in range(n_sems)]
        self.counts = [0] * n_sems
        self.wave_i = 0
        self.prev_wave_final = {}

    def wave(self, dmas):
        si = self.wave_i % self.n_sems
        sem = self.sems[si]
        if self.wave_i >= self.n_sems:
            old = self.prev_wave_final[si]
            self.bld.raw_wait(self.eng, sem, old)
        for emit, reads, writes in dmas:
            self.counts[si] += 16
            val = self.counts[si]
            op = self.bld.op(self.eng, emit, reads, writes, kind="d",
                             tag=f"{self.name}w{self.wave_i}")
            op.token = (sem, val)
        self.prev_wave_final[si] = self.counts[si]
        tok = (sem, self.counts[si])
        self.wave_i += 1
        return tok


class Builder:
    def __init__(self, nc, sim_sync=False, hw_same_engine_sync=True):
        self.nc = nc
        self.sim_sync = sim_sync
        self.hw_same_engine_sync = hw_same_engine_sync
        self.ops = []
        self.writers = []
        self.readers = []
        self.sem_names = set()
        self.streams = {}
        self.raw_waits = {}
        self.ar_base = 16416
        self.ar_top = 229344
        self.free_list = [(self.ar_base, self.ar_top)]
        self.tensors = {}
        self.sb_handles = {}

    def alloc(self, name, shape, dtype, align=64, top=None):
        size = int(np.prod(shape[1:])) * mybir.dt.size(dtype)
        size_al = -(-size // align) * align
        from_top = (size_al < 20480) if top is None else top
        blocks = list(enumerate(self.free_list))
        if from_top:
            blocks = reversed(blocks)
        for i, (a, b) in blocks:
            a_al = -(-a // align) * align
            if b - a_al >= size_al:
                if from_top:
                    addr = (b - size_al) // align * align
                    if addr < a:
                        continue
                else:
                    addr = a_al
                pieces = []
                if a < addr:
                    pieces.append((a, addr))
                if addr + size_al < b:
                    pieces.append((addr + size_al, b))
                self.free_list[i:i + 1] = pieces
                h = self.nc.alloc_sbuf_tensor_at(name, list(shape), dtype, offset=addr)
                self.tensors[h.name] = (h, addr, size_al)
                return h
        live = sorted((a, a + s, n) for n, (_, a, s) in self.tensors.items())
        raise MemoryError(f"SBUF arena full allocating {name} ({size_al}B/part); "
                          f"free={self.free_list}; live={live}")

    def free(self, h):
        name = h.name
        _, addr, size = self.tensors.pop(name)
        self.free_list.append((addr, addr + size))
        self.free_list.sort()
        merged = []
        for a, b in self.free_list:
            if merged and merged[-1][1] >= a:
                merged[-1] = (merged[-1][0], max(merged[-1][1], b))
            else:
                merged.append((a, b))
        self.free_list = [tuple(x) for x in merged]

    def rs(self, h, c0=None, c1=None, p0=0, p1=None):
        name = h.name if hasattr(h, "name") else h
        hh, addr, _ = self.tensors[name]
        sz = mybir.dt.size(hh.dtype)
        ncols = hh.shape[1]
        if c0 is None:
            c0, c1 = 0, ncols
        if p1 is None:
            p1 = hh.shape[0]
        return ("S", p0, p1, addr + c0 * sz, addr + c1 * sz)

    @staticmethod
    def rp(bank_name):
        return ("P", bank_name)

    @staticmethod
    def rd(dram_handle):
        return ("D", dram_handle.name if hasattr(dram_handle, "name") else dram_handle)

    @staticmethod
    def _overlap(r1, r2):
        if r1[0] != r2[0]:
            return False
        if r1[0] == "S":
            return not (r1[2] <= r2[1] or r2[2] <= r1[1] or
                        r1[4] <= r2[3] or r2[4] <= r1[3])
        return r1[1] == r2[1]

    def op(self, eng, emit, reads=(), writes=(), kind="c", tag=""):
        o = Op(len(self.ops), eng, kind, emit, list(reads), list(writes), tag)
        deps = {}
        for r in o.reads:
            for res, w in self.writers:
                if self._overlap(res, r):
                    deps[w.idx] = w
        for w_ in o.writes:
            for res, w in self.writers:
                if self._overlap(res, w_):
                    deps[w.idx] = w
            for res, rdr in self.readers:
                if self._overlap(res, w_):
                    deps[rdr.idx] = rdr
        for d in deps.values():
            if d is o:
                continue
            if d.eng == o.eng and d.kind == "c" and o.kind == "c":
                if not self.sim_sync:
                    if o.eng == "pe" or not self.hw_same_engine_sync:
                        continue
            o.deps.append(d)
        for w_ in o.writes:
            self.writers = [(res, w) for (res, w) in self.writers
                            if not self._covered(res, w_)]
            self.readers = [(res, r) for (res, r) in self.readers
                            if not self._covered(res, w_)]
            self.writers.append((w_, o))
        for r in o.reads:
            self.readers.append((r, o))
        self.ops.append(o)
        return o

    @staticmethod
    def _covered(res, w):
        if res[0] != w[0]:
            return False
        if res[0] == "S":
            return (w[1] <= res[1] and res[2] <= w[2] and
                    w[3] <= res[3] and res[4] <= w[4])
        return res[1] == w[1]

    def raw_wait(self, eng, sem, val):
        o = Op(len(self.ops), eng, "w", None, [], [], "gate")
        o.token = (sem, val)
        self.ops.append(o)

    def stream(self, name, eng, n_sems=4):
        s = Stream(self, name, eng, n_sems)
        self.streams[name] = s
        return s

    def finalize(self, eng_objs, final_eng="sp"):
        nc = self.nc
        for o in self.ops:
            for d in o.deps:
                if d.token is None:
                    d.need_inc = True
        eng_counts = {e: 0 for e in ENGS}
        for o in self.ops:
            if o.kind == "c" and o.need_inc:
                eng_counts[o.eng] += 1
                o.token = (f"eng_{o.eng}", eng_counts[o.eng])
        sems = {}
        for o in self.ops:
            if o.token is not None:
                sems.setdefault(o.token[0], 0)
                sems[o.token[0]] = max(sems[o.token[0]], o.token[1])
        sched = {e: [] for e in ENGS}
        waited = {e: {} for e in ENGS}
        for o in self.ops:
            if o.kind == "w":
                sem, val = o.token
                if waited[o.eng].get(sem, 0) < val:
                    sched[o.eng].append(("wait", sem, val))
                    waited[o.eng][sem] = val
                continue
            for d in o.deps:
                sem, val = d.token
                if waited[o.eng].get(sem, 0) < val:
                    sched[o.eng].append(("wait", sem, val))
                    waited[o.eng][sem] = val
            sched[o.eng].append(("inst", o))
        final_waits = []
        for sem, top in sems.items():
            if waited[final_eng].get(sem, 0) < top:
                final_waits.append((sem, top))
        sem_handles = {}
        with ExitStack() as ctx:
            for sem in sems:
                sem_handles[sem] = ctx.enter_context(nc.semaphore(sem))
            blk = ctx.enter_context(nc.Block())

            def run_engine(ename):
                def body(eobj):
                    for item in sched[ename]:
                        if item[0] == "wait":
                            eobj.wait_ge(sem_handles[item[1]], item[2])
                        else:
                            o = item[1]
                            inst = o.emit(eobj)
                            if o.token is not None:
                                sem, val = o.token
                                prev = getattr(o, "_prev_val", None)
                                amt = 16 if o.kind == "d" else 1
                                inst.then_inc(sem_handles[sem], amt)
                    if ename == final_eng:
                        for sem, top in final_waits:
                            eobj.wait_ge(sem_handles[sem], top)
                return body

            blk.tensor(run_engine("pe"))
            blk.scalar(run_engine("act"))
            blk.vector(run_engine("dve"))
            blk.gpsimd(run_engine("pool"))
            blk.sync(run_engine("sp"))
        return dict(n_ops=len(self.ops), sems=len(sems))



SMALLS = {}


def _smalls_layout():
    cols = 0
    spec = {}
    def add(name, n):
        nonlocal cols
        spec[name] = (cols, n)
        cols += n
    add("g1", NG); add("b1", NG); add("g2", NG); add("b2", NG)
    for cv in ("cpe1", "dwc", "lepe", "cpe2"):
        add(cv + "_w", 3 * NG)
        add(cv + "_b", NG)
    add("in_b", NG); add("actp_b", NG); add("qk_b", 2 * NG); add("qk_nb", 2 * NG); add("out_b", NG)
    add("fc1_b", NGH); add("fc2_b", NG)
    add("w1_s", NGH); add("w2_s", NGH); add("h1_s", NGH); add("h2_s", NGH)
    add("w3_s", NG)
    add("lmask", HALO); add("rmask", HALO)
    add("bsel", 2)
    return spec, cols


SMALLS, SMALLS_COLS = _smalls_layout()


def build_program(upto="B4", fake_cc=False, sim_sync=False, hw_same_engine_sync=True):
    nc = bass.Bass("TRN2", target_bir_lowering=False)
    bld = Builder(nc, sim_sync=sim_sync, hw_same_engine_sync=hw_same_engine_sync)
    phases_all = ["PH1", "PH2", "PH3", "PH4", "PH5", "PH6", "PH7", "PH8",
                  "PH9", "F1", "F2", "F3", "F4", "B0", "B1", "B2", "B3", "B4"]
    phases = phases_all[:phases_all.index(upto) + 1]

    dt_in = {}
    def din(name, shape, dtype=F32):
        dt_in[name] = nc.dram_tensor(name, list(shape), dtype, kind="ExternalInput")
        return dt_in[name]

    u_t = din("u_t", [P, NG * TP])
    smalls = din("smalls", [P, SMALLS_COLS])
    w_in = din("w_in", [P, NG * C])
    w_actp = din("w_actp", [P, NG * C])
    w_qk = din("w_qk", [P, NG * 2 * C])
    w_out = din("w_out", [P, NG * C])
    w_fc1 = din("w_fc1", [NGH, P, NG * P])
    w_fc2 = din("w_fc2", [NG, P, NGH * P])
    w_w1 = din("w_w1", [NGH, P, NG * P])
    w_w2 = din("w_w2", [NGH, P, NG * P])
    w_h1 = din("w_h1", [NGH, P, NGH * P])
    w_h2 = din("w_h2", [NGH, P, NGH * P])
    w_w3 = din("w_w3", [NG, P, NGH * P])

    out_t = nc.dram_tensor("out_t", [P, NG * T], F32, kind="ExternalOutput")

    a_sp = nc.dram_tensor("a_sp", [P, NG * TP], F32, kind="Internal")
    q_sp = nc.dram_tensor("q_sp", [P, NG * TP], BF16, kind="Internal")
    ar_sp = nc.dram_tensor("ar_sp", [P, NG * TP], BF16, kind="Internal")
    lp_sp = nc.dram_tensor("lp_sp", [P, NG * TP], BF16, kind="Internal")
    x2_sp = nc.dram_tensor("x2_sp", [P, NG * T], F32, kind="Internal")
    h0_sp = nc.dram_tensor("h0_sp", [P, NGH * T], BF16, kind="Internal")
    h1_sp = nc.dram_tensor("h1_sp", [P, NGH * T], BF16, kind="Internal")
    h2_sp = nc.dram_tensor("h2_sp", [P, NGH * T], BF16, kind="Internal")
    CCN = NG * P + NG
    cc_in = nc.dram_tensor("cc_in", [2, P * CCN], F32, kind="Internal")
    cc_out = nc.dram_tensor("cc_out", [2, P * CCN], F32, kind="Internal")

    rope_d = nc.inline_tensor(np.ascontiguousarray(_rope_lhst().reshape(NG * P, P)), "rope_d")
    zmask_d = nc.inline_tensor(np.ascontiguousarray(_zmask().reshape(NG * P, NH)), "zmask_d")
    zsel_d = nc.inline_tensor(np.ascontiguousarray(_zsel().reshape(NG * NH, P)), "zsel_d")
    ident_d = nc.inline_tensor(np.eye(P, dtype=BF), "ident_d")
    onesc_d = nc.inline_tensor(np.ones((P, 1), np.float32), "onesc_d")
    onesr_d = nc.inline_tensor(np.ones((1, P), np.float32), "onesr_d")

    psum = {}
    with ExitStack() as psctx:
        for nm, dt in [("pmm0", F32), ("pmm1", F32), ("pmm2", F32), ("pst", F32),
                       ("pkv0", F32), ("pkv1", F32), ("ptra", BF16), ("ptrb", BF16)]:
            psum[nm] = psctx.enter_context(nc.psum_tensor(nm, [P, 512], dt))
        _build_body(nc, bld, phases, fake_cc, locals())
        meta = bld.finalize(None)
    return nc, meta


def _build_body(nc, bld, phases, fake_cc, env):
    psum = env["psum"]
    u_t, smalls = env["u_t"], env["smalls"]
    out_t = env["out_t"]
    a_sp, q_sp, ar_sp, lp_sp = env["a_sp"], env["q_sp"], env["ar_sp"], env["lp_sp"]
    x2_sp, h1_sp, h2_sp = env["x2_sp"], env["h1_sp"], env["h2_sp"]
    cc_in, cc_out = env["cc_in"], env["cc_out"]
    CCN = env["CCN"]

    st_w = bld.stream("sw", "pool", 8)
    st_io = bld.stream("sio", "sp", 12)
    st_h = bld.stream("sh", "act", 8)
    st_cc = bld.stream("scc", "pool", 2)

    R = bld.rs
    RP = bld.rp
    RD = bld.rd

    cst = {}
    cst["ones_c"] = bld.alloc("ones_c", [P, 1], F32)
    cst["ones_r"] = bld.alloc("ones_r", [1, P], F32)
    cst["ident"] = bld.alloc("ident", [P, P], BF16)
    cst["rope"] = bld.alloc("rope", [P, NG * P], BF16)
    cst["zmask"] = bld.alloc("zmask", [P, NG * NH], BF16)
    cst["zsel"] = bld.alloc("zsel", [NH, NG * P], F32)
    cst["sm"] = bld.alloc("sm", [P, SMALLS_COLS], F32)

    init_dmas = []
    for dname, key, hsl in [("onesc_d", "ones_c", None), ("onesr_d", "ones_r", None),
                            ("ident_d", "ident", None), ("rope_d", "rope", "g"),
                            ("zmask_d", "zmask", "g"), ("zsel_d", "zsel", "g16")]:
        dh = env[dname] if dname in env else None
    def dma(stream, out_ap, in_ap, reads, writes, tag=""):
        return stream.wave([(lambda e, o=out_ap, i=in_ap: e.dma_start(out=o, in_=i),
                             reads, writes)])

    rope_d, zmask_d, zsel_d = env["rope_d"], env["zmask_d"], env["zsel_d"]
    ident_d, onesc_d, onesr_d = env["ident_d"], env["onesc_d"], env["onesr_d"]

    dma(st_io, cst["ones_c"][:], onesc_d.ap(), [RD(onesc_d)], [R(cst["ones_c"])])
    dma(st_io, cst["ones_r"][:], onesr_d.ap(), [RD(onesr_d)], [R(cst["ones_r"])])
    dma(st_io, cst["ident"][:], ident_d.ap(), [RD(ident_d)], [R(cst["ident"])])
    dma(st_io, cst["rope"][:].rearrange("p (g f) -> p g f", g=NG),
        rope_d.ap().rearrange("(g p) f -> p g f", g=NG),
        [RD(rope_d)], [R(cst["rope"])])
    dma(st_io, cst["zmask"][:].rearrange("p (g f) -> p g f", g=NG),
        zmask_d.ap().rearrange("(g p) f -> p g f", g=NG),
        [RD(zmask_d)], [R(cst["zmask"])])
    dma(st_io, cst["zsel"][:].rearrange("p (g f) -> p g f", g=NG),
        zsel_d.ap().rearrange("(g p) f -> p g f", g=NG),
        [RD(zsel_d)], [R(cst["zsel"])])
    dma(st_io, cst["sm"][:], smalls.ap(), [RD(smalls)], [R(cst["sm"])])

    def smcol(name, i=0):
        off, n = SMALLS[name]
        assert i < n
        return cst["sm"][:, off + i:off + i + 1]

    def smres(name):
        off, n = SMALLS[name]
        return R(cst["sm"], off, off + n)

    sm = cst["sm"]

    def mm(bank, bc0, bc1, lhsT_ap, rhs_ap, reads, start, stop, tag=""):
        ps = psum[bank]
        bld.op("pe",
               lambda e, o=ps[:, bc0:bc1], l=lhsT_ap, r=rhs_ap, st=start, sp=stop:
                   e.matmul(o, l, r, start=st, stop=sp),
               reads=reads + [RP(bank)] if not start else reads,
               writes=[RP(bank)], tag=tag)

    def act(out_ap, in_ap, func, reads, writes, bias=None, scale=None, tag=""):
        kw = {}
        if bias is not None:
            kw["bias"] = bias
        if scale is not None:
            kw["scale"] = scale
        bld.op("act", lambda e, o=out_ap, i=in_ap, f=func, k=kw: e.activation(o, i, f, **k),
               reads=reads, writes=writes, tag=tag)

    def dve(meth, args, reads, writes, tag=""):
        bld.op("dve", lambda e, m=meth, a=args: getattr(e, m)(*a),
               reads=reads, writes=writes, tag=tag)

    rows2 = [bld.alloc("rows0", [1, 4 * 512], F32)] * 2
    sq2 = [bld.alloc("sqs0", [P, NG * 512], F32)] * 2
    rbmb2 = [bld.alloc(f"rbmb{i}", [P, 2 * 512], F32) for i in range(2)]
    _ln_pi = [0]

    def ln_pass(src_h, src_base_fn, subs, dst_write, gamma=None, beta=None, tag="ln",
                banks=("pst", "pmm2", "pmm0", "pmm1")):
        bk_sum, bk_sq, bk_rb, bk_mb = banks
        for (c0, w) in subs:
            pi = _ln_pi[0] % 2
            _ln_pi[0] += 1
            rows = rows2[pi]
            sq_sub = sq2[pi]
            rbmb = rbmb2[pi]
            for g in range(NG):
                sap, sres = src_base_fn(g, c0, w)
                if g % 2 == 0:
                    act(sq_sub[:, g * 512:g * 512 + w], sap, AF.Square,
                        [sres], [R(sq_sub, g * 512, g * 512 + w)], tag=f"{tag}sq")
                else:
                    bld.op("pool", lambda e, o=sq_sub[:, g * 512:g * 512 + w], i=sap:
                           e.tensor_tensor(o, i, i, ALU.mult),
                           reads=[sres], writes=[R(sq_sub, g * 512, g * 512 + w)],
                           tag=f"{tag}sqp")
            for g in range(NG):
                sap, sres = src_base_fn(g, c0, w)
                ps = psum[bk_sum]
                bld.op("pe", lambda e, o=ps[0:1, 0:w], l=cst["ones_c"][:], r=sap,
                       st=(g == 0), sp=(g == NG - 1): e.matmul(o, l, r, start=st, stop=sp),
                       reads=[sres, R(cst["ones_c"])] + ([] if g == 0 else [RP(bk_sum)]),
                       writes=[RP(bk_sum)], tag=f"{tag}st1")
            for g in range(NG):
                ps = psum[bk_sq]
                bld.op("pe", lambda e, o=ps[0:1, 0:w], l=cst["ones_c"][:],
                       r=sq_sub[:, g * 512:g * 512 + w],
                       st=(g == 0), sp=(g == NG - 1): e.matmul(o, l, r, start=st, stop=sp),
                       reads=[R(sq_sub, g * 512, g * 512 + w), R(cst["ones_c"])] +
                             ([] if g == 0 else [RP(bk_sq)]),
                       writes=[RP(bk_sq)], tag=f"{tag}st2")
            srow = rows[0:1, 0:w]
            vrow = rows[0:1, 512:512 + w]
            rrow = rows[0:1, 1024:1024 + w]
            nmrow = rows[0:1, 1536:1536 + w]
            dve("tensor_scalar", (srow, psum[bk_sum][0:1, 0:w], 1.0 / C, None, ALU.mult),
                [RP(bk_sum)], [R(rows, 0, w, 0, 1)], tag=f"{tag}r1")
            dve("tensor_scalar", (vrow, psum[bk_sq][0:1, 0:w], 1.0 / C, None, ALU.mult),
                [RP(bk_sq)], [R(rows, 512, 512 + w, 0, 1)], tag=f"{tag}r2")
            dve("tensor_tensor", (nmrow, srow, srow, ALU.mult),
                [R(rows, 0, w, 0, 1)], [R(rows, 1536, 1536 + w, 0, 1)], tag=f"{tag}r3")
            dve("tensor_tensor", (vrow, vrow, nmrow, ALU.subtract),
                [R(rows, 512, 512 + w, 0, 1), R(rows, 1536, 1536 + w, 0, 1)],
                [R(rows, 512, 512 + w, 0, 1)], tag=f"{tag}r4")
            dve("tensor_scalar", (vrow, vrow, 1e-5, None, ALU.add),
                [R(rows, 512, 512 + w, 0, 1)], [R(rows, 512, 512 + w, 0, 1)], tag=f"{tag}r5")
            act(vrow, vrow, AF.Sqrt, [R(rows, 512, 512 + w, 0, 1)],
                [R(rows, 512, 512 + w, 0, 1)], tag=f"{tag}r6")
            dve("reciprocal", (rrow, vrow),
                [R(rows, 512, 512 + w, 0, 1)], [R(rows, 1024, 1024 + w, 0, 1)], tag=f"{tag}r7")
            dve("scalar_tensor_tensor", (nmrow, srow, -1.0, rrow, ALU.mult, ALU.mult),
                [R(rows, 0, w, 0, 1), R(rows, 1024, 1024 + w, 0, 1)],
                [R(rows, 1536, 1536 + w, 0, 1)], tag=f"{tag}r8")
            bld.op("pe", lambda e, o=psum[bk_rb][:, 0:w], l=cst["ones_r"][:], r=rrow:
                   e.matmul(o, l, r, start=True, stop=True),
                   reads=[R(rows, 1024, 1024 + w, 0, 1), R(cst["ones_r"])],
                   writes=[RP(bk_rb)], tag=f"{tag}b1")
            bld.op("pe", lambda e, o=psum[bk_mb][:, 0:w], l=cst["ones_r"][:], r=nmrow:
                   e.matmul(o, l, r, start=True, stop=True),
                   reads=[R(rows, 1536, 1536 + w, 0, 1), R(cst["ones_r"])],
                   writes=[RP(bk_mb)], tag=f"{tag}b2")
            dve("tensor_copy", (rbmb[:, 0:w], psum[bk_rb][:, 0:w]),
                [RP(bk_rb)], [R(rbmb, 0, w)], tag=f"{tag}be1")
            dve("tensor_copy", (rbmb[:, 512:512 + w], psum[bk_mb][:, 0:w]),
                [RP(bk_mb)], [R(rbmb, 512, 512 + w)], tag=f"{tag}be2")
            rb_ap = rbmb[:, 0:w]
            mb_ap = rbmb[:, 512:512 + w]
            for g in range(NG):
                dst_write(g, c0, w, rb_ap, mb_ap, [R(rbmb, 0, w), R(rbmb, 512, 512 + w)])

    def conv_pass(src_h, dst_emit, taps, bias, subs, tmp_h, tag="cv"):
        for (c0, w) in subs:
            for g in range(NG):
                base = g * TP
                s_rs = lambda a0, a1: R(src_h, base + a0, base + a1)
                tc = (g % 2) * 512
                t_ap = tmp_h[:, tc:tc + w]
                t_rs = R(tmp_h, tc, tc + w)
                off, _ = SMALLS[taps]
                w0 = sm[:, off + g * 3 + 0:off + g * 3 + 1]
                w1 = sm[:, off + g * 3 + 1:off + g * 3 + 2]
                w2 = sm[:, off + g * 3 + 2:off + g * 3 + 3]
                boff, _ = SMALLS[bias]
                bcol = sm[:, boff + g:boff + g + 1]
                dve("tensor_scalar", (t_ap, src_h[:, base + c0:base + c0 + w],
                                      w1, bcol, ALU.mult, ALU.add),
                    [s_rs(c0, c0 + w), smres(taps), smres(bias)], [t_rs], tag=f"{tag}c")
                lo = max(c0, 1)
                if lo < c0 + w:
                    dve("scalar_tensor_tensor",
                        (tmp_h[:, tc + lo - c0:tc + w], src_h[:, base + lo - 1:base + c0 + w - 1],
                         w0, tmp_h[:, tc + lo - c0:tc + w], ALU.mult, ALU.add),
                        [s_rs(lo - 1, c0 + w - 1), t_rs, smres(taps)], [t_rs], tag=f"{tag}l")
                hi = min(c0 + w, TP - 1)
                if hi > c0:
                    dve("scalar_tensor_tensor",
                        (tmp_h[:, tc:tc + hi - c0], src_h[:, base + c0 + 1:base + hi + 1],
                         w2, tmp_h[:, tc:tc + hi - c0], ALU.mult, ALU.add),
                        [s_rs(c0 + 1, hi + 1), t_rs, smres(taps)], [t_rs], tag=f"{tag}r")
                dst_emit(g, c0, w, t_ap, [t_rs])

    def mask_halo(h, tag=""):
        for g in range(NG):
            base = g * TP
            dve("tensor_tensor", (h[:, base:base + HALO], h[:, base:base + HALO],
                                  sm[:, SMALLS["lmask"][0]:SMALLS["lmask"][0] + HALO],
                                  ALU.mult),
                [R(h, base, base + HALO), smres("lmask")], [R(h, base, base + HALO)],
                tag=f"mskl{tag}")
            dve("tensor_tensor", (h[:, base + TP - HALO:base + TP],
                                  h[:, base + TP - HALO:base + TP],
                                  sm[:, SMALLS["rmask"][0]:SMALLS["rmask"][0] + HALO],
                                  ALU.mult),
                [R(h, base + TP - HALO, base + TP), smres("rmask")],
                [R(h, base + TP - HALO, base + TP)], tag=f"mskr{tag}")

    def wload(dram_h, sb_h, tag=""):
        return dma(st_w, sb_h[:], dram_h.ap(), [RD(dram_h)], [R(sb_h)], tag)

    def proj(w_sb, F_out, x_h, x_base_fn, subs, evac, banks=("pmm0", "pmm1", "pmm2"),
             n_k=NG, tag="pj"):
        bi = 0
        for (c0, w) in subs:
            for oc in range(F_out // P):
                bank = banks[bi % len(banks)]
                bi += 1
                for k in range(n_k):
                    l_ap = w_sb[:, k * F_out + oc * P:k * F_out + (oc + 1) * P]
                    x_ap, x_res = x_base_fn(k, c0, w)
                    ps = psum[bank]
                    bld.op("pe",
                           lambda e, o=ps[:, 0:w], l=l_ap, r=x_ap,
                           st=(k == 0), sp=(k == n_k - 1): e.matmul(o, l, r, start=st, stop=sp),
                           reads=[R(w_sb, k * F_out + oc * P, k * F_out + (oc + 1) * P),
                                  x_res] + ([] if k == 0 else [RP(bank)]),
                           writes=[RP(bank)], tag=f"{tag}mm")
                evac(oc, c0, w, bank, [RP(bank)])

    xn0b = bld.alloc("xn0b", [P, NG * TP], BF16)
    bufs = {}

    def load_u_sub(c0, w, tag="u"):
        u_sub = bufs["u_sub"]
        tok = dma(st_io, u_sub[:, 0:NG * 512].rearrange("p (g m) -> p g m", g=NG)[:, :, 0:w],
                  u_t.ap().rearrange("p (g m) -> p g m", g=NG)[:, :, c0:c0 + w],
                  [RD(u_t)], [R(u_sub)], tag)
        return tok

    cvt = bld.alloc("cvt", [P, 2 * 512], F32)
    if "PH1" in phases:
        bufs["u_sub"] = bld.alloc("u_sub", [P, NG * 512], F32)
        u_sub = bufs["u_sub"]
        for (c0, w) in SUBS:
            load_u_sub(c0, w)
            def srcfn(g, cc0, ww, _c0=c0):
                return (u_sub[:, g * 512:g * 512 + ww], R(u_sub, g * 512, g * 512 + ww))
            def wr(g, cc0, ww, rb, mb, deps, _c0=c0):
                t = cvt[:, 0:ww]
                tres = R(cvt, 0, ww)
                dve("tensor_tensor", (t, u_sub[:, g * 512:g * 512 + ww], rb, ALU.mult),
                    [R(u_sub, g * 512, g * 512 + ww)] + deps, [tres], tag="ln0a")
                o = xn0b[:, g * TP + _c0:g * TP + _c0 + ww]
                dve("tensor_tensor", (o, t, mb, ALU.add), [tres] + deps,
                    [R(xn0b, g * TP + _c0, g * TP + _c0 + ww)], tag="ln0b")
            ln_pass(u_sub, srcfn, [(c0, w)], wr, tag="ln0")
        bld.free(u_sub)

    thunks = []
    if "PH1" in phases:
        a_h = bld.alloc("a_h", [P, NG * TP], F32)
        def a_emit(g, c0, w, t_ap, deps):
            o = a_h[:, g * TP + c0:g * TP + c0 + w]
            ores = R(a_h, g * TP + c0, g * TP + c0 + w)
            dve("tensor_tensor", (o, t_ap, xn0b[:, g * TP + c0:g * TP + c0 + w], ALU.add),
                deps + [R(xn0b, g * TP + c0, g * TP + c0 + w)], [ores], tag="a")
        for (c0, w) in SUBS:
            thunks.append(lambda c0=c0, w=w: conv_pass(
                xn0b, a_emit, "cpe1_w", "cpe1_b", [(c0, w)], cvt, tag="cv1"))
        thunks.append(lambda: dma(st_io, a_sp.ap(), a_h[:], [R(a_h)], [RD(a_sp)],
                                  "aspill"))

    if "PH2" in phases:
        xn1 = bld.alloc("xn1", [P, NG * TP], BF16)
        def srcfn2(g, c0, w):
            return (a_h[:, g * TP + c0:g * TP + c0 + w], R(a_h, g * TP + c0, g * TP + c0 + w))
        def wr2(g, c0, w, rb, mb, deps):
            t = cvt[:, 0:w]
            tres = R(cvt, 0, w)
            dve("tensor_tensor", (t, a_h[:, g * TP + c0:g * TP + c0 + w], rb, ALU.mult),
                [R(a_h, g * TP + c0, g * TP + c0 + w)] + deps, [tres], tag="ln1a")
            dve("tensor_tensor", (t, t, mb, ALU.add), [tres] + deps, [tres], tag="ln1b")
            o = xn1[:, g * TP + c0:g * TP + c0 + w]
            dve("tensor_scalar", (o, t, smcol("g1", g), smcol("b1", g), ALU.mult, ALU.add),
                [tres, smres("g1"), smres("b1")],
                [R(xn1, g * TP + c0, g * TP + c0 + w)], tag="ln1c")
        for (c0, w) in SUBS:
            thunks.append(lambda c0=c0, w=w: ln_pass(
                a_h, srcfn2, [(c0, w)], wr2, tag="ln1",
                banks=("pkv0", "pkv1", "pkv0", "pkv1")))

    f1_units = []
    if "F1" in phases:
        wslr = [bld.alloc(f"w_sl{i}", [P, 2 * NG * P], BF16) for i in range(4)]
        t1s = bld.alloc("t1s", [P, 4 * 512], F32)
        h0ev = bld.alloc("h0ev", [P, 2 * 512], BF16)
        h0_sp = env["h0_sp"]
        w_w1, w_w2 = env["w_w1"], env["w_w2"]

        def f1_unit(oc):
            slb = wslr[oc % 4]
            dma(st_w, slb[:, 0:NG * P], w_w1.ap()[oc], [RD(w_w1)], [R(slb, 0, NG * P)],
                f"w1s{oc}")
            dma(st_w, slb[:, NG * P:2 * NG * P], w_w2.ap()[oc], [RD(w_w2)],
                [R(slb, NG * P, 2 * NG * P)], f"w2s{oc}")
            banks4w = ("pmm0", "pmm1", "pmm2", "pst")
            for k in range(NG):
                for si, (c0, w) in enumerate(RSUBS):
                    b1 = banks4w[si]
                    bld.op("pe", lambda e, o=psum[b1][:, 0:w],
                           l=slb[:, k * P:(k + 1) * P],
                           r=xn0b[:, k * TP + c0:k * TP + c0 + w],
                           st=(k == 0), sp=(k == NG - 1): e.matmul(o, l, r, start=st, stop=sp),
                           reads=[R(slb, k * P, (k + 1) * P),
                                  R(xn0b, k * TP + c0, k * TP + c0 + w)] +
                                 ([] if k == 0 else [RP(b1)]),
                           writes=[RP(b1)], tag="w1mm")
            for si, (c0, w) in enumerate(RSUBS):
                b1 = banks4w[si]
                act(t1s[:, si * 512:si * 512 + w], psum[b1][:, 0:w], AF.Silu,
                    [RP(b1), smres("w1_s")], [R(t1s, si * 512, si * 512 + w)],
                    scale=smcol("w1_s", oc), tag="f1silu")
            for k in range(NG):
                for si, (c0, w) in enumerate(RSUBS):
                    b2 = banks4w[si]
                    bld.op("pe", lambda e, o=psum[b2][:, 0:w],
                           l=slb[:, NG * P + k * P:NG * P + (k + 1) * P],
                           r=xn0b[:, k * TP + c0:k * TP + c0 + w],
                           st=(k == 0), sp=(k == NG - 1): e.matmul(o, l, r, start=st, stop=sp),
                           reads=[R(slb, NG * P + k * P, NG * P + (k + 1) * P),
                                  R(xn0b, k * TP + c0, k * TP + c0 + w)] +
                                 ([] if k == 0 else [RP(b2)]),
                           writes=[RP(b2)], tag="w2mm")
            for si, (c0, w) in enumerate(RSUBS):
                b2 = banks4w[si]
                par = si % 2
                dve("scalar_tensor_tensor",
                    (h0ev[:, par * 512:par * 512 + w], psum[b2][:, 0:w],
                     smcol("w2_s", oc), t1s[:, si * 512:si * 512 + w], ALU.mult, ALU.mult),
                    [RP(b2), R(t1s, si * 512, si * 512 + w), smres("w2_s")],
                    [R(h0ev, par * 512, par * 512 + w)], tag="f1mul")
                hcol = oc * T + (c0 - R0)
                dma(st_h, h0_sp.ap()[:, hcol:hcol + w], h0ev[:, par * 512:par * 512 + w],
                    [R(h0ev, par * 512, par * 512 + w)], [RD(h0_sp)], "h0sp")

        f1_units = [lambda oc=oc: f1_unit(oc) for oc in range(NGH)]

    na, nb = len(f1_units), len(thunks)
    ia = ib = 0
    while ia < na or ib < nb:
        if ib >= nb or (ia < na and ia * (nb + 1) <= ib * (na + 1)):
            f1_units[ia]()
            ia += 1
        else:
            thunks[ib]()
            ib += 1

    if "F1" in phases:
        for _h in wslr:
            bld.free(_h)
        bld.free(t1s)
        bld.free(h0ev)
    if "PH2" in phases:
        bld.free(a_h)
    bld.free(xn0b)
    bld.free(sq2[0])
    bld.free(rows2[0])
    bld.free(rbmb2[0])
    bld.free(rbmb2[1])

    if "PH3" in phases:
        in_wb = bld.alloc("in_wb", [P, NG * C], BF16)
        wload(env["w_in"], in_wb, "w_in")
        xi0 = bld.alloc("xi0", [P, NG * TP], BF16)
        def xfn(k, c0, w):
            return (xn1[:, k * TP + c0:k * TP + c0 + w],
                    R(xn1, k * TP + c0, k * TP + c0 + w))
        def ev3(oc, c0, w, bank, deps):
            act(xi0[:, oc * TP + c0:oc * TP + c0 + w], psum[bank][:, 0:w], AF.Identity,
                deps + [smres("in_b")], [R(xi0, oc * TP + c0, oc * TP + c0 + w)],
                bias=smcol("in_b", oc), tag="ev3")
        proj(in_wb, C, xn1, xfn, SUBS, ev3, tag="inw")
        bld.free(in_wb)

    if "PH4" in phases:
        mask_halo(xi0, "xi0")
        xi = bld.alloc("xi", [P, NG * TP], BF16)
        def xi_emit(g, c0, w, t_ap, deps):
            act(xi[:, g * TP + c0:g * TP + c0 + w], t_ap, AF.Silu,
                deps, [R(xi, g * TP + c0, g * TP + c0 + w)], tag="silu_xi")
        conv_pass(xi0, xi_emit, "dwc_w", "dwc_b", SUBS, cvt, tag="cv2")
        mask_halo(xi, "xi")
        bld.free(xi0)

    if "PH5" in phases:
        qk_wb = bld.alloc("qk_wb", [P, NG * 2 * C], BF16)
        wload(env["w_qk"], qk_wb, "w_qk")
        q_h = bld.alloc("q_h", [P, NG * TP], BF16)
        krope = bld.alloc("krope", [P, NG * TP], BF16)
        k_sub = bld.alloc("k_sub", [P, NG * 512], BF16)
        elu_e = bld.alloc("elu_e", [P, 2 * 512], F32)
        elu_p = bld.alloc("elu_p", [P, 2 * 512], F32)
        ksum = bld.alloc("ksum", [P, NG], F32)
        ksum_t = bld.alloc("ksum_t", [P, NG], F32)
        dve("memset", (ksum[:], 0.0), [], [R(ksum)], tag="ks0")

        def xfn5(k, c0, w):
            return (xi[:, k * TP + c0:k * TP + c0 + w],
                    R(xi, k * TP + c0, k * TP + c0 + w))

        def elu_evac(oc, c0, w, bank, deps, out_ap, out_res):
            bcol = smcol("qk_b", oc)
            nbcol = smcol("qk_nb", oc)
            eo = (oc % 2) * 512
            act(elu_e[:, eo:eo + w], psum[bank][:, 0:w], AF.Relu,
                deps + [smres("qk_nb")], [R(elu_e, eo, eo + w)],
                bias=nbcol, scale=-1.0, tag="elm")
            act(elu_e[:, eo:eo + w], elu_e[:, eo:eo + w], AF.Exp,
                [R(elu_e, eo, eo + w)], [R(elu_e, eo, eo + w)], scale=-1.0, tag="elexp")
            act(elu_p[:, eo:eo + w], psum[bank][:, 0:w], AF.Relu,
                deps + [smres("qk_b")], [R(elu_p, eo, eo + w)], bias=bcol, tag="elp")
            bld.op("pool", lambda e, o=out_ap, a=elu_e[:, eo:eo + w],
                   b=elu_p[:, eo:eo + w]: e.tensor_tensor(o, a, b, ALU.add),
                   reads=[R(elu_e, eo, eo + w), R(elu_p, eo, eo + w)],
                   writes=[out_res], tag="eladd")

        for (c0, w) in SUBS:
            for oc in range(NG):
                bank = ("pmm0", "pmm1", "pmm2")[oc % 3]
                for k in range(NG):
                    l_ap = qk_wb[:, k * 2 * C + oc * P:k * 2 * C + (oc + 1) * P]
                    x_ap, x_res = xfn5(k, c0, w)
                    bld.op("pe", lambda e, o=psum[bank][:, 0:w], l=l_ap, r=x_ap,
                           st=(k == 0), sp=(k == NG - 1): e.matmul(o, l, r, start=st, stop=sp),
                           reads=[R(qk_wb, k * 2 * C + oc * P, k * 2 * C + (oc + 1) * P),
                                  x_res] + ([] if k == 0 else [RP(bank)]),
                           writes=[RP(bank)], tag="qmm")
                elu_evac(oc, c0, w, bank, [RP(bank)],
                         q_h[:, oc * TP + c0:oc * TP + c0 + w],
                         R(q_h, oc * TP + c0, oc * TP + c0 + w))
            for oc in range(NG):
                bank = ("pmm0", "pmm1", "pmm2")[oc % 3]
                for k in range(NG):
                    l_ap = qk_wb[:, k * 2 * C + (NG + oc) * P:k * 2 * C + (NG + oc + 1) * P]
                    x_ap, x_res = xfn5(k, c0, w)
                    bld.op("pe", lambda e, o=psum[bank][:, 0:w], l=l_ap, r=x_ap,
                           st=(k == 0), sp=(k == NG - 1): e.matmul(o, l, r, start=st, stop=sp),
                           reads=[R(qk_wb, k * 2 * C + (NG + oc) * P,
                                    k * 2 * C + (NG + oc + 1) * P),
                                  x_res] + ([] if k == 0 else [RP(bank)]),
                           writes=[RP(bank)], tag="kmm")
                elu_evac(NG + oc, c0, w, bank, [RP(bank)],
                         k_sub[:, oc * 512:oc * 512 + w],
                         R(k_sub, oc * 512, oc * 512 + w))
            r0 = max(c0, R0)
            r1_ = min(c0 + w, R1)
            if r0 < r1_:
                for g in range(NG):
                    dve("reduce_sum", (ksum_t[:, g:g + 1],
                                       k_sub[:, g * 512 + r0 - c0:g * 512 + r1_ - c0], AX.X),
                        [R(k_sub, g * 512 + r0 - c0, g * 512 + r1_ - c0)],
                        [R(ksum_t, g, g + 1)], tag="ksr")
                dve("tensor_tensor", (ksum[:], ksum[:], ksum_t[:], ALU.add),
                    [R(ksum), R(ksum_t)], [R(ksum)], tag="ksa")
            for g in range(NG):
                bank = ("pmm0", "pmm1")[g % 2]
                bld.op("pe", lambda e, o=psum[bank][:, 0:w],
                       l=cst["rope"][:, g * P:(g + 1) * P], r=k_sub[:, g * 512:g * 512 + w]:
                       e.matmul(o, l, r, start=True, stop=True),
                       reads=[R(cst["rope"], g * P, (g + 1) * P),
                              R(k_sub, g * 512, g * 512 + w)],
                       writes=[RP(bank)], tag="krmm")
                dve("tensor_copy", (krope[:, g * TP + c0:g * TP + c0 + w],
                                    psum[bank][:, 0:w]),
                    [RP(bank)], [R(krope, g * TP + c0, g * TP + c0 + w)], tag="krev")
        bld.free(qk_wb)
        bld.free(k_sub)
        bld.free(elu_e)
        bld.free(elu_p)
        bld.free(ksum_t)

    if "PH6" in phases:
        krt = bld.alloc("krt", [P, 2 * P], BF16)
        vht = bld.alloc("vht", [P, 2 * P], BF16)
        def kv_bank(g):
            return ("pkv0", g % 4) if g < 4 else ("pkv1", g % 4)

        def emit_tr(g, ti):
            tc0 = R0 + ti * P
            par = ti % 2
            tb = ("ptra", "ptrb")[par]
            bld.op("pe", lambda e, o=psum[tb][:, 0:P],
                   l=krope[:, g * TP + tc0:g * TP + tc0 + P], i=cst["ident"][:]:
                   e.transpose(o, l, i),
                   reads=[R(krope, g * TP + tc0, g * TP + tc0 + P), R(cst["ident"]),
                          RP(tb)],
                   writes=[RP(tb)], tag="trk")
            dve("tensor_copy", (krt[:, par * P:(par + 1) * P], psum[tb][:, 0:P]),
                [RP(tb)], [R(krt, par * P, (par + 1) * P)], tag="trke")
            bld.op("pe", lambda e, o=psum[tb][:, P:2 * P],
                   l=xi[:, g * TP + tc0:g * TP + tc0 + P], i=cst["ident"][:]:
                   e.transpose(o, l, i),
                   reads=[R(xi, g * TP + tc0, g * TP + tc0 + P), R(cst["ident"]),
                          RP(tb)],
                   writes=[RP(tb)], tag="trv")
            dve("tensor_copy", (vht[:, par * P:(par + 1) * P], psum[tb][:, P:2 * P]),
                [RP(tb)], [R(vht, par * P, (par + 1) * P)], tag="trve")

        def emit_kvmm(g, ti):
            par = ti % 2
            kb, slot = kv_bank(g)
            bld.op("pe", lambda e, o=psum[kb][:, slot * P:(slot + 1) * P],
                   l=krt[:, par * P:(par + 1) * P], r=vht[:, par * P:(par + 1) * P],
                   st=(ti == 0), sp=(ti == 15): e.matmul(o, l, r, start=st, stop=sp),
                   reads=[R(krt, par * P, (par + 1) * P), R(vht, par * P, (par + 1) * P)]
                         + ([] if ti == 0 else [RP(kb)]),
                   writes=[RP(kb)], tag="kvmm")

        for g in range(NG):
            emit_tr(g, 0)
            for ti in range(16):
                if ti < 15:
                    emit_tr(g, ti + 1)
                emit_kvmm(g, ti)
        ccs = bld.alloc("ccs", [P, 2 * CCN], F32)
        kv_sb = bld.alloc("kv_sb", [P, NG * P], F32)
        for g in range(NG):
            kb, slot = kv_bank(g)
            dve("tensor_scalar", (kv_sb[:, g * P:(g + 1) * P],
                                  psum[kb][:, slot * P:(slot + 1) * P],
                                  1.0 / 8192.0, None, ALU.mult),
                [RP(kb)], [R(kv_sb, g * P, (g + 1) * P)], tag="kvev")
        for b in range(2):
            dve("tensor_scalar", (ccs[:, b * CCN:b * CCN + NG * P], kv_sb[:],
                                  smcol("bsel", b), None, ALU.mult),
                [R(kv_sb), smres("bsel")], [R(ccs, b * CCN, b * CCN + NG * P)], tag="cckv")
            dve("tensor_scalar", (ccs[:, b * CCN + NG * P:(b + 1) * CCN], ksum[:],
                                  smcol("bsel", b), None, ALU.mult),
                [R(ksum), smres("bsel")],
                [R(ccs, b * CCN + NG * P, (b + 1) * CCN)], tag="ccks")
        dma(st_cc, cc_in.ap().rearrange("b (p m) -> p b m", p=P),
            ccs[:].rearrange("p (b m) -> p b m", b=2),
            [R(ccs)], [RD(cc_in)], "ccst")
        if fake_cc:
            dma(st_cc, cc_out.ap(), cc_in.ap(), [RD(cc_in)], [RD(cc_out)], "fakecc")
        else:
            o = bld.op("pool", lambda e: e.collective_compute(
                "AllReduce", ALU.add,
                replica_groups=[[0, 1, 2, 3, 4, 5, 6, 7]],
                ins=[cc_in.ap().opt()], outs=[cc_out.ap().opt()]),
                reads=[RD(cc_in)], writes=[RD(cc_out)], kind="cc", tag="cc")
            o.token = ("cc_sem", 1)
        bld.free(krt)
        bld.free(vht)
        bld.free(kv_sb)
        bld.free(ccs)
        bld.free(krope)
        bld.free(ksum)

    if "PH7" in phases:
        lepe = bld.alloc("lepe", [P, NG * TP], BF16)
        def lp_emit(g, c0, w, t_ap, deps):
            dve("tensor_copy", (lepe[:, g * TP + c0:g * TP + c0 + w], t_ap),
                deps, [R(lepe, g * TP + c0, g * TP + c0 + w)], tag="lpcp")
        conv_pass(xi, lp_emit, "lepe_w", "lepe_b", SUBS, cvt, tag="cv3")
        bld.free(xi)
        bld.free(cvt)

    if "PH8" in phases:
        actp_wb = bld.alloc("actp_wb", [P, NG * C], BF16)
        wload(env["w_actp"], actp_wb, "w_actp")
        ar_h = bld.alloc("ar_h", [P, NG * TP], BF16)
        def xfn8(k, c0, w):
            return (xn1[:, k * TP + c0:k * TP + c0 + w],
                    R(xn1, k * TP + c0, k * TP + c0 + w))
        def ev8(oc, c0, w, bank, deps):
            act(ar_h[:, oc * TP + c0:oc * TP + c0 + w], psum[bank][:, 0:w], AF.Silu,
                deps + [smres("actp_b")], [R(ar_h, oc * TP + c0, oc * TP + c0 + w)],
                bias=smcol("actp_b", oc), tag="ev8")
        proj(actp_wb, C, xn1, xfn8, SUBS, ev8, tag="actp")
        bld.free(actp_wb)
        bld.free(xn1)

    if "PH9" in phases:
        dma(st_io, q_sp.ap(), q_h[:], [R(q_h)], [RD(q_sp)], "qsp")
        dma(st_io, ar_sp.ap(), ar_h[:], [R(ar_h)], [RD(ar_sp)], "arsp")
        dma(st_io, lp_sp.ap(), lepe[:], [R(lepe)], [RD(lp_sp)], "lpsp")
        bld.free(q_h)
        bld.free(ar_h)
        bld.free(lepe)

    if "F2" in phases:
        h_full = bld.alloc("h_full", [P, NGH * T], BF16)
        dma(st_h, h_full[:], env["h0_sp"].ap(), [RD(env["h0_sp"])], [R(h_full)], "h0rl")

    def h_pass(w_dram, s_name, dst_sp, tag):
        slr = [bld.alloc(f"hsl{i}{tag}", [P, NGH * P], BF16) for i in range(4)]
        ev = bld.alloc(f"hev{tag}", [P, 2 * 512], BF16)
        banks6 = ("pmm0", "pmm1", "pmm2", "pst", "pkv0", "pkv1")
        for oc in range(NGH):
            slb = slr[oc % 4]
            dma(st_w, slb[:], w_dram.ap()[oc], [RD(w_dram)], [R(slb)], f"{tag}s{oc}")
            for k in range(NGH):
                for si, (c0, w) in enumerate(RSUBS):
                    bank = banks6[(oc * 4 + si) % 6]
                    bld.op("pe", lambda e, o=psum[bank][:, 0:w],
                           l=slb[:, k * P:(k + 1) * P],
                           r=h_full[:, k * T + (c0 - R0):k * T + (c0 - R0) + w],
                           st=(k == 0), sp=(k == NGH - 1): e.matmul(o, l, r, start=st, stop=sp),
                           reads=[R(slb, k * P, (k + 1) * P),
                                  R(h_full, k * T + (c0 - R0), k * T + (c0 - R0) + w)] +
                                 ([] if k == 0 else [RP(bank)]),
                           writes=[RP(bank)], tag=f"{tag}mm")
            for si, (c0, w) in enumerate(RSUBS):
                bank = banks6[(oc * 4 + si) % 6]
                epar = si % 2
                act(ev[:, epar * 512:epar * 512 + w], psum[bank][:, 0:w], AF.Gelu,
                    [RP(bank), smres(s_name)], [R(ev, epar * 512, epar * 512 + w)],
                    scale=smcol(s_name, oc), tag=f"{tag}g")
                dma(st_h, dst_sp.ap()[:, oc * T + (c0 - R0):oc * T + (c0 - R0) + w],
                    ev[:, epar * 512:epar * 512 + w],
                    [R(ev, epar * 512, epar * 512 + w)], [RD(dst_sp)], f"{tag}sp")
        for _h in slr:
            bld.free(_h)
        bld.free(ev)

    if "F2" in phases:
        h_pass(env["w_h1"], "h1_s", h1_sp, "f2")
        dma(st_h, h_full[:], h1_sp.ap(), [RD(h1_sp)], [R(h_full)], "h1rl")
    if "F3" in phases:
        h_pass(env["w_h2"], "h2_s", h2_sp, "f3")
        dma(st_h, h_full[:], h2_sp.ap(), [RD(h2_sp)], [R(h_full)], "h2rl")

    if "F4" in phases:
        w3r = [bld.alloc(f"w3sl{i}", [P, NGH * P], BF16) for i in range(4)]
        uo_sub = bld.alloc("uo_sub", [P, 2 * 512], F32)
        fo_sub = bld.alloc("fo_sub", [P, 2 * 512], F32)
        w_w3 = env["w_w3"]
        banks4f = ("pmm0", "pmm1", "pmm2", "pst", "pkv0", "pkv1")
        for oc in range(NG):
            slb = w3r[oc % 4]
            dma(st_w, slb[:], w_w3.ap()[oc], [RD(w_w3)], [R(slb)], f"w3s{oc}")
            for k in range(NGH):
                for si, (c0, w) in enumerate(RSUBS):
                    bank = banks4f[(oc * 4 + si) % 6]
                    bld.op("pe", lambda e, o=psum[bank][:, 0:w],
                           l=slb[:, k * P:(k + 1) * P],
                           r=h_full[:, k * T + (c0 - R0):k * T + (c0 - R0) + w],
                           st=(k == 0), sp=(k == NGH - 1): e.matmul(o, l, r, start=st, stop=sp),
                           reads=[R(slb, k * P, (k + 1) * P),
                                  R(h_full, k * T + (c0 - R0), k * T + (c0 - R0) + w)] +
                                 ([] if k == 0 else [RP(bank)]),
                           writes=[RP(bank)], tag="w3mm")
            for si, (c0, w) in enumerate(RSUBS):
                bank = banks4f[(oc * 4 + si) % 6]
                po = (si % 2) * 512
                dma(st_io, uo_sub[:, po:po + w],
                    u_t.ap()[:, oc * TP + c0:oc * TP + c0 + w],
                    [RD(u_t)], [R(uo_sub, po, po + w)], "u4")
                act(fo_sub[:, po:po + w], psum[bank][:, 0:w], AF.Identity,
                    [RP(bank), smres("w3_s")], [R(fo_sub, po, po + w)],
                    scale=smcol("w3_s", oc), tag="f4ev")
                dve("tensor_tensor", (fo_sub[:, po:po + w], fo_sub[:, po:po + w],
                                      uo_sub[:, po:po + w], ALU.add),
                    [R(fo_sub, po, po + w), R(uo_sub, po, po + w)],
                    [R(fo_sub, po, po + w)], tag="f4add")
                dma(st_io, out_t.ap()[:, oc * T + (c0 - R0):oc * T + (c0 - R0) + w],
                    fo_sub[:, po:po + w], [R(fo_sub, po, po + w)], [RD(out_t)], "f4st")
        for _h in w3r:
            bld.free(_h)
        bld.free(h_full)
        bld.free(uo_sub)
        bld.free(fo_sub)

    if "B0" in phases:
        ccr = bld.alloc("ccr", [P, 2 * CCN], F32)
        kvb = bld.alloc("kvb", [P, NG * P], BF16)
        kml = bld.alloc("kml", [P, NG * NH], BF16)
        kvf = bld.alloc("kvf", [P, NG * P], F32)
        ksg = bld.alloc("ksg", [P, NG], F32)
        dma(st_cc, ccr[:].rearrange("p (b m) -> p b m", b=2),
            cc_out.ap().rearrange("b (p m) -> p b m", p=P),
            [RD(cc_out)], [R(ccr)], "ccrd")
        dve("tensor_scalar", (kvf[:], ccr[:, 0:NG * P], smcol("bsel", 0), None, ALU.mult),
            [R(ccr, 0, NG * P), smres("bsel")], [R(kvf)], tag="b0a")
        dve("scalar_tensor_tensor", (kvf[:], ccr[:, CCN:CCN + NG * P], smcol("bsel", 1),
                                     kvf[:], ALU.mult, ALU.add),
            [R(ccr, CCN, CCN + NG * P), R(kvf), smres("bsel")], [R(kvf)], tag="b0b")
        dve("tensor_scalar", (ksg[:], ccr[:, NG * P:CCN], smcol("bsel", 0), None, ALU.mult),
            [R(ccr, NG * P, CCN), smres("bsel")], [R(ksg)], tag="b0c")
        dve("scalar_tensor_tensor", (ksg[:], ccr[:, CCN + NG * P:2 * CCN], smcol("bsel", 1),
                                     ksg[:], ALU.mult, ALU.add),
            [R(ccr, CCN + NG * P, 2 * CCN), R(ksg), smres("bsel")], [R(ksg)], tag="b0d")
        dve("memset", (kvb[:], 0.0), [], [R(kvb)], tag="b0z")
        for g in range(NG):
            dve("tensor_copy", (kvb[0:64, g * P:g * P + 64], kvf[0:64, g * P:g * P + 64]),
                [R(kvf, g * P, g * P + 64, 0, 64)],
                [R(kvb, g * P, g * P + 64, 0, 64)], tag="b0d1")
            dve("tensor_copy", (kvb[64:128, g * P + 64:(g + 1) * P],
                                kvf[64:128, g * P + 64:(g + 1) * P]),
                [R(kvf, g * P + 64, (g + 1) * P, 64, 128)],
                [R(kvb, g * P + 64, (g + 1) * P, 64, 128)], tag="b0d2")
        for g in range(NG):
            dve("tensor_scalar", (kml[:, g * NH:(g + 1) * NH],
                                  cst["zmask"][:, g * NH:(g + 1) * NH],
                                  ksg[:, g:g + 1], None, ALU.mult),
                [R(cst["zmask"], g * NH, (g + 1) * NH), R(ksg, g, g + 1)],
                [R(kml, g * NH, (g + 1) * NH)], tag="b0k")
        bld.free(ccr)
        bld.free(kvf)

        outw_b = bld.alloc("outw_b", [P, NG * C], BF16)
        wload(env["w_out"], outw_b, "w_out")

    if "B1" in phases:
        y_h = bld.alloc("y_h", [P, NG * TP], F32)
        qs2 = [bld.alloc(f"qs_sub{i}", [P, NG * 512], BF16) for i in range(2)]
        ars2 = [bld.alloc(f"ars_sub{i}", [P, NG * 512], BF16) for i in range(2)]
        lps2 = [bld.alloc(f"lps_sub{i}", [P, NG * 512], BF16) for i in range(2)]
        as_sub = bld.alloc("as_sub", [P, NG * 512], F32)
        qr2 = [bld.alloc(f"qr_sub{i}", [P, NG * 512], BF16) for i in range(2)]
        at2 = [bld.alloc(f"at_sub{i}", [P, NG * 512], BF16) for i in range(2)]
        zsb = bld.alloc("zsb", [NH, 512], F32)
        zbb = bld.alloc("zbb", [P, 2 * 512], F32)
        t_f = bld.alloc("t_f", [P, 2 * 512], F32)

        for sidx, (c0, w) in enumerate(SUBS):
            qs_sub = qs2[sidx % 2]
            ars_sub = ars2[sidx % 2]
            lps_sub = lps2[sidx % 2]
            qr_sub = qr2[sidx % 2]
            at_sub = at2[sidx % 2]
            dma(st_io, qs_sub[:].rearrange("p (g m) -> p g m", g=NG)[:, :, 0:w],
                q_sp.ap().rearrange("p (g m) -> p g m", g=NG)[:, :, c0:c0 + w],
                [RD(q_sp)], [R(qs_sub)], "qld")
            dma(st_io, ars_sub[:].rearrange("p (g m) -> p g m", g=NG)[:, :, 0:w],
                ar_sp.ap().rearrange("p (g m) -> p g m", g=NG)[:, :, c0:c0 + w],
                [RD(ar_sp)], [R(ars_sub)], "arld")
            dma(st_io, lps_sub[:].rearrange("p (g m) -> p g m", g=NG)[:, :, 0:w],
                lp_sp.ap().rearrange("p (g m) -> p g m", g=NG)[:, :, c0:c0 + w],
                [RD(lp_sp)], [R(lps_sub)], "lpld")
            dma(st_io, as_sub[:].rearrange("p (g m) -> p g m", g=NG)[:, :, 0:w],
                a_sp.ap().rearrange("p (g m) -> p g m", g=NG)[:, :, c0:c0 + w],
                [RD(a_sp)], [R(as_sub)], "ald")
            for g in range(NG):
                bank = ("pmm0", "pmm1")[g % 2]
                bld.op("pe", lambda e, o=psum[bank][:, 0:w],
                       l=cst["rope"][:, g * P:(g + 1) * P],
                       r=qs_sub[:, g * 512:g * 512 + w]:
                       e.matmul(o, l, r, start=True, stop=True),
                       reads=[R(cst["rope"], g * P, (g + 1) * P),
                              R(qs_sub, g * 512, g * 512 + w)],
                       writes=[RP(bank)], tag="qrmm")
                dve("tensor_copy", (qr_sub[:, g * 512:g * 512 + w], psum[bank][:, 0:w]),
                    [RP(bank)], [R(qr_sub, g * 512, g * 512 + w)], tag="qrev")
            for g in range(NG):
                bld.op("pe", lambda e, o=psum["pst"][0:NH, 0:w],
                       l=kml[:, g * NH:(g + 1) * NH], r=qs_sub[:, g * 512:g * 512 + w],
                       st=(g == 0), sp=(g == NG - 1): e.matmul(o, l, r, start=st, stop=sp),
                       reads=[R(kml, g * NH, (g + 1) * NH),
                              R(qs_sub, g * 512, g * 512 + w)] +
                             ([] if g == 0 else [RP("pst")]),
                       writes=[RP("pst")], tag="zmm")
            dve("tensor_scalar", (zsb[:, 0:w], psum["pst"][0:NH, 0:w], 1e-6, None, ALU.add),
                [RP("pst")], [R(zsb, 0, w)], tag="zadd")
            dve("reciprocal", (zsb[:, 0:w], zsb[:, 0:w]),
                [R(zsb, 0, w)], [R(zsb, 0, w)], tag="zrec")
            for g in range(NG):
                zb_bank = ("pkv0", "pkv1")[g % 2]
                at_bank = ("pmm2", "pst")[g % 2]
                bld.op("pe", lambda e, o=psum[zb_bank][:, 0:w],
                       l=cst["zsel"][:, g * P:(g + 1) * P], r=zsb[0:NH, 0:w]:
                       e.matmul(o, l, r, start=True, stop=True),
                       reads=[R(cst["zsel"], g * P, (g + 1) * P), R(zsb, 0, w)],
                       writes=[RP(zb_bank)], tag="zbmm")
                zo = (g % 2) * 512
                dve("tensor_copy", (zbb[:, zo:zo + w], psum[zb_bank][:, 0:w]),
                    [RP(zb_bank)], [R(zbb, zo, zo + w)], tag="zbev")
                bld.op("pe", lambda e, o=psum[at_bank][:, 0:w],
                       l=kvb[:, g * P:(g + 1) * P], r=qr_sub[:, g * 512:g * 512 + w]:
                       e.matmul(o, l, r, start=True, stop=True),
                       reads=[R(kvb, g * P, (g + 1) * P),
                              R(qr_sub, g * 512, g * 512 + w)],
                       writes=[RP(at_bank)], tag="atmm")
                dve("tensor_tensor", (t_f[:, zo:zo + w], psum[at_bank][:, 0:w],
                                      zbb[:, zo:zo + w], ALU.mult),
                    [RP(at_bank), R(zbb, zo, zo + w)], [R(t_f, zo, zo + w)], tag="atz")
                dve("tensor_tensor", (t_f[:, zo:zo + w], t_f[:, zo:zo + w],
                                      lps_sub[:, g * 512:g * 512 + w], ALU.add),
                    [R(t_f, zo, zo + w), R(lps_sub, g * 512, g * 512 + w)],
                    [R(t_f, zo, zo + w)], tag="atl")
                dve("tensor_tensor", (at_sub[:, g * 512:g * 512 + w], t_f[:, zo:zo + w],
                                      ars_sub[:, g * 512:g * 512 + w], ALU.mult),
                    [R(t_f, zo, zo + w), R(ars_sub, g * 512, g * 512 + w)],
                    [R(at_sub, g * 512, g * 512 + w)], tag="atar")
            for oc in range(NG):
                bank = ("pmm0", "pmm1")[oc % 2]
                for k in range(NG):
                    bld.op("pe", lambda e, o=psum[bank][:, 0:w],
                           l=outw_b[:, k * C + oc * P:k * C + (oc + 1) * P],
                           r=at_sub[:, k * 512:k * 512 + w],
                           st=(k == 0), sp=(k == NG - 1): e.matmul(o, l, r, start=st, stop=sp),
                           reads=[R(outw_b, k * C + oc * P, k * C + (oc + 1) * P),
                                  R(at_sub, k * 512, k * 512 + w)] +
                                 ([] if k == 0 else [RP(bank)]),
                           writes=[RP(bank)], tag="ywmm")
                yo = (oc % 2) * 512
                act(t_f[:, yo:yo + w], psum[bank][:, 0:w], AF.Identity,
                    [RP(bank), smres("out_b")], [R(t_f, yo, yo + w)],
                    bias=smcol("out_b", oc), tag="yev")
                dve("tensor_tensor", (y_h[:, oc * TP + c0:oc * TP + c0 + w],
                                      t_f[:, yo:yo + w],
                                      as_sub[:, oc * 512:oc * 512 + w], ALU.add),
                    [R(t_f, yo, yo + w), R(as_sub, oc * 512, oc * 512 + w)],
                    [R(y_h, oc * TP + c0, oc * TP + c0 + w)], tag="ya")
        for _h in qs2 + ars2 + lps2 + qr2 + at2 + [as_sub, outw_b, kvb, kml, ksg,
                                                    zsb, zbb, t_f]:
            bld.free(_h)

    if "B2" in phases:
        sq2[0] = sq2[1] = bld.alloc("sqs0b", [P, NG * 512], F32)
        rows2[0] = rows2[1] = bld.alloc("rows0b", [1, 4 * 512], F32)
        rbmb2[0] = bld.alloc("rbmb0b", [P, 2 * 512], F32)
        rbmb2[1] = bld.alloc("rbmb1b", [P, 2 * 512], F32)
        cvt = bld.alloc("cvtb", [P, 2 * 512], F32)
        mask_halo(y_h, "y")
        xn2b = bld.alloc("xn2b", [P, NG * T], BF16, top=True)
        x2_sub = bld.alloc("x2_sub", [P, NG * 512], F32)
        def x2_emit(g, c0, w, t_ap, deps):
            o = x2_sub[:, g * 512:g * 512 + w]
            dve("tensor_tensor", (o, t_ap, y_h[:, g * TP + c0:g * TP + c0 + w], ALU.add),
                deps + [R(y_h, g * TP + c0, g * TP + c0 + w)],
                [R(x2_sub, g * 512, g * 512 + w)], tag="x2")
        for (c0, w) in RSUBS:
            conv_pass(y_h, x2_emit, "cpe2_w", "cpe2_b", [(c0, w)], cvt, tag="cv4")
            dma(st_io, x2_sp.ap().rearrange("p (g m) -> p g m", g=NG)[:, :, c0 - R0:c0 - R0 + w],
                x2_sub[:].rearrange("p (g m) -> p g m", g=NG)[:, :, 0:w],
                [R(x2_sub)], [RD(x2_sp)], "x2sp")
            def srcb(g, cc0, ww):
                return (x2_sub[:, g * 512:g * 512 + ww], R(x2_sub, g * 512, g * 512 + ww))
            def wrb(g, cc0, ww, rb, mb, deps, _c0=c0):
                t = cvt[:, 0:ww]
                tres = R(cvt, 0, ww)
                dve("tensor_tensor", (t, x2_sub[:, g * 512:g * 512 + ww], rb, ALU.mult),
                    [R(x2_sub, g * 512, g * 512 + ww)] + deps, [tres], tag="ln2a")
                dve("tensor_tensor", (t, t, mb, ALU.add), [tres] + deps, [tres], tag="ln2b")
                o = xn2b[:, g * T + _c0 - R0:g * T + _c0 - R0 + ww]
                dve("tensor_scalar", (o, t, smcol("g2", g), smcol("b2", g), ALU.mult, ALU.add),
                    [tres, smres("g2"), smres("b2")],
                    [R(xn2b, g * T + _c0 - R0, g * T + _c0 - R0 + ww)], tag="ln2c")
            ln_pass(x2_sub, srcb, [(c0, w)], wrb, tag="ln2")
        bld.free(y_h)
        bld.free(x2_sub)
        bld.free(cvt)
        for _h in {h.name: h for h in sq2 + rows2 + rbmb2}.values():
            bld.free(_h)

    if "B3" in phases:
        g_full = bld.alloc("g_full", [P, NGH * T], BF16)
        f1r = [bld.alloc(f"f1sl{i}", [P, NG * P], BF16) for i in range(4)]
        w_fc1 = env["w_fc1"]
        banks4b = ("pmm0", "pmm1", "pmm2", "pst", "pkv0", "pkv1")
        for oc in range(NGH):
            slb = f1r[oc % 4]
            dma(st_w, slb[:], w_fc1.ap()[oc], [RD(w_fc1)], [R(slb)], f"fc1s{oc}")
            for k in range(NG):
                for si, (c0, w) in enumerate(RSUBS):
                    bank = banks4b[(oc * 4 + si) % 6]
                    bld.op("pe", lambda e, o=psum[bank][:, 0:w],
                           l=slb[:, k * P:(k + 1) * P],
                           r=xn2b[:, k * T + (c0 - R0):k * T + (c0 - R0) + w],
                           st=(k == 0), sp=(k == NG - 1): e.matmul(o, l, r, start=st, stop=sp),
                           reads=[R(slb, k * P, (k + 1) * P),
                                  R(xn2b, k * T + (c0 - R0), k * T + (c0 - R0) + w)] +
                                 ([] if k == 0 else [RP(bank)]),
                           writes=[RP(bank)], tag="fc1mm")
            for si, (c0, w) in enumerate(RSUBS):
                bank = banks4b[(oc * 4 + si) % 6]
                act(g_full[:, oc * T + (c0 - R0):oc * T + (c0 - R0) + w],
                    psum[bank][:, 0:w], AF.Gelu,
                    [RP(bank), smres("fc1_b")],
                    [R(g_full, oc * T + (c0 - R0), oc * T + (c0 - R0) + w)],
                    bias=smcol("fc1_b", oc), tag="fc1g")
        for _h in f1r:
            bld.free(_h)
        bld.free(xn2b)

    if "B4" in phases:
        f2r = [bld.alloc(f"f2sl{i}", [P, NGH * P], BF16) for i in range(4)]
        x2r = bld.alloc("x2r", [P, 2 * 512], F32)
        po_r = bld.alloc("po_r", [P, 2 * 512], F32)
        o_sub = bld.alloc("o_sub", [P, 2 * 512], F32)
        w_fc2 = env["w_fc2"]
        banks4c = ("pmm0", "pmm1", "pmm2", "pst", "pkv0", "pkv1")
        for oc in range(NG):
            slb = f2r[oc % 4]
            dma(st_w, slb[:], w_fc2.ap()[oc], [RD(w_fc2)], [R(slb)], f"fc2s{oc}")
            for k in range(NGH):
                for si, (c0, w) in enumerate(RSUBS):
                    bank = banks4c[(oc * 4 + si) % 6]
                    bld.op("pe", lambda e, o=psum[bank][:, 0:w],
                           l=slb[:, k * P:(k + 1) * P],
                           r=g_full[:, k * T + (c0 - R0):k * T + (c0 - R0) + w],
                           st=(k == 0), sp=(k == NGH - 1): e.matmul(o, l, r, start=st, stop=sp),
                           reads=[R(slb, k * P, (k + 1) * P),
                                  R(g_full, k * T + (c0 - R0), k * T + (c0 - R0) + w)] +
                                 ([] if k == 0 else [RP(bank)]),
                           writes=[RP(bank)], tag="fc2mm")
            for si, (c0, w) in enumerate(RSUBS):
                bank = banks4c[(oc * 4 + si) % 6]
                rc = oc * T + (c0 - R0)
                po = ((oc * 4 + si) % 2) * 512
                dma(st_io, x2r[:, po:po + w], x2_sp.ap()[:, rc:rc + w],
                    [RD(x2_sp)], [R(x2r, po, po + w)], "x2rl")
                dma(st_io, po_r[:, po:po + w], out_t.ap()[:, rc:rc + w],
                    [RD(out_t)], [R(po_r, po, po + w)], "porl")
                act(o_sub[:, po:po + w], psum[bank][:, 0:w], AF.Identity,
                    [RP(bank), smres("fc2_b")], [R(o_sub, po, po + w)],
                    bias=smcol("fc2_b", oc), tag="f2ev")
                dve("tensor_tensor", (o_sub[:, po:po + w], o_sub[:, po:po + w],
                                      x2r[:, po:po + w], ALU.add),
                    [R(o_sub, po, po + w), R(x2r, po, po + w)], [R(o_sub, po, po + w)],
                    tag="f2a1")
                dve("tensor_tensor", (o_sub[:, po:po + w], o_sub[:, po:po + w],
                                      po_r[:, po:po + w], ALU.add),
                    [R(o_sub, po, po + w), R(po_r, po, po + w)], [R(o_sub, po, po + w)],
                    tag="f2a2")
                dma(st_io, out_t.ap()[:, rc:rc + w], o_sub[:, po:po + w],
                    [R(o_sub, po, po + w)], [RD(out_t)], "ost")



def _tile_rows(a, ng):
    f = a.shape[1]
    return np.ascontiguousarray(a.reshape(ng, P, f).transpose(1, 0, 2).reshape(P, ng * f))


def _oc_slices(w_oi, n_oc, n_k):
    O, I = w_oi.shape
    assert O == n_oc * P and I == n_k * P
    a = w_oi.reshape(n_oc, P, n_k, P).transpose(0, 3, 2, 1)
    return np.ascontiguousarray(a.reshape(n_oc, P, n_k * P))


def _oc_slices_kf(w_kf, n_oc, n_k):
    K, F = w_kf.shape
    assert K == n_k * P and F == n_oc * P
    a = w_kf.reshape(n_k, P, n_oc, P).transpose(2, 1, 0, 3)
    return np.ascontiguousarray(a.reshape(n_oc, P, n_k * P))


def make_smalls(params, core):
    chunk = core % 4
    b = core // 4
    sm = np.zeros((P, SMALLS_COLS), np.float32)

    def put(name, arr):
        off, n = SMALLS[name]
        sm[:, off:off + n] = arr

    def chunks_vec(v):
        return np.ascontiguousarray(v.reshape(NG, P).T)

    put("g1", chunks_vec(params["norm1_g"]))
    put("b1", chunks_vec(params["norm1_b"]))
    put("g2", chunks_vec(params["norm2_g"]))
    put("b2", chunks_vec(params["norm2_b"]))
    for cv, wname, bname in [("cpe1", "cpe1_w", "cpe1_b"), ("dwc", "dwc_w", "dwc_b"),
                             ("lepe", "lepe_w", "lepe_b"), ("cpe2", "cpe2_w", "cpe2_b")]:
        w3 = params[wname].reshape(C, 3).reshape(NG, P, 3).transpose(1, 0, 2).reshape(P, NG * 3)
        put(cv + "_w", w3)
        put(cv + "_b", chunks_vec(params[bname]))
    put("in_b", chunks_vec(params["in_b"]))
    put("actp_b", chunks_vec(params["actp_b"]))
    put("qk_b", np.ascontiguousarray(params["qk_b"].reshape(2 * NG, P).T))
    put("qk_nb", np.ascontiguousarray(-params["qk_b"].reshape(2 * NG, P).T))
    put("out_b", chunks_vec(params["out_b"]))
    put("fc1_b", np.ascontiguousarray(params["fc1_b"].reshape(NGH, P).T))
    put("fc2_b", chunks_vec(params["fc2_b"]))
    put("w1_s", np.ascontiguousarray(params["w1_s"].reshape(NGH, P).T))
    put("w2_s", np.ascontiguousarray(params["w2_s"].reshape(NGH, P).T))
    put("h1_s", np.ascontiguousarray(params["h1_s"].reshape(NGH, P).T))
    put("h2_s", np.ascontiguousarray(params["h2_s"].reshape(NGH, P).T))
    put("w3_s", chunks_vec(params["w3_s"]))
    put("lmask", np.full((P, HALO), 0.0 if chunk == 0 else 1.0, np.float32))
    put("rmask", np.full((P, HALO), 0.0 if chunk == 3 else 1.0, np.float32))
    bs = np.zeros((P, 2), np.float32)
    bs[:, b] = 1.0
    put("bsel", bs)
    return sm


def make_in_maps(inputs):
    x = inputs["L0_attn_x"]
    pr = {k[len("L0_attn_"):]: np.asarray(v) for k, v in inputs.items()
          if k.startswith("L0_attn_")}
    fr = {k[len("L0_ffn_"):]: np.asarray(v) for k, v in inputs.items()
          if k.startswith("L0_ffn_")}
    pr.update({k: v for k, v in fr.items()})
    params = pr

    w_shared = dict(
        w_in=_tile_rows(params["in_w"], NG),
        w_actp=_tile_rows(params["actp_w"], NG),
        w_qk=_tile_rows(params["qk_w"], NG),
        w_out=_tile_rows(params["out_w"], NG),
        w_fc1=_oc_slices_kf(params["fc1_w"], NGH, NG),
        w_fc2=_oc_slices_kf(params["fc2_w"], NG, NGH),
        w_w1=_oc_slices(params["w1_w"], NGH, NG),
        w_w2=_oc_slices(params["w2_w"], NGH, NG),
        w_h1=_oc_slices(params["h1_w"], NGH, NGH),
        w_h2=_oc_slices(params["h2_w"], NGH, NGH),
        w_w3=_oc_slices(params["w3_w"], NG, NGH),
    )

    in_maps = []
    for core in range(8):
        b, chunk = core // 4, core % 4
        t0 = chunk * T
        lo, hi = t0 - HALO, t0 + T + HALO
        seg = np.zeros((TP, C), np.float32)
        s0, s1 = max(lo, 0), min(hi, 8192)
        seg[s0 - lo:s1 - lo, :] = x[b, s0:s1, :]
        u_fm = np.ascontiguousarray(seg.T)
        u_t = _tile_rows(u_fm, NG)
        m = dict(u_t=u_t, smalls=make_smalls(params, core), **w_shared)
        in_maps.append(m)
    return in_maps


_CACHE = {}


def kernel(**inputs):
    key = "full"
    if key not in _CACHE:
        _CACHE[key] = build_program()
    nc, meta = _CACHE[key]
    in_maps = make_in_maps(inputs)
    res = run_bass_kernel_spmd(nc, in_maps, core_ids=list(range(8)))
    out = np.empty((2, 8192, 1024), np.float32)
    for core in range(8):
        b, chunk = core // 4, core % 4
        o = res.results[core]["out_t"]
        o = o.reshape(P, NG, T).transpose(1, 0, 2).reshape(C, T)
        out[b, chunk * T:(chunk + 1) * T, :] = o.T
    return out


# version 11
# speedup vs baseline: 1.1911x; 1.0293x over previous
import numpy as np
import ml_dtypes
from contextlib import ExitStack

import concourse.bass as bass
import concourse.mybir as mybir
from concourse.bass_utils import run_bass_kernel_spmd

F32 = mybir.dt.float32
BF16 = mybir.dt.bfloat16
AF = mybir.ActivationFunctionType
ALU = mybir.AluOpType
AX = mybir.AxisListType

P = 128
C = 1024
NG = 8
NH = 16
HD = 64
T = 2048
HALO = 4
TP = T + 2 * HALO
R0, R1 = HALO, HALO + T
HID = 4096
NGH = 32

SUBS = [(0, 512), (512, 512), (1024, 512), (1536, 512), (2048, 8)]
RSUBS = [(4, 512), (516, 512), (1028, 512), (1540, 512)]

BF = ml_dtypes.bfloat16


def _rope_lhst():
    k_max = C // 2
    theta = 1.0 / 10000.0 ** (np.arange(k_max, dtype=np.float64) / k_max)
    cs, sn = np.cos(theta), np.sin(theta)
    out = np.zeros((NG, P, P), np.float32)
    for g in range(NG):
        for jl in range(64):
            j = 64 * g + jl
            out[g, 2 * jl, 2 * jl] = cs[j]
            out[g, 2 * jl + 1, 2 * jl] = -sn[j]
            out[g, 2 * jl, 2 * jl + 1] = sn[j]
            out[g, 2 * jl + 1, 2 * jl + 1] = cs[j]
    return out.astype(BF)


def _zmask():
    out = np.zeros((NG, P, NH), np.float32)
    for g in range(NG):
        for p in range(P):
            out[g, p, (128 * g + p) // HD] = 1.0 / 8192.0
    return out.astype(BF)


def _zsel():
    out = np.zeros((NG, NH, P), np.float32)
    for g in range(NG):
        for p in range(P):
            out[g, (128 * g + p) // HD, p] = 1.0
    return out



ENGS = ("pe", "act", "dve", "pool", "sp")


class Op:
    __slots__ = ("idx", "eng", "kind", "emit", "reads", "writes", "token",
                 "need_inc", "deps", "tag")

    def __init__(self, idx, eng, kind, emit, reads, writes, tag=""):
        self.idx = idx
        self.eng = eng
        self.kind = kind
        self.emit = emit
        self.reads = reads
        self.writes = writes
        self.token = None
        self.need_inc = False
        self.deps = []
        self.tag = tag


class Stream:

    def __init__(self, bld, name, eng, n_sems=4):
        self.bld = bld
        self.name = name
        self.eng = eng
        self.n_sems = n_sems
        self.sems = [f"{name}{i}" for i in range(n_sems)]
        self.counts = [0] * n_sems
        self.wave_i = 0
        self.prev_wave_final = {}

    def wave(self, dmas):
        si = self.wave_i % self.n_sems
        sem = self.sems[si]
        if self.wave_i >= self.n_sems:
            old = self.prev_wave_final[si]
            self.bld.raw_wait(self.eng, sem, old)
        for emit, reads, writes in dmas:
            self.counts[si] += 16
            val = self.counts[si]
            op = self.bld.op(self.eng, emit, reads, writes, kind="d",
                             tag=f"{self.name}w{self.wave_i}")
            op.token = (sem, val)
        self.prev_wave_final[si] = self.counts[si]
        tok = (sem, self.counts[si])
        self.wave_i += 1
        return tok


class Builder:
    def __init__(self, nc, sim_sync=False, hw_same_engine_sync=True):
        self.nc = nc
        self.sim_sync = sim_sync
        self.hw_same_engine_sync = hw_same_engine_sync
        self.ops = []
        self.writers = []
        self.readers = []
        self.sem_names = set()
        self.streams = {}
        self.raw_waits = {}
        self.ar_base = 16416
        self.ar_top = 229344
        self.free_list = [(self.ar_base, self.ar_top)]
        self.tensors = {}
        self.sb_handles = {}

    def alloc(self, name, shape, dtype, align=64, top=None):
        size = int(np.prod(shape[1:])) * mybir.dt.size(dtype)
        size_al = -(-size // align) * align
        from_top = (size_al < 20480) if top is None else top
        blocks = list(enumerate(self.free_list))
        if from_top:
            blocks = reversed(blocks)
        for i, (a, b) in blocks:
            a_al = -(-a // align) * align
            if b - a_al >= size_al:
                if from_top:
                    addr = (b - size_al) // align * align
                    if addr < a:
                        continue
                else:
                    addr = a_al
                pieces = []
                if a < addr:
                    pieces.append((a, addr))
                if addr + size_al < b:
                    pieces.append((addr + size_al, b))
                self.free_list[i:i + 1] = pieces
                h = self.nc.alloc_sbuf_tensor_at(name, list(shape), dtype, offset=addr)
                self.tensors[h.name] = (h, addr, size_al)
                return h
        live = sorted((a, a + s, n) for n, (_, a, s) in self.tensors.items())
        raise MemoryError(f"SBUF arena full allocating {name} ({size_al}B/part); "
                          f"free={self.free_list}; live={live}")

    def free(self, h):
        name = h.name
        _, addr, size = self.tensors.pop(name)
        self.free_list.append((addr, addr + size))
        self.free_list.sort()
        merged = []
        for a, b in self.free_list:
            if merged and merged[-1][1] >= a:
                merged[-1] = (merged[-1][0], max(merged[-1][1], b))
            else:
                merged.append((a, b))
        self.free_list = [tuple(x) for x in merged]

    def rs(self, h, c0=None, c1=None, p0=0, p1=None):
        name = h.name if hasattr(h, "name") else h
        hh, addr, _ = self.tensors[name]
        sz = mybir.dt.size(hh.dtype)
        ncols = hh.shape[1]
        if c0 is None:
            c0, c1 = 0, ncols
        if p1 is None:
            p1 = hh.shape[0]
        return ("S", p0, p1, addr + c0 * sz, addr + c1 * sz)

    @staticmethod
    def rp(bank_name):
        return ("P", bank_name)

    @staticmethod
    def rd(dram_handle):
        return ("D", dram_handle.name if hasattr(dram_handle, "name") else dram_handle)

    @staticmethod
    def _overlap(r1, r2):
        if r1[0] != r2[0]:
            return False
        if r1[0] == "S":
            return not (r1[2] <= r2[1] or r2[2] <= r1[1] or
                        r1[4] <= r2[3] or r2[4] <= r1[3])
        return r1[1] == r2[1]

    def op(self, eng, emit, reads=(), writes=(), kind="c", tag=""):
        o = Op(len(self.ops), eng, kind, emit, list(reads), list(writes), tag)
        deps = {}
        for r in o.reads:
            for res, w in self.writers:
                if self._overlap(res, r):
                    deps[w.idx] = w
        for w_ in o.writes:
            for res, w in self.writers:
                if self._overlap(res, w_):
                    deps[w.idx] = w
            for res, rdr in self.readers:
                if self._overlap(res, w_):
                    deps[rdr.idx] = rdr
        for d in deps.values():
            if d is o:
                continue
            if d.eng == o.eng and d.kind == "c" and o.kind == "c":
                if not self.sim_sync:
                    if o.eng == "pe" or not self.hw_same_engine_sync:
                        continue
            o.deps.append(d)
        for w_ in o.writes:
            self.writers = [(res, w) for (res, w) in self.writers
                            if not self._covered(res, w_)]
            self.readers = [(res, r) for (res, r) in self.readers
                            if not self._covered(res, w_)]
            self.writers.append((w_, o))
        for r in o.reads:
            self.readers.append((r, o))
        self.ops.append(o)
        return o

    @staticmethod
    def _covered(res, w):
        if res[0] != w[0]:
            return False
        if res[0] == "S":
            return (w[1] <= res[1] and res[2] <= w[2] and
                    w[3] <= res[3] and res[4] <= w[4])
        return res[1] == w[1]

    def raw_wait(self, eng, sem, val):
        o = Op(len(self.ops), eng, "w", None, [], [], "gate")
        o.token = (sem, val)
        self.ops.append(o)

    def stream(self, name, eng, n_sems=4):
        s = Stream(self, name, eng, n_sems)
        self.streams[name] = s
        return s

    def finalize(self, eng_objs, final_eng="sp"):
        nc = self.nc
        for o in self.ops:
            for d in o.deps:
                if d.token is None:
                    d.need_inc = True
        eng_counts = {e: 0 for e in ENGS}
        for o in self.ops:
            if o.kind == "c" and o.need_inc:
                eng_counts[o.eng] += 1
                o.token = (f"eng_{o.eng}", eng_counts[o.eng])
        sems = {}
        for o in self.ops:
            if o.token is not None:
                sems.setdefault(o.token[0], 0)
                sems[o.token[0]] = max(sems[o.token[0]], o.token[1])
        sched = {e: [] for e in ENGS}
        waited = {e: {} for e in ENGS}
        for o in self.ops:
            if o.kind == "w":
                sem, val = o.token
                if waited[o.eng].get(sem, 0) < val:
                    sched[o.eng].append(("wait", sem, val))
                    waited[o.eng][sem] = val
                continue
            for d in o.deps:
                sem, val = d.token
                if waited[o.eng].get(sem, 0) < val:
                    sched[o.eng].append(("wait", sem, val))
                    waited[o.eng][sem] = val
            sched[o.eng].append(("inst", o))
        final_waits = []
        for sem, top in sems.items():
            if waited[final_eng].get(sem, 0) < top:
                final_waits.append((sem, top))
        sem_handles = {}
        with ExitStack() as ctx:
            for sem in sems:
                sem_handles[sem] = ctx.enter_context(nc.semaphore(sem))
            blk = ctx.enter_context(nc.Block())

            def run_engine(ename):
                def body(eobj):
                    for item in sched[ename]:
                        if item[0] == "wait":
                            eobj.wait_ge(sem_handles[item[1]], item[2])
                        else:
                            o = item[1]
                            inst = o.emit(eobj)
                            if o.token is not None:
                                sem, val = o.token
                                prev = getattr(o, "_prev_val", None)
                                amt = 16 if o.kind == "d" else 1
                                inst.then_inc(sem_handles[sem], amt)
                    if ename == final_eng:
                        for sem, top in final_waits:
                            eobj.wait_ge(sem_handles[sem], top)
                return body

            blk.tensor(run_engine("pe"))
            blk.scalar(run_engine("act"))
            blk.vector(run_engine("dve"))
            blk.gpsimd(run_engine("pool"))
            blk.sync(run_engine("sp"))
        return dict(n_ops=len(self.ops), sems=len(sems))



SMALLS = {}


def _smalls_layout():
    cols = 0
    spec = {}
    def add(name, n):
        nonlocal cols
        spec[name] = (cols, n)
        cols += n
    add("g1", NG); add("b1", NG); add("g2", NG); add("b2", NG)
    for cv in ("cpe1", "dwc", "lepe", "cpe2"):
        add(cv + "_w", 3 * NG)
        add(cv + "_b", NG)
    add("in_b", NG); add("actp_b", NG); add("qk_b", 2 * NG); add("qk_nb", 2 * NG); add("out_b", NG)
    add("fc1_b", NGH); add("fc2_b", NG)
    add("w1_s", NGH); add("w2_s", NGH); add("h1_s", NGH); add("h2_s", NGH)
    add("w3_s", NG)
    add("lmask", HALO); add("rmask", HALO)
    add("bsel", 2)
    return spec, cols


SMALLS, SMALLS_COLS = _smalls_layout()


def build_program(upto="B4", fake_cc=False, sim_sync=False, hw_same_engine_sync=True):
    nc = bass.Bass("TRN2", target_bir_lowering=False)
    bld = Builder(nc, sim_sync=sim_sync, hw_same_engine_sync=hw_same_engine_sync)
    phases_all = ["PH1", "PH2", "PH3", "PH4", "PH5", "PH6", "PH7", "PH8",
                  "PH9", "F1", "F2", "F3", "F4", "B0", "B1", "B2", "B3", "B4"]
    phases = phases_all[:phases_all.index(upto) + 1]

    dt_in = {}
    def din(name, shape, dtype=F32):
        dt_in[name] = nc.dram_tensor(name, list(shape), dtype, kind="ExternalInput")
        return dt_in[name]

    u_t = din("u_t", [P, NG * TP])
    smalls = din("smalls", [P, SMALLS_COLS])
    w_in = din("w_in", [P, NG * C])
    w_actp = din("w_actp", [P, NG * C])
    w_qk = din("w_qk", [P, NG * 2 * C])
    w_out = din("w_out", [P, NG * C])
    w_fc1 = din("w_fc1", [NGH, P, NG * P])
    w_fc2 = din("w_fc2", [NG, P, NGH * P])
    w_w1 = din("w_w1", [NGH, P, NG * P])
    w_w2 = din("w_w2", [NGH, P, NG * P])
    w_h1 = din("w_h1", [NGH, P, NGH * P])
    w_h2 = din("w_h2", [NGH, P, NGH * P])
    w_w3 = din("w_w3", [NG, P, NGH * P])

    out_t = nc.dram_tensor("out_t", [P, NG * T], F32, kind="ExternalOutput")

    a_sp = nc.dram_tensor("a_sp", [P, NG * TP], F32, kind="Internal")
    q_sp = nc.dram_tensor("q_sp", [P, NG * TP], BF16, kind="Internal")
    ar_sp = nc.dram_tensor("ar_sp", [P, NG * TP], BF16, kind="Internal")
    lp_sp = nc.dram_tensor("lp_sp", [P, NG * TP], BF16, kind="Internal")
    x2_sp = nc.dram_tensor("x2_sp", [P, NG * T], F32, kind="Internal")
    h0_sp = nc.dram_tensor("h0_sp", [P, NGH * T], BF16, kind="Internal")
    h1_sp = nc.dram_tensor("h1_sp", [P, NGH * T], BF16, kind="Internal")
    h2_sp = nc.dram_tensor("h2_sp", [P, NGH * T], BF16, kind="Internal")
    CCN = NG * P + NG
    cc_in = nc.dram_tensor("cc_in", [2, P * CCN], F32, kind="Internal")
    cc_out = nc.dram_tensor("cc_out", [2, P * CCN], F32, kind="Internal")

    rope_d = nc.inline_tensor(np.ascontiguousarray(_rope_lhst().reshape(NG * P, P)), "rope_d")
    zmask_d = nc.inline_tensor(np.ascontiguousarray(_zmask().reshape(NG * P, NH)), "zmask_d")
    zsel_d = nc.inline_tensor(np.ascontiguousarray(_zsel().reshape(NG * NH, P)), "zsel_d")
    ident_d = nc.inline_tensor(np.eye(P, dtype=BF), "ident_d")
    onesc_d = nc.inline_tensor(np.ones((P, 1), np.float32), "onesc_d")
    onesr_d = nc.inline_tensor(np.ones((1, P), np.float32), "onesr_d")

    psum = {}
    with ExitStack() as psctx:
        for nm, dt in [("pmm0", F32), ("pmm1", F32), ("pmm2", F32), ("pst", F32),
                       ("pkv0", F32), ("pkv1", F32), ("ptra", BF16), ("ptrb", BF16)]:
            psum[nm] = psctx.enter_context(nc.psum_tensor(nm, [P, 512], dt))
        _build_body(nc, bld, phases, fake_cc, locals())
        meta = bld.finalize(None)
    return nc, meta


def _build_body(nc, bld, phases, fake_cc, env):
    psum = env["psum"]
    u_t, smalls = env["u_t"], env["smalls"]
    out_t = env["out_t"]
    a_sp, q_sp, ar_sp, lp_sp = env["a_sp"], env["q_sp"], env["ar_sp"], env["lp_sp"]
    x2_sp, h1_sp, h2_sp = env["x2_sp"], env["h1_sp"], env["h2_sp"]
    cc_in, cc_out = env["cc_in"], env["cc_out"]
    CCN = env["CCN"]

    st_w = bld.stream("sw", "pool", 8)
    st_io = bld.stream("sio", "sp", 12)
    st_h = bld.stream("sh", "act", 8)
    st_cc = bld.stream("scc", "pool", 2)

    R = bld.rs
    RP = bld.rp
    RD = bld.rd

    cst = {}
    cst["ones_c"] = bld.alloc("ones_c", [P, 1], F32)
    cst["ones_r"] = bld.alloc("ones_r", [1, P], F32)
    cst["ident"] = bld.alloc("ident", [P, P], BF16)
    cst["rope"] = bld.alloc("rope", [P, NG * P], BF16)
    cst["zmask"] = bld.alloc("zmask", [P, NG * NH], BF16)
    cst["zsel"] = bld.alloc("zsel", [NH, NG * P], F32)
    cst["sm"] = bld.alloc("sm", [P, SMALLS_COLS], F32)

    init_dmas = []
    for dname, key, hsl in [("onesc_d", "ones_c", None), ("onesr_d", "ones_r", None),
                            ("ident_d", "ident", None), ("rope_d", "rope", "g"),
                            ("zmask_d", "zmask", "g"), ("zsel_d", "zsel", "g16")]:
        dh = env[dname] if dname in env else None
    def dma(stream, out_ap, in_ap, reads, writes, tag=""):
        return stream.wave([(lambda e, o=out_ap, i=in_ap: e.dma_start(out=o, in_=i),
                             reads, writes)])

    rope_d, zmask_d, zsel_d = env["rope_d"], env["zmask_d"], env["zsel_d"]
    ident_d, onesc_d, onesr_d = env["ident_d"], env["onesc_d"], env["onesr_d"]

    dma(st_io, cst["ones_c"][:], onesc_d.ap(), [RD(onesc_d)], [R(cst["ones_c"])])
    dma(st_io, cst["ones_r"][:], onesr_d.ap(), [RD(onesr_d)], [R(cst["ones_r"])])
    dma(st_io, cst["ident"][:], ident_d.ap(), [RD(ident_d)], [R(cst["ident"])])
    dma(st_io, cst["rope"][:].rearrange("p (g f) -> p g f", g=NG),
        rope_d.ap().rearrange("(g p) f -> p g f", g=NG),
        [RD(rope_d)], [R(cst["rope"])])
    dma(st_io, cst["zmask"][:].rearrange("p (g f) -> p g f", g=NG),
        zmask_d.ap().rearrange("(g p) f -> p g f", g=NG),
        [RD(zmask_d)], [R(cst["zmask"])])
    dma(st_io, cst["zsel"][:].rearrange("p (g f) -> p g f", g=NG),
        zsel_d.ap().rearrange("(g p) f -> p g f", g=NG),
        [RD(zsel_d)], [R(cst["zsel"])])
    dma(st_io, cst["sm"][:], smalls.ap(), [RD(smalls)], [R(cst["sm"])])

    def smcol(name, i=0):
        off, n = SMALLS[name]
        assert i < n
        return cst["sm"][:, off + i:off + i + 1]

    def smres(name):
        off, n = SMALLS[name]
        return R(cst["sm"], off, off + n)

    sm = cst["sm"]

    def mm(bank, bc0, bc1, lhsT_ap, rhs_ap, reads, start, stop, tag=""):
        ps = psum[bank]
        bld.op("pe",
               lambda e, o=ps[:, bc0:bc1], l=lhsT_ap, r=rhs_ap, st=start, sp=stop:
                   e.matmul(o, l, r, start=st, stop=sp),
               reads=reads + [RP(bank)] if not start else reads,
               writes=[RP(bank)], tag=tag)

    def act(out_ap, in_ap, func, reads, writes, bias=None, scale=None, tag=""):
        kw = {}
        if bias is not None:
            kw["bias"] = bias
        if scale is not None:
            kw["scale"] = scale
        bld.op("act", lambda e, o=out_ap, i=in_ap, f=func, k=kw: e.activation(o, i, f, **k),
               reads=reads, writes=writes, tag=tag)

    def dve(meth, args, reads, writes, tag=""):
        bld.op("dve", lambda e, m=meth, a=args: getattr(e, m)(*a),
               reads=reads, writes=writes, tag=tag)

    rows2 = [bld.alloc("rows0", [1, 4 * 512], F32)] * 2
    sq2 = [bld.alloc("sqs0", [P, NG * 512], F32)] * 2
    rbmb2 = [bld.alloc(f"rbmb{i}", [P, 2 * 512], F32) for i in range(2)]
    _ln_pi = [0]

    def ln_pass(src_h, src_base_fn, subs, dst_write, gamma=None, beta=None, tag="ln",
                banks=("pst", "pmm2", "pmm0", "pmm1")):
        bk_sum, bk_sq, bk_rb, bk_mb = banks
        for (c0, w) in subs:
            pi = _ln_pi[0] % 2
            _ln_pi[0] += 1
            rows = rows2[pi]
            sq_sub = sq2[pi]
            rbmb = rbmb2[pi]
            for g in range(NG):
                sap, sres = src_base_fn(g, c0, w)
                if g % 2 == 0:
                    act(sq_sub[:, g * 512:g * 512 + w], sap, AF.Square,
                        [sres], [R(sq_sub, g * 512, g * 512 + w)], tag=f"{tag}sq")
                else:
                    bld.op("pool", lambda e, o=sq_sub[:, g * 512:g * 512 + w], i=sap:
                           e.tensor_tensor(o, i, i, ALU.mult),
                           reads=[sres], writes=[R(sq_sub, g * 512, g * 512 + w)],
                           tag=f"{tag}sqp")
            for g in range(NG):
                sap, sres = src_base_fn(g, c0, w)
                ps = psum[bk_sum]
                bld.op("pe", lambda e, o=ps[0:1, 0:w], l=cst["ones_c"][:], r=sap,
                       st=(g == 0), sp=(g == NG - 1): e.matmul(o, l, r, start=st, stop=sp),
                       reads=[sres, R(cst["ones_c"])] + ([] if g == 0 else [RP(bk_sum)]),
                       writes=[RP(bk_sum)], tag=f"{tag}st1")
            for g in range(NG):
                ps = psum[bk_sq]
                bld.op("pe", lambda e, o=ps[0:1, 0:w], l=cst["ones_c"][:],
                       r=sq_sub[:, g * 512:g * 512 + w],
                       st=(g == 0), sp=(g == NG - 1): e.matmul(o, l, r, start=st, stop=sp),
                       reads=[R(sq_sub, g * 512, g * 512 + w), R(cst["ones_c"])] +
                             ([] if g == 0 else [RP(bk_sq)]),
                       writes=[RP(bk_sq)], tag=f"{tag}st2")
            srow = rows[0:1, 0:w]
            vrow = rows[0:1, 512:512 + w]
            rrow = rows[0:1, 1024:1024 + w]
            nmrow = rows[0:1, 1536:1536 + w]
            dve("tensor_scalar", (srow, psum[bk_sum][0:1, 0:w], 1.0 / C, None, ALU.mult),
                [RP(bk_sum)], [R(rows, 0, w, 0, 1)], tag=f"{tag}r1")
            dve("tensor_scalar", (vrow, psum[bk_sq][0:1, 0:w], 1.0 / C, None, ALU.mult),
                [RP(bk_sq)], [R(rows, 512, 512 + w, 0, 1)], tag=f"{tag}r2")
            dve("tensor_tensor", (nmrow, srow, srow, ALU.mult),
                [R(rows, 0, w, 0, 1)], [R(rows, 1536, 1536 + w, 0, 1)], tag=f"{tag}r3")
            dve("tensor_tensor", (vrow, vrow, nmrow, ALU.subtract),
                [R(rows, 512, 512 + w, 0, 1), R(rows, 1536, 1536 + w, 0, 1)],
                [R(rows, 512, 512 + w, 0, 1)], tag=f"{tag}r4")
            dve("tensor_scalar", (vrow, vrow, 1e-5, None, ALU.add),
                [R(rows, 512, 512 + w, 0, 1)], [R(rows, 512, 512 + w, 0, 1)], tag=f"{tag}r5")
            act(vrow, vrow, AF.Sqrt, [R(rows, 512, 512 + w, 0, 1)],
                [R(rows, 512, 512 + w, 0, 1)], tag=f"{tag}r6")
            dve("reciprocal", (rrow, vrow),
                [R(rows, 512, 512 + w, 0, 1)], [R(rows, 1024, 1024 + w, 0, 1)], tag=f"{tag}r7")
            dve("scalar_tensor_tensor", (nmrow, srow, -1.0, rrow, ALU.mult, ALU.mult),
                [R(rows, 0, w, 0, 1), R(rows, 1024, 1024 + w, 0, 1)],
                [R(rows, 1536, 1536 + w, 0, 1)], tag=f"{tag}r8")
            bld.op("pe", lambda e, o=psum[bk_rb][:, 0:w], l=cst["ones_r"][:], r=rrow:
                   e.matmul(o, l, r, start=True, stop=True),
                   reads=[R(rows, 1024, 1024 + w, 0, 1), R(cst["ones_r"])],
                   writes=[RP(bk_rb)], tag=f"{tag}b1")
            bld.op("pe", lambda e, o=psum[bk_mb][:, 0:w], l=cst["ones_r"][:], r=nmrow:
                   e.matmul(o, l, r, start=True, stop=True),
                   reads=[R(rows, 1536, 1536 + w, 0, 1), R(cst["ones_r"])],
                   writes=[RP(bk_mb)], tag=f"{tag}b2")
            dve("tensor_copy", (rbmb[:, 0:w], psum[bk_rb][:, 0:w]),
                [RP(bk_rb)], [R(rbmb, 0, w)], tag=f"{tag}be1")
            dve("tensor_copy", (rbmb[:, 512:512 + w], psum[bk_mb][:, 0:w]),
                [RP(bk_mb)], [R(rbmb, 512, 512 + w)], tag=f"{tag}be2")
            rb_ap = rbmb[:, 0:w]
            mb_ap = rbmb[:, 512:512 + w]
            for g in range(NG):
                dst_write(g, c0, w, rb_ap, mb_ap, [R(rbmb, 0, w), R(rbmb, 512, 512 + w)])

    def conv_pass(src_h, dst_emit, taps, bias, subs, tmp_h, tag="cv"):
        for (c0, w) in subs:
            for g in range(NG):
                base = g * TP
                s_rs = lambda a0, a1: R(src_h, base + a0, base + a1)
                tc = (g % 2) * 512
                t_ap = tmp_h[:, tc:tc + w]
                t_rs = R(tmp_h, tc, tc + w)
                off, _ = SMALLS[taps]
                w0 = sm[:, off + g * 3 + 0:off + g * 3 + 1]
                w1 = sm[:, off + g * 3 + 1:off + g * 3 + 2]
                w2 = sm[:, off + g * 3 + 2:off + g * 3 + 3]
                boff, _ = SMALLS[bias]
                bcol = sm[:, boff + g:boff + g + 1]
                dve("tensor_scalar", (t_ap, src_h[:, base + c0:base + c0 + w],
                                      w1, bcol, ALU.mult, ALU.add),
                    [s_rs(c0, c0 + w), smres(taps), smres(bias)], [t_rs], tag=f"{tag}c")
                lo = max(c0, 1)
                if lo < c0 + w:
                    dve("scalar_tensor_tensor",
                        (tmp_h[:, tc + lo - c0:tc + w], src_h[:, base + lo - 1:base + c0 + w - 1],
                         w0, tmp_h[:, tc + lo - c0:tc + w], ALU.mult, ALU.add),
                        [s_rs(lo - 1, c0 + w - 1), t_rs, smres(taps)], [t_rs], tag=f"{tag}l")
                hi = min(c0 + w, TP - 1)
                if hi > c0:
                    dve("scalar_tensor_tensor",
                        (tmp_h[:, tc:tc + hi - c0], src_h[:, base + c0 + 1:base + hi + 1],
                         w2, tmp_h[:, tc:tc + hi - c0], ALU.mult, ALU.add),
                        [s_rs(c0 + 1, hi + 1), t_rs, smres(taps)], [t_rs], tag=f"{tag}r")
                dst_emit(g, c0, w, t_ap, [t_rs])

    def mask_halo(h, tag=""):
        for g in range(NG):
            base = g * TP
            dve("tensor_tensor", (h[:, base:base + HALO], h[:, base:base + HALO],
                                  sm[:, SMALLS["lmask"][0]:SMALLS["lmask"][0] + HALO],
                                  ALU.mult),
                [R(h, base, base + HALO), smres("lmask")], [R(h, base, base + HALO)],
                tag=f"mskl{tag}")
            dve("tensor_tensor", (h[:, base + TP - HALO:base + TP],
                                  h[:, base + TP - HALO:base + TP],
                                  sm[:, SMALLS["rmask"][0]:SMALLS["rmask"][0] + HALO],
                                  ALU.mult),
                [R(h, base + TP - HALO, base + TP), smres("rmask")],
                [R(h, base + TP - HALO, base + TP)], tag=f"mskr{tag}")

    def wload(dram_h, sb_h, tag=""):
        return dma(st_w, sb_h[:], dram_h.ap(), [RD(dram_h)], [R(sb_h)], tag)

    def proj(w_sb, F_out, x_h, x_base_fn, subs, evac, banks=("pmm0", "pmm1", "pmm2"),
             n_k=NG, tag="pj"):
        bi = 0
        for (c0, w) in subs:
            for oc in range(F_out // P):
                bank = banks[bi % len(banks)]
                bi += 1
                for k in range(n_k):
                    l_ap = w_sb[:, k * F_out + oc * P:k * F_out + (oc + 1) * P]
                    x_ap, x_res = x_base_fn(k, c0, w)
                    ps = psum[bank]
                    bld.op("pe",
                           lambda e, o=ps[:, 0:w], l=l_ap, r=x_ap,
                           st=(k == 0), sp=(k == n_k - 1): e.matmul(o, l, r, start=st, stop=sp),
                           reads=[R(w_sb, k * F_out + oc * P, k * F_out + (oc + 1) * P),
                                  x_res] + ([] if k == 0 else [RP(bank)]),
                           writes=[RP(bank)], tag=f"{tag}mm")
                evac(oc, c0, w, bank, [RP(bank)])

    xn0b = bld.alloc("xn0b", [P, NG * TP], BF16)
    bufs = {}

    def load_u_sub(c0, w, tag="u"):
        u_sub = bufs["u_sub"]
        tok = dma(st_io, u_sub[:, 0:NG * 512].rearrange("p (g m) -> p g m", g=NG)[:, :, 0:w],
                  u_t.ap().rearrange("p (g m) -> p g m", g=NG)[:, :, c0:c0 + w],
                  [RD(u_t)], [R(u_sub)], tag)
        return tok

    cvt = bld.alloc("cvt", [P, 2 * 512], F32)
    if "PH1" in phases:
        bufs["u_sub"] = bld.alloc("u_sub", [P, NG * 512], F32)
        u_sub = bufs["u_sub"]
        for (c0, w) in SUBS:
            load_u_sub(c0, w)
            def srcfn(g, cc0, ww, _c0=c0):
                return (u_sub[:, g * 512:g * 512 + ww], R(u_sub, g * 512, g * 512 + ww))
            def wr(g, cc0, ww, rb, mb, deps, _c0=c0):
                t = cvt[:, 0:ww]
                tres = R(cvt, 0, ww)
                dve("tensor_tensor", (t, u_sub[:, g * 512:g * 512 + ww], rb, ALU.mult),
                    [R(u_sub, g * 512, g * 512 + ww)] + deps, [tres], tag="ln0a")
                o = xn0b[:, g * TP + _c0:g * TP + _c0 + ww]
                dve("tensor_tensor", (o, t, mb, ALU.add), [tres] + deps,
                    [R(xn0b, g * TP + _c0, g * TP + _c0 + ww)], tag="ln0b")
            ln_pass(u_sub, srcfn, [(c0, w)], wr, tag="ln0")
        bld.free(u_sub)

    thunks = []
    if "PH1" in phases:
        a_h = bld.alloc("a_h", [P, NG * TP], F32)
        def a_emit(g, c0, w, t_ap, deps):
            o = a_h[:, g * TP + c0:g * TP + c0 + w]
            ores = R(a_h, g * TP + c0, g * TP + c0 + w)
            dve("tensor_tensor", (o, t_ap, xn0b[:, g * TP + c0:g * TP + c0 + w], ALU.add),
                deps + [R(xn0b, g * TP + c0, g * TP + c0 + w)], [ores], tag="a")
        for (c0, w) in SUBS:
            thunks.append(lambda c0=c0, w=w: conv_pass(
                xn0b, a_emit, "cpe1_w", "cpe1_b", [(c0, w)], cvt, tag="cv1"))
        thunks.append(lambda: dma(st_io, a_sp.ap(), a_h[:], [R(a_h)], [RD(a_sp)],
                                  "aspill"))

    if "PH2" in phases:
        xn1 = bld.alloc("xn1", [P, NG * TP], BF16)
        def srcfn2(g, c0, w):
            return (a_h[:, g * TP + c0:g * TP + c0 + w], R(a_h, g * TP + c0, g * TP + c0 + w))
        def wr2(g, c0, w, rb, mb, deps):
            t = cvt[:, 0:w]
            tres = R(cvt, 0, w)
            dve("tensor_tensor", (t, a_h[:, g * TP + c0:g * TP + c0 + w], rb, ALU.mult),
                [R(a_h, g * TP + c0, g * TP + c0 + w)] + deps, [tres], tag="ln1a")
            dve("tensor_tensor", (t, t, mb, ALU.add), [tres] + deps, [tres], tag="ln1b")
            o = xn1[:, g * TP + c0:g * TP + c0 + w]
            dve("tensor_scalar", (o, t, smcol("g1", g), smcol("b1", g), ALU.mult, ALU.add),
                [tres, smres("g1"), smres("b1")],
                [R(xn1, g * TP + c0, g * TP + c0 + w)], tag="ln1c")
        for (c0, w) in SUBS:
            thunks.append(lambda c0=c0, w=w: ln_pass(
                a_h, srcfn2, [(c0, w)], wr2, tag="ln1",
                banks=("pkv0", "pkv1", "pkv0", "pkv1")))

    f1_units = []
    if "F1" in phases:
        wslr = [bld.alloc(f"w_sl{i}", [P, 2 * NG * P], BF16) for i in range(4)]
        t1s = bld.alloc("t1s", [P, 4 * 512], F32)
        h0ev = bld.alloc("h0ev", [P, 2 * 512], BF16)
        h0_sp = env["h0_sp"]
        w_w1, w_w2 = env["w_w1"], env["w_w2"]

        def f1_unit(oc):
            slb = wslr[oc % 4]
            dma(st_w, slb[:, 0:NG * P], w_w1.ap()[oc], [RD(w_w1)], [R(slb, 0, NG * P)],
                f"w1s{oc}")
            dma(st_w, slb[:, NG * P:2 * NG * P], w_w2.ap()[oc], [RD(w_w2)],
                [R(slb, NG * P, 2 * NG * P)], f"w2s{oc}")
            banks4w = ("pmm0", "pmm1", "pmm2", "pst")
            for k in range(NG):
                for si, (c0, w) in enumerate(RSUBS):
                    b1 = banks4w[si]
                    bld.op("pe", lambda e, o=psum[b1][:, 0:w],
                           l=slb[:, k * P:(k + 1) * P],
                           r=xn0b[:, k * TP + c0:k * TP + c0 + w],
                           st=(k == 0), sp=(k == NG - 1): e.matmul(o, l, r, start=st, stop=sp),
                           reads=[R(slb, k * P, (k + 1) * P),
                                  R(xn0b, k * TP + c0, k * TP + c0 + w)] +
                                 ([] if k == 0 else [RP(b1)]),
                           writes=[RP(b1)], tag="w1mm")
            for si, (c0, w) in enumerate(RSUBS):
                b1 = banks4w[si]
                act(t1s[:, si * 512:si * 512 + w], psum[b1][:, 0:w], AF.Silu,
                    [RP(b1), smres("w1_s")], [R(t1s, si * 512, si * 512 + w)],
                    scale=smcol("w1_s", oc), tag="f1silu")
            for k in range(NG):
                for si, (c0, w) in enumerate(RSUBS):
                    b2 = banks4w[si]
                    bld.op("pe", lambda e, o=psum[b2][:, 0:w],
                           l=slb[:, NG * P + k * P:NG * P + (k + 1) * P],
                           r=xn0b[:, k * TP + c0:k * TP + c0 + w],
                           st=(k == 0), sp=(k == NG - 1): e.matmul(o, l, r, start=st, stop=sp),
                           reads=[R(slb, NG * P + k * P, NG * P + (k + 1) * P),
                                  R(xn0b, k * TP + c0, k * TP + c0 + w)] +
                                 ([] if k == 0 else [RP(b2)]),
                           writes=[RP(b2)], tag="w2mm")
            for si, (c0, w) in enumerate(RSUBS):
                b2 = banks4w[si]
                par = si % 2
                dve("scalar_tensor_tensor",
                    (h0ev[:, par * 512:par * 512 + w], psum[b2][:, 0:w],
                     smcol("w2_s", oc), t1s[:, si * 512:si * 512 + w], ALU.mult, ALU.mult),
                    [RP(b2), R(t1s, si * 512, si * 512 + w), smres("w2_s")],
                    [R(h0ev, par * 512, par * 512 + w)], tag="f1mul")
                hcol = oc * T + (c0 - R0)
                dma(st_h, h0_sp.ap()[:, hcol:hcol + w], h0ev[:, par * 512:par * 512 + w],
                    [R(h0ev, par * 512, par * 512 + w)], [RD(h0_sp)], "h0sp")

        f1_units = [lambda oc=oc: f1_unit(oc) for oc in range(NGH)]

    na, nb = len(f1_units), len(thunks)
    ia = ib = 0
    while ia < na or ib < nb:
        if ib >= nb or (ia < na and ia * (nb + 1) <= ib * (na + 1)):
            f1_units[ia]()
            ia += 1
        else:
            thunks[ib]()
            ib += 1

    if "F1" in phases:
        for _h in wslr:
            bld.free(_h)
        bld.free(t1s)
        bld.free(h0ev)
    if "PH2" in phases:
        bld.free(a_h)
    bld.free(xn0b)
    bld.free(sq2[0])
    bld.free(rows2[0])
    bld.free(rbmb2[0])
    bld.free(rbmb2[1])

    if "PH3" in phases:
        in_wb = bld.alloc("in_wb", [P, NG * C], BF16)
        wload(env["w_in"], in_wb, "w_in")
        xi0 = bld.alloc("xi0", [P, NG * TP], BF16)
        def xfn(k, c0, w):
            return (xn1[:, k * TP + c0:k * TP + c0 + w],
                    R(xn1, k * TP + c0, k * TP + c0 + w))
        def ev3(oc, c0, w, bank, deps):
            act(xi0[:, oc * TP + c0:oc * TP + c0 + w], psum[bank][:, 0:w], AF.Identity,
                deps + [smres("in_b")], [R(xi0, oc * TP + c0, oc * TP + c0 + w)],
                bias=smcol("in_b", oc), tag="ev3")
        proj(in_wb, C, xn1, xfn, SUBS, ev3, tag="inw")
        bld.free(in_wb)

    if "PH4" in phases:
        mask_halo(xi0, "xi0")
        xi = bld.alloc("xi", [P, NG * TP], BF16)
        def xi_emit(g, c0, w, t_ap, deps):
            act(xi[:, g * TP + c0:g * TP + c0 + w], t_ap, AF.Silu,
                deps, [R(xi, g * TP + c0, g * TP + c0 + w)], tag="silu_xi")
        conv_pass(xi0, xi_emit, "dwc_w", "dwc_b", SUBS, cvt, tag="cv2")
        mask_halo(xi, "xi")
        bld.free(xi0)

    if "PH5" in phases:
        qk_wb = bld.alloc("qk_wb", [P, NG * 2 * C], BF16)
        wload(env["w_qk"], qk_wb, "w_qk")
        q_h = bld.alloc("q_h", [P, NG * TP], BF16)
        krope = bld.alloc("krope", [P, NG * TP], BF16)
        k_sub = bld.alloc("k_sub", [P, NG * 512], BF16)
        elu_e = bld.alloc("elu_e", [P, 2 * 512], F32)
        elu_p = bld.alloc("elu_p", [P, 2 * 512], F32)
        ksum = bld.alloc("ksum", [P, NG], F32)
        ksum_t = bld.alloc("ksum_t", [P, NG], F32)
        dve("memset", (ksum[:], 0.0), [], [R(ksum)], tag="ks0")

        def xfn5(k, c0, w):
            return (xi[:, k * TP + c0:k * TP + c0 + w],
                    R(xi, k * TP + c0, k * TP + c0 + w))

        def elu_evac(oc, c0, w, bank, deps, out_ap, out_res):
            bcol = smcol("qk_b", oc)
            nbcol = smcol("qk_nb", oc)
            eo = (oc % 2) * 512
            act(elu_e[:, eo:eo + w], psum[bank][:, 0:w], AF.Relu,
                deps + [smres("qk_nb")], [R(elu_e, eo, eo + w)],
                bias=nbcol, scale=-1.0, tag="elm")
            act(elu_e[:, eo:eo + w], elu_e[:, eo:eo + w], AF.Exp,
                [R(elu_e, eo, eo + w)], [R(elu_e, eo, eo + w)], scale=-1.0, tag="elexp")
            act(elu_p[:, eo:eo + w], psum[bank][:, 0:w], AF.Relu,
                deps + [smres("qk_b")], [R(elu_p, eo, eo + w)], bias=bcol, tag="elp")
            bld.op("pool", lambda e, o=out_ap, a=elu_e[:, eo:eo + w],
                   b=elu_p[:, eo:eo + w]: e.tensor_tensor(o, a, b, ALU.add),
                   reads=[R(elu_e, eo, eo + w), R(elu_p, eo, eo + w)],
                   writes=[out_res], tag="eladd")

        for (c0, w) in SUBS:
            for oc in range(NG):
                bank = ("pmm0", "pmm1", "pmm2")[oc % 3]
                for k in range(NG):
                    l_ap = qk_wb[:, k * 2 * C + oc * P:k * 2 * C + (oc + 1) * P]
                    x_ap, x_res = xfn5(k, c0, w)
                    bld.op("pe", lambda e, o=psum[bank][:, 0:w], l=l_ap, r=x_ap,
                           st=(k == 0), sp=(k == NG - 1): e.matmul(o, l, r, start=st, stop=sp),
                           reads=[R(qk_wb, k * 2 * C + oc * P, k * 2 * C + (oc + 1) * P),
                                  x_res] + ([] if k == 0 else [RP(bank)]),
                           writes=[RP(bank)], tag="qmm")
                elu_evac(oc, c0, w, bank, [RP(bank)],
                         q_h[:, oc * TP + c0:oc * TP + c0 + w],
                         R(q_h, oc * TP + c0, oc * TP + c0 + w))
            for oc in range(NG):
                bank = ("pmm0", "pmm1", "pmm2")[oc % 3]
                for k in range(NG):
                    l_ap = qk_wb[:, k * 2 * C + (NG + oc) * P:k * 2 * C + (NG + oc + 1) * P]
                    x_ap, x_res = xfn5(k, c0, w)
                    bld.op("pe", lambda e, o=psum[bank][:, 0:w], l=l_ap, r=x_ap,
                           st=(k == 0), sp=(k == NG - 1): e.matmul(o, l, r, start=st, stop=sp),
                           reads=[R(qk_wb, k * 2 * C + (NG + oc) * P,
                                    k * 2 * C + (NG + oc + 1) * P),
                                  x_res] + ([] if k == 0 else [RP(bank)]),
                           writes=[RP(bank)], tag="kmm")
                elu_evac(NG + oc, c0, w, bank, [RP(bank)],
                         k_sub[:, oc * 512:oc * 512 + w],
                         R(k_sub, oc * 512, oc * 512 + w))
            r0 = max(c0, R0)
            r1_ = min(c0 + w, R1)
            if r0 < r1_:
                for g in range(NG):
                    dve("reduce_sum", (ksum_t[:, g:g + 1],
                                       k_sub[:, g * 512 + r0 - c0:g * 512 + r1_ - c0], AX.X),
                        [R(k_sub, g * 512 + r0 - c0, g * 512 + r1_ - c0)],
                        [R(ksum_t, g, g + 1)], tag="ksr")
                dve("tensor_tensor", (ksum[:], ksum[:], ksum_t[:], ALU.add),
                    [R(ksum), R(ksum_t)], [R(ksum)], tag="ksa")
            for g in range(NG):
                bank = ("pmm0", "pmm1")[g % 2]
                bld.op("pe", lambda e, o=psum[bank][:, 0:w],
                       l=cst["rope"][:, g * P:(g + 1) * P], r=k_sub[:, g * 512:g * 512 + w]:
                       e.matmul(o, l, r, start=True, stop=True),
                       reads=[R(cst["rope"], g * P, (g + 1) * P),
                              R(k_sub, g * 512, g * 512 + w)],
                       writes=[RP(bank)], tag="krmm")
                dve("tensor_copy", (krope[:, g * TP + c0:g * TP + c0 + w],
                                    psum[bank][:, 0:w]),
                    [RP(bank)], [R(krope, g * TP + c0, g * TP + c0 + w)], tag="krev")
        bld.free(qk_wb)
        bld.free(k_sub)
        bld.free(elu_e)
        bld.free(elu_p)
        bld.free(ksum_t)

    if "PH6" in phases:
        krt = bld.alloc("krt", [P, 2 * P], BF16)
        vht = bld.alloc("vht", [P, 2 * P], BF16)
        def kv_bank(g):
            return ("pkv0", g % 4) if g < 4 else ("pkv1", g % 4)

        def emit_tr(g, ti):
            tc0 = R0 + ti * P
            par = ti % 2
            tb = ("ptra", "ptrb")[par]
            bld.op("pe", lambda e, o=psum[tb][:, 0:P],
                   l=krope[:, g * TP + tc0:g * TP + tc0 + P], i=cst["ident"][:]:
                   e.transpose(o, l, i),
                   reads=[R(krope, g * TP + tc0, g * TP + tc0 + P), R(cst["ident"]),
                          RP(tb)],
                   writes=[RP(tb)], tag="trk")
            dve("tensor_copy", (krt[:, par * P:(par + 1) * P], psum[tb][:, 0:P]),
                [RP(tb)], [R(krt, par * P, (par + 1) * P)], tag="trke")
            bld.op("pe", lambda e, o=psum[tb][:, P:2 * P],
                   l=xi[:, g * TP + tc0:g * TP + tc0 + P], i=cst["ident"][:]:
                   e.transpose(o, l, i),
                   reads=[R(xi, g * TP + tc0, g * TP + tc0 + P), R(cst["ident"]),
                          RP(tb)],
                   writes=[RP(tb)], tag="trv")
            dve("tensor_copy", (vht[:, par * P:(par + 1) * P], psum[tb][:, P:2 * P]),
                [RP(tb)], [R(vht, par * P, (par + 1) * P)], tag="trve")

        def emit_kvmm(g, ti):
            par = ti % 2
            kb, slot = kv_bank(g)
            bld.op("pe", lambda e, o=psum[kb][:, slot * P:(slot + 1) * P],
                   l=krt[:, par * P:(par + 1) * P], r=vht[:, par * P:(par + 1) * P],
                   st=(ti == 0), sp=(ti == 15): e.matmul(o, l, r, start=st, stop=sp),
                   reads=[R(krt, par * P, (par + 1) * P), R(vht, par * P, (par + 1) * P)]
                         + ([] if ti == 0 else [RP(kb)]),
                   writes=[RP(kb)], tag="kvmm")

        for g in range(NG):
            emit_tr(g, 0)
            for ti in range(16):
                if ti < 15:
                    emit_tr(g, ti + 1)
                emit_kvmm(g, ti)
        ccs = bld.alloc("ccs", [P, 2 * CCN], F32)
        kv_sb = bld.alloc("kv_sb", [P, NG * P], F32)
        for g in range(NG):
            kb, slot = kv_bank(g)
            dve("tensor_scalar", (kv_sb[:, g * P:(g + 1) * P],
                                  psum[kb][:, slot * P:(slot + 1) * P],
                                  1.0 / 8192.0, None, ALU.mult),
                [RP(kb)], [R(kv_sb, g * P, (g + 1) * P)], tag="kvev")
        for b in range(2):
            dve("tensor_scalar", (ccs[:, b * CCN:b * CCN + NG * P], kv_sb[:],
                                  smcol("bsel", b), None, ALU.mult),
                [R(kv_sb), smres("bsel")], [R(ccs, b * CCN, b * CCN + NG * P)], tag="cckv")
            dve("tensor_scalar", (ccs[:, b * CCN + NG * P:(b + 1) * CCN], ksum[:],
                                  smcol("bsel", b), None, ALU.mult),
                [R(ksum), smres("bsel")],
                [R(ccs, b * CCN + NG * P, (b + 1) * CCN)], tag="ccks")
        dma(st_cc, cc_in.ap().rearrange("b (p m) -> p b m", p=P),
            ccs[:].rearrange("p (b m) -> p b m", b=2),
            [R(ccs)], [RD(cc_in)], "ccst")
        if fake_cc:
            dma(st_cc, cc_out.ap(), cc_in.ap(), [RD(cc_in)], [RD(cc_out)], "fakecc")
        else:
            o = bld.op("pool", lambda e: e.collective_compute(
                "AllReduce", ALU.add,
                replica_groups=[[0, 1, 2, 3, 4, 5, 6, 7]],
                ins=[cc_in.ap().opt()], outs=[cc_out.ap().opt()]),
                reads=[RD(cc_in)], writes=[RD(cc_out)], kind="cc", tag="cc")
            o.token = ("cc_sem", 1)
        bld.free(krt)
        bld.free(vht)
        bld.free(kv_sb)
        bld.free(ccs)
        bld.free(krope)
        bld.free(ksum)

    if "PH7" in phases:
        lepe = bld.alloc("lepe", [P, NG * TP], BF16)
        def lp_emit(g, c0, w, t_ap, deps):
            dve("tensor_copy", (lepe[:, g * TP + c0:g * TP + c0 + w], t_ap),
                deps, [R(lepe, g * TP + c0, g * TP + c0 + w)], tag="lpcp")
        conv_pass(xi, lp_emit, "lepe_w", "lepe_b", SUBS, cvt, tag="cv3")
        bld.free(xi)
        bld.free(cvt)

    if "PH8" in phases:
        actp_wb = bld.alloc("actp_wb", [P, NG * C], BF16)
        wload(env["w_actp"], actp_wb, "w_actp")
        ar_h = bld.alloc("ar_h", [P, NG * TP], BF16)
        def xfn8(k, c0, w):
            return (xn1[:, k * TP + c0:k * TP + c0 + w],
                    R(xn1, k * TP + c0, k * TP + c0 + w))
        def ev8(oc, c0, w, bank, deps):
            act(ar_h[:, oc * TP + c0:oc * TP + c0 + w], psum[bank][:, 0:w], AF.Silu,
                deps + [smres("actp_b")], [R(ar_h, oc * TP + c0, oc * TP + c0 + w)],
                bias=smcol("actp_b", oc), tag="ev8")
        proj(actp_wb, C, xn1, xfn8, SUBS, ev8, tag="actp")
        bld.free(actp_wb)
        bld.free(xn1)

    if "PH9" in phases:
        dma(st_io, q_sp.ap(), q_h[:], [R(q_h)], [RD(q_sp)], "qsp")
        dma(st_io, ar_sp.ap(), ar_h[:], [R(ar_h)], [RD(ar_sp)], "arsp")
        dma(st_io, lp_sp.ap(), lepe[:], [R(lepe)], [RD(lp_sp)], "lpsp")
        bld.free(q_h)
        bld.free(ar_h)
        bld.free(lepe)

    def h_reload(src_sp, tag):
        for j in range(8):
            lo, hi = j * 4 * T, (j + 1) * 4 * T
            dma(st_h, h_full[:, lo:hi], src_sp.ap()[:, lo:hi],
                [RD(src_sp)], [R(h_full, lo, hi)], f"{tag}{j}")

    if "F2" in phases:
        h_full = bld.alloc("h_full", [P, NGH * T], BF16)
        h_reload(env["h0_sp"], "h0rl")

    def h_pass(w_dram, s_name, dst_sp, tag):
        slr = [bld.alloc(f"hsl{i}{tag}", [P, NGH * P], BF16) for i in range(4)]
        ev = bld.alloc(f"hev{tag}", [P, 2 * 512], BF16)
        banks6 = ("pmm0", "pmm1", "pmm2", "pst", "pkv0", "pkv1")
        for oc in range(NGH):
            slb = slr[oc % 4]
            dma(st_w, slb[:], w_dram.ap()[oc], [RD(w_dram)], [R(slb)], f"{tag}s{oc}")
            for k in range(NGH):
                for si, (c0, w) in enumerate(RSUBS):
                    bank = banks6[(oc * 4 + si) % 6]
                    bld.op("pe", lambda e, o=psum[bank][:, 0:w],
                           l=slb[:, k * P:(k + 1) * P],
                           r=h_full[:, k * T + (c0 - R0):k * T + (c0 - R0) + w],
                           st=(k == 0), sp=(k == NGH - 1): e.matmul(o, l, r, start=st, stop=sp),
                           reads=[R(slb, k * P, (k + 1) * P),
                                  R(h_full, k * T + (c0 - R0), k * T + (c0 - R0) + w)] +
                                 ([] if k == 0 else [RP(bank)]),
                           writes=[RP(bank)], tag=f"{tag}mm")
            for si, (c0, w) in enumerate(RSUBS):
                bank = banks6[(oc * 4 + si) % 6]
                epar = si % 2
                act(ev[:, epar * 512:epar * 512 + w], psum[bank][:, 0:w], AF.Gelu,
                    [RP(bank), smres(s_name)], [R(ev, epar * 512, epar * 512 + w)],
                    scale=smcol(s_name, oc), tag=f"{tag}g")
                dma(st_h, dst_sp.ap()[:, oc * T + (c0 - R0):oc * T + (c0 - R0) + w],
                    ev[:, epar * 512:epar * 512 + w],
                    [R(ev, epar * 512, epar * 512 + w)], [RD(dst_sp)], f"{tag}sp")
        for _h in slr:
            bld.free(_h)
        bld.free(ev)

    if "F2" in phases:
        h_pass(env["w_h1"], "h1_s", h1_sp, "f2")
        h_reload(h1_sp, "h1rl")
    if "F3" in phases:
        h_pass(env["w_h2"], "h2_s", h2_sp, "f3")
        h_reload(h2_sp, "h2rl")

    if "F4" in phases:
        w3r = [bld.alloc(f"w3sl{i}", [P, NGH * P], BF16) for i in range(4)]
        uo_sub = bld.alloc("uo_sub", [P, 2 * 512], F32)
        fo_sub = bld.alloc("fo_sub", [P, 2 * 512], F32)
        w_w3 = env["w_w3"]
        banks4f = ("pmm0", "pmm1", "pmm2", "pst", "pkv0", "pkv1")
        for oc in range(NG):
            slb = w3r[oc % 4]
            dma(st_w, slb[:], w_w3.ap()[oc], [RD(w_w3)], [R(slb)], f"w3s{oc}")
            for k in range(NGH):
                for si, (c0, w) in enumerate(RSUBS):
                    bank = banks4f[(oc * 4 + si) % 6]
                    bld.op("pe", lambda e, o=psum[bank][:, 0:w],
                           l=slb[:, k * P:(k + 1) * P],
                           r=h_full[:, k * T + (c0 - R0):k * T + (c0 - R0) + w],
                           st=(k == 0), sp=(k == NGH - 1): e.matmul(o, l, r, start=st, stop=sp),
                           reads=[R(slb, k * P, (k + 1) * P),
                                  R(h_full, k * T + (c0 - R0), k * T + (c0 - R0) + w)] +
                                 ([] if k == 0 else [RP(bank)]),
                           writes=[RP(bank)], tag="w3mm")
            for si, (c0, w) in enumerate(RSUBS):
                bank = banks4f[(oc * 4 + si) % 6]
                po = (si % 2) * 512
                dma(st_io, uo_sub[:, po:po + w],
                    u_t.ap()[:, oc * TP + c0:oc * TP + c0 + w],
                    [RD(u_t)], [R(uo_sub, po, po + w)], "u4")
                act(fo_sub[:, po:po + w], psum[bank][:, 0:w], AF.Identity,
                    [RP(bank), smres("w3_s")], [R(fo_sub, po, po + w)],
                    scale=smcol("w3_s", oc), tag="f4ev")
                dve("tensor_tensor", (fo_sub[:, po:po + w], fo_sub[:, po:po + w],
                                      uo_sub[:, po:po + w], ALU.add),
                    [R(fo_sub, po, po + w), R(uo_sub, po, po + w)],
                    [R(fo_sub, po, po + w)], tag="f4add")
                dma(st_io, out_t.ap()[:, oc * T + (c0 - R0):oc * T + (c0 - R0) + w],
                    fo_sub[:, po:po + w], [R(fo_sub, po, po + w)], [RD(out_t)], "f4st")
        for _h in w3r:
            bld.free(_h)
        bld.free(h_full)
        bld.free(uo_sub)
        bld.free(fo_sub)

    if "B0" in phases:
        ccr = bld.alloc("ccr", [P, 2 * CCN], F32)
        kvb = bld.alloc("kvb", [P, NG * P], BF16)
        kml = bld.alloc("kml", [P, NG * NH], BF16)
        kvf = bld.alloc("kvf", [P, NG * P], F32)
        ksg = bld.alloc("ksg", [P, NG], F32)
        dma(st_cc, ccr[:].rearrange("p (b m) -> p b m", b=2),
            cc_out.ap().rearrange("b (p m) -> p b m", p=P),
            [RD(cc_out)], [R(ccr)], "ccrd")
        dve("tensor_scalar", (kvf[:], ccr[:, 0:NG * P], smcol("bsel", 0), None, ALU.mult),
            [R(ccr, 0, NG * P), smres("bsel")], [R(kvf)], tag="b0a")
        dve("scalar_tensor_tensor", (kvf[:], ccr[:, CCN:CCN + NG * P], smcol("bsel", 1),
                                     kvf[:], ALU.mult, ALU.add),
            [R(ccr, CCN, CCN + NG * P), R(kvf), smres("bsel")], [R(kvf)], tag="b0b")
        dve("tensor_scalar", (ksg[:], ccr[:, NG * P:CCN], smcol("bsel", 0), None, ALU.mult),
            [R(ccr, NG * P, CCN), smres("bsel")], [R(ksg)], tag="b0c")
        dve("scalar_tensor_tensor", (ksg[:], ccr[:, CCN + NG * P:2 * CCN], smcol("bsel", 1),
                                     ksg[:], ALU.mult, ALU.add),
            [R(ccr, CCN + NG * P, 2 * CCN), R(ksg), smres("bsel")], [R(ksg)], tag="b0d")
        dve("memset", (kvb[:], 0.0), [], [R(kvb)], tag="b0z")
        for g in range(NG):
            dve("tensor_copy", (kvb[0:64, g * P:g * P + 64], kvf[0:64, g * P:g * P + 64]),
                [R(kvf, g * P, g * P + 64, 0, 64)],
                [R(kvb, g * P, g * P + 64, 0, 64)], tag="b0d1")
            dve("tensor_copy", (kvb[64:128, g * P + 64:(g + 1) * P],
                                kvf[64:128, g * P + 64:(g + 1) * P]),
                [R(kvf, g * P + 64, (g + 1) * P, 64, 128)],
                [R(kvb, g * P + 64, (g + 1) * P, 64, 128)], tag="b0d2")
        for g in range(NG):
            dve("tensor_scalar", (kml[:, g * NH:(g + 1) * NH],
                                  cst["zmask"][:, g * NH:(g + 1) * NH],
                                  ksg[:, g:g + 1], None, ALU.mult),
                [R(cst["zmask"], g * NH, (g + 1) * NH), R(ksg, g, g + 1)],
                [R(kml, g * NH, (g + 1) * NH)], tag="b0k")
        bld.free(ccr)
        bld.free(kvf)

        outw_b = bld.alloc("outw_b", [P, NG * C], BF16)
        wload(env["w_out"], outw_b, "w_out")

    if "B1" in phases:
        y_h = bld.alloc("y_h", [P, NG * TP], F32)
        qs2 = [bld.alloc(f"qs_sub{i}", [P, NG * 512], BF16) for i in range(2)]
        ars2 = [bld.alloc(f"ars_sub{i}", [P, NG * 512], BF16) for i in range(2)]
        lps2 = [bld.alloc(f"lps_sub{i}", [P, NG * 512], BF16) for i in range(2)]
        as_sub = bld.alloc("as_sub", [P, NG * 512], F32)
        qr2 = [bld.alloc(f"qr_sub{i}", [P, NG * 512], BF16) for i in range(2)]
        at2 = [bld.alloc(f"at_sub{i}", [P, NG * 512], BF16) for i in range(2)]
        zsb = bld.alloc("zsb", [NH, 512], F32)
        zbb = bld.alloc("zbb", [P, 2 * 512], F32)
        t_f = bld.alloc("t_f", [P, 2 * 512], F32)

        for sidx, (c0, w) in enumerate(SUBS):
            qs_sub = qs2[sidx % 2]
            ars_sub = ars2[sidx % 2]
            lps_sub = lps2[sidx % 2]
            qr_sub = qr2[sidx % 2]
            at_sub = at2[sidx % 2]
            dma(st_io, qs_sub[:].rearrange("p (g m) -> p g m", g=NG)[:, :, 0:w],
                q_sp.ap().rearrange("p (g m) -> p g m", g=NG)[:, :, c0:c0 + w],
                [RD(q_sp)], [R(qs_sub)], "qld")
            dma(st_io, ars_sub[:].rearrange("p (g m) -> p g m", g=NG)[:, :, 0:w],
                ar_sp.ap().rearrange("p (g m) -> p g m", g=NG)[:, :, c0:c0 + w],
                [RD(ar_sp)], [R(ars_sub)], "arld")
            dma(st_io, lps_sub[:].rearrange("p (g m) -> p g m", g=NG)[:, :, 0:w],
                lp_sp.ap().rearrange("p (g m) -> p g m", g=NG)[:, :, c0:c0 + w],
                [RD(lp_sp)], [R(lps_sub)], "lpld")
            dma(st_io, as_sub[:].rearrange("p (g m) -> p g m", g=NG)[:, :, 0:w],
                a_sp.ap().rearrange("p (g m) -> p g m", g=NG)[:, :, c0:c0 + w],
                [RD(a_sp)], [R(as_sub)], "ald")
            for g in range(NG):
                bank = ("pkv0", "pkv1")[g % 2]
                bld.op("pe", lambda e, o=psum[bank][:, 0:w],
                       l=cst["rope"][:, g * P:(g + 1) * P],
                       r=qs_sub[:, g * 512:g * 512 + w]:
                       e.matmul(o, l, r, start=True, stop=True),
                       reads=[R(cst["rope"], g * P, (g + 1) * P),
                              R(qs_sub, g * 512, g * 512 + w)],
                       writes=[RP(bank)], tag="qrmm")
                dve("tensor_copy", (qr_sub[:, g * 512:g * 512 + w], psum[bank][:, 0:w]),
                    [RP(bank)], [R(qr_sub, g * 512, g * 512 + w)], tag="qrev")
            for g in range(NG):
                bld.op("pe", lambda e, o=psum["pst"][0:NH, 0:w],
                       l=kml[:, g * NH:(g + 1) * NH], r=qs_sub[:, g * 512:g * 512 + w],
                       st=(g == 0), sp=(g == NG - 1): e.matmul(o, l, r, start=st, stop=sp),
                       reads=[R(kml, g * NH, (g + 1) * NH),
                              R(qs_sub, g * 512, g * 512 + w)] +
                             ([] if g == 0 else [RP("pst")]),
                       writes=[RP("pst")], tag="zmm")
            dve("tensor_scalar", (zsb[:, 0:w], psum["pst"][0:NH, 0:w], 1e-6, None, ALU.add),
                [RP("pst")], [R(zsb, 0, w)], tag="zadd")
            dve("reciprocal", (zsb[:, 0:w], zsb[:, 0:w]),
                [R(zsb, 0, w)], [R(zsb, 0, w)], tag="zrec")
            for g in range(NG):
                zb_bank = ("pmm2", "pst")[g % 2]
                at_bank = ("pmm0", "pmm1")[g % 2]
                bld.op("pe", lambda e, o=psum[zb_bank][:, 0:w],
                       l=cst["zsel"][:, g * P:(g + 1) * P], r=zsb[0:NH, 0:w]:
                       e.matmul(o, l, r, start=True, stop=True),
                       reads=[R(cst["zsel"], g * P, (g + 1) * P), R(zsb, 0, w)],
                       writes=[RP(zb_bank)], tag="zbmm")
                zo = (g % 2) * 512
                dve("tensor_copy", (zbb[:, zo:zo + w], psum[zb_bank][:, 0:w]),
                    [RP(zb_bank)], [R(zbb, zo, zo + w)], tag="zbev")
                bld.op("pe", lambda e, o=psum[at_bank][:, 0:w],
                       l=kvb[:, g * P:(g + 1) * P], r=qr_sub[:, g * 512:g * 512 + w]:
                       e.matmul(o, l, r, start=True, stop=True),
                       reads=[R(kvb, g * P, (g + 1) * P),
                              R(qr_sub, g * 512, g * 512 + w)],
                       writes=[RP(at_bank)], tag="atmm")
                dve("tensor_tensor", (t_f[:, zo:zo + w], psum[at_bank][:, 0:w],
                                      zbb[:, zo:zo + w], ALU.mult),
                    [RP(at_bank), R(zbb, zo, zo + w)], [R(t_f, zo, zo + w)], tag="atz")
                dve("tensor_tensor", (t_f[:, zo:zo + w], t_f[:, zo:zo + w],
                                      lps_sub[:, g * 512:g * 512 + w], ALU.add),
                    [R(t_f, zo, zo + w), R(lps_sub, g * 512, g * 512 + w)],
                    [R(t_f, zo, zo + w)], tag="atl")
                dve("tensor_tensor", (at_sub[:, g * 512:g * 512 + w], t_f[:, zo:zo + w],
                                      ars_sub[:, g * 512:g * 512 + w], ALU.mult),
                    [R(t_f, zo, zo + w), R(ars_sub, g * 512, g * 512 + w)],
                    [R(at_sub, g * 512, g * 512 + w)], tag="atar")
            for oc in range(NG):
                bank = ("pmm2", "pst")[oc % 2]
                for k in range(NG):
                    bld.op("pe", lambda e, o=psum[bank][:, 0:w],
                           l=outw_b[:, k * C + oc * P:k * C + (oc + 1) * P],
                           r=at_sub[:, k * 512:k * 512 + w],
                           st=(k == 0), sp=(k == NG - 1): e.matmul(o, l, r, start=st, stop=sp),
                           reads=[R(outw_b, k * C + oc * P, k * C + (oc + 1) * P),
                                  R(at_sub, k * 512, k * 512 + w)] +
                                 ([] if k == 0 else [RP(bank)]),
                           writes=[RP(bank)], tag="ywmm")
                yo = (oc % 2) * 512
                act(t_f[:, yo:yo + w], psum[bank][:, 0:w], AF.Identity,
                    [RP(bank), smres("out_b")], [R(t_f, yo, yo + w)],
                    bias=smcol("out_b", oc), tag="yev")
                dve("tensor_tensor", (y_h[:, oc * TP + c0:oc * TP + c0 + w],
                                      t_f[:, yo:yo + w],
                                      as_sub[:, oc * 512:oc * 512 + w], ALU.add),
                    [R(t_f, yo, yo + w), R(as_sub, oc * 512, oc * 512 + w)],
                    [R(y_h, oc * TP + c0, oc * TP + c0 + w)], tag="ya")
        for _h in qs2 + ars2 + lps2 + qr2 + at2 + [as_sub, outw_b, kvb, kml, ksg,
                                                    zsb, zbb, t_f]:
            bld.free(_h)

    if "B2" in phases:
        sq2[0] = sq2[1] = bld.alloc("sqs0b", [P, NG * 512], F32)
        rows2[0] = rows2[1] = bld.alloc("rows0b", [1, 4 * 512], F32)
        rbmb2[0] = bld.alloc("rbmb0b", [P, 2 * 512], F32)
        rbmb2[1] = bld.alloc("rbmb1b", [P, 2 * 512], F32)
        cvt = bld.alloc("cvtb", [P, 2 * 512], F32)
        mask_halo(y_h, "y")
        xn2b = bld.alloc("xn2b", [P, NG * T], BF16, top=True)
        x2_sub = bld.alloc("x2_sub", [P, NG * 512], F32)
        def x2_emit(g, c0, w, t_ap, deps):
            o = x2_sub[:, g * 512:g * 512 + w]
            dve("tensor_tensor", (o, t_ap, y_h[:, g * TP + c0:g * TP + c0 + w], ALU.add),
                deps + [R(y_h, g * TP + c0, g * TP + c0 + w)],
                [R(x2_sub, g * 512, g * 512 + w)], tag="x2")
        for (c0, w) in RSUBS:
            conv_pass(y_h, x2_emit, "cpe2_w", "cpe2_b", [(c0, w)], cvt, tag="cv4")
            dma(st_io, x2_sp.ap().rearrange("p (g m) -> p g m", g=NG)[:, :, c0 - R0:c0 - R0 + w],
                x2_sub[:].rearrange("p (g m) -> p g m", g=NG)[:, :, 0:w],
                [R(x2_sub)], [RD(x2_sp)], "x2sp")
            def srcb(g, cc0, ww):
                return (x2_sub[:, g * 512:g * 512 + ww], R(x2_sub, g * 512, g * 512 + ww))
            def wrb(g, cc0, ww, rb, mb, deps, _c0=c0):
                t = cvt[:, 0:ww]
                tres = R(cvt, 0, ww)
                dve("tensor_tensor", (t, x2_sub[:, g * 512:g * 512 + ww], rb, ALU.mult),
                    [R(x2_sub, g * 512, g * 512 + ww)] + deps, [tres], tag="ln2a")
                dve("tensor_tensor", (t, t, mb, ALU.add), [tres] + deps, [tres], tag="ln2b")
                o = xn2b[:, g * T + _c0 - R0:g * T + _c0 - R0 + ww]
                dve("tensor_scalar", (o, t, smcol("g2", g), smcol("b2", g), ALU.mult, ALU.add),
                    [tres, smres("g2"), smres("b2")],
                    [R(xn2b, g * T + _c0 - R0, g * T + _c0 - R0 + ww)], tag="ln2c")
            ln_pass(x2_sub, srcb, [(c0, w)], wrb, tag="ln2")
        bld.free(y_h)
        bld.free(x2_sub)
        bld.free(cvt)
        for _h in {h.name: h for h in sq2 + rows2 + rbmb2}.values():
            bld.free(_h)

    if "B3" in phases:
        g_full = bld.alloc("g_full", [P, NGH * T], BF16)
        f1r = [bld.alloc(f"f1sl{i}", [P, NG * P], BF16) for i in range(4)]
        w_fc1 = env["w_fc1"]
        banks4b = ("pmm0", "pmm1", "pmm2", "pst", "pkv0", "pkv1")
        for oc in range(NGH):
            slb = f1r[oc % 4]
            dma(st_w, slb[:], w_fc1.ap()[oc], [RD(w_fc1)], [R(slb)], f"fc1s{oc}")
            for k in range(NG):
                for si, (c0, w) in enumerate(RSUBS):
                    bank = banks4b[(oc * 4 + si) % 6]
                    bld.op("pe", lambda e, o=psum[bank][:, 0:w],
                           l=slb[:, k * P:(k + 1) * P],
                           r=xn2b[:, k * T + (c0 - R0):k * T + (c0 - R0) + w],
                           st=(k == 0), sp=(k == NG - 1): e.matmul(o, l, r, start=st, stop=sp),
                           reads=[R(slb, k * P, (k + 1) * P),
                                  R(xn2b, k * T + (c0 - R0), k * T + (c0 - R0) + w)] +
                                 ([] if k == 0 else [RP(bank)]),
                           writes=[RP(bank)], tag="fc1mm")
            for si, (c0, w) in enumerate(RSUBS):
                bank = banks4b[(oc * 4 + si) % 6]
                act(g_full[:, oc * T + (c0 - R0):oc * T + (c0 - R0) + w],
                    psum[bank][:, 0:w], AF.Gelu,
                    [RP(bank), smres("fc1_b")],
                    [R(g_full, oc * T + (c0 - R0), oc * T + (c0 - R0) + w)],
                    bias=smcol("fc1_b", oc), tag="fc1g")
        for _h in f1r:
            bld.free(_h)
        bld.free(xn2b)

    if "B4" in phases:
        f2r = [bld.alloc(f"f2sl{i}", [P, NGH * P], BF16) for i in range(4)]
        x2r = bld.alloc("x2r", [P, 2 * 512], F32)
        po_r = bld.alloc("po_r", [P, 2 * 512], F32)
        o_sub = bld.alloc("o_sub", [P, 2 * 512], F32)
        w_fc2 = env["w_fc2"]
        banks4c = ("pmm0", "pmm1", "pmm2", "pst", "pkv0", "pkv1")
        for oc in range(NG):
            slb = f2r[oc % 4]
            dma(st_w, slb[:], w_fc2.ap()[oc], [RD(w_fc2)], [R(slb)], f"fc2s{oc}")
            for k in range(NGH):
                for si, (c0, w) in enumerate(RSUBS):
                    bank = banks4c[(oc * 4 + si) % 6]
                    bld.op("pe", lambda e, o=psum[bank][:, 0:w],
                           l=slb[:, k * P:(k + 1) * P],
                           r=g_full[:, k * T + (c0 - R0):k * T + (c0 - R0) + w],
                           st=(k == 0), sp=(k == NGH - 1): e.matmul(o, l, r, start=st, stop=sp),
                           reads=[R(slb, k * P, (k + 1) * P),
                                  R(g_full, k * T + (c0 - R0), k * T + (c0 - R0) + w)] +
                                 ([] if k == 0 else [RP(bank)]),
                           writes=[RP(bank)], tag="fc2mm")
            for si, (c0, w) in enumerate(RSUBS):
                bank = banks4c[(oc * 4 + si) % 6]
                rc = oc * T + (c0 - R0)
                po = ((oc * 4 + si) % 2) * 512
                dma(st_io, x2r[:, po:po + w], x2_sp.ap()[:, rc:rc + w],
                    [RD(x2_sp)], [R(x2r, po, po + w)], "x2rl")
                dma(st_io, po_r[:, po:po + w], out_t.ap()[:, rc:rc + w],
                    [RD(out_t)], [R(po_r, po, po + w)], "porl")
                act(o_sub[:, po:po + w], psum[bank][:, 0:w], AF.Identity,
                    [RP(bank), smres("fc2_b")], [R(o_sub, po, po + w)],
                    bias=smcol("fc2_b", oc), tag="f2ev")
                dve("tensor_tensor", (o_sub[:, po:po + w], o_sub[:, po:po + w],
                                      x2r[:, po:po + w], ALU.add),
                    [R(o_sub, po, po + w), R(x2r, po, po + w)], [R(o_sub, po, po + w)],
                    tag="f2a1")
                dve("tensor_tensor", (o_sub[:, po:po + w], o_sub[:, po:po + w],
                                      po_r[:, po:po + w], ALU.add),
                    [R(o_sub, po, po + w), R(po_r, po, po + w)], [R(o_sub, po, po + w)],
                    tag="f2a2")
                dma(st_io, out_t.ap()[:, rc:rc + w], o_sub[:, po:po + w],
                    [R(o_sub, po, po + w)], [RD(out_t)], "ost")



def _tile_rows(a, ng):
    f = a.shape[1]
    return np.ascontiguousarray(a.reshape(ng, P, f).transpose(1, 0, 2).reshape(P, ng * f))


def _oc_slices(w_oi, n_oc, n_k):
    O, I = w_oi.shape
    assert O == n_oc * P and I == n_k * P
    a = w_oi.reshape(n_oc, P, n_k, P).transpose(0, 3, 2, 1)
    return np.ascontiguousarray(a.reshape(n_oc, P, n_k * P))


def _oc_slices_kf(w_kf, n_oc, n_k):
    K, F = w_kf.shape
    assert K == n_k * P and F == n_oc * P
    a = w_kf.reshape(n_k, P, n_oc, P).transpose(2, 1, 0, 3)
    return np.ascontiguousarray(a.reshape(n_oc, P, n_k * P))


def make_smalls(params, core):
    chunk = core % 4
    b = core // 4
    sm = np.zeros((P, SMALLS_COLS), np.float32)

    def put(name, arr):
        off, n = SMALLS[name]
        sm[:, off:off + n] = arr

    def chunks_vec(v):
        return np.ascontiguousarray(v.reshape(NG, P).T)

    put("g1", chunks_vec(params["norm1_g"]))
    put("b1", chunks_vec(params["norm1_b"]))
    put("g2", chunks_vec(params["norm2_g"]))
    put("b2", chunks_vec(params["norm2_b"]))
    for cv, wname, bname in [("cpe1", "cpe1_w", "cpe1_b"), ("dwc", "dwc_w", "dwc_b"),
                             ("lepe", "lepe_w", "lepe_b"), ("cpe2", "cpe2_w", "cpe2_b")]:
        w3 = params[wname].reshape(C, 3).reshape(NG, P, 3).transpose(1, 0, 2).reshape(P, NG * 3)
        put(cv + "_w", w3)
        put(cv + "_b", chunks_vec(params[bname]))
    put("in_b", chunks_vec(params["in_b"]))
    put("actp_b", chunks_vec(params["actp_b"]))
    put("qk_b", np.ascontiguousarray(params["qk_b"].reshape(2 * NG, P).T))
    put("qk_nb", np.ascontiguousarray(-params["qk_b"].reshape(2 * NG, P).T))
    put("out_b", chunks_vec(params["out_b"]))
    put("fc1_b", np.ascontiguousarray(params["fc1_b"].reshape(NGH, P).T))
    put("fc2_b", chunks_vec(params["fc2_b"]))
    put("w1_s", np.ascontiguousarray(params["w1_s"].reshape(NGH, P).T))
    put("w2_s", np.ascontiguousarray(params["w2_s"].reshape(NGH, P).T))
    put("h1_s", np.ascontiguousarray(params["h1_s"].reshape(NGH, P).T))
    put("h2_s", np.ascontiguousarray(params["h2_s"].reshape(NGH, P).T))
    put("w3_s", chunks_vec(params["w3_s"]))
    put("lmask", np.full((P, HALO), 0.0 if chunk == 0 else 1.0, np.float32))
    put("rmask", np.full((P, HALO), 0.0 if chunk == 3 else 1.0, np.float32))
    bs = np.zeros((P, 2), np.float32)
    bs[:, b] = 1.0
    put("bsel", bs)
    return sm


def make_in_maps(inputs):
    x = inputs["L0_attn_x"]
    pr = {k[len("L0_attn_"):]: np.asarray(v) for k, v in inputs.items()
          if k.startswith("L0_attn_")}
    fr = {k[len("L0_ffn_"):]: np.asarray(v) for k, v in inputs.items()
          if k.startswith("L0_ffn_")}
    pr.update({k: v for k, v in fr.items()})
    params = pr

    w_shared = dict(
        w_in=_tile_rows(params["in_w"], NG),
        w_actp=_tile_rows(params["actp_w"], NG),
        w_qk=_tile_rows(params["qk_w"], NG),
        w_out=_tile_rows(params["out_w"], NG),
        w_fc1=_oc_slices_kf(params["fc1_w"], NGH, NG),
        w_fc2=_oc_slices_kf(params["fc2_w"], NG, NGH),
        w_w1=_oc_slices(params["w1_w"], NGH, NG),
        w_w2=_oc_slices(params["w2_w"], NGH, NG),
        w_h1=_oc_slices(params["h1_w"], NGH, NGH),
        w_h2=_oc_slices(params["h2_w"], NGH, NGH),
        w_w3=_oc_slices(params["w3_w"], NG, NGH),
    )

    in_maps = []
    for core in range(8):
        b, chunk = core // 4, core % 4
        t0 = chunk * T
        lo, hi = t0 - HALO, t0 + T + HALO
        seg = np.zeros((TP, C), np.float32)
        s0, s1 = max(lo, 0), min(hi, 8192)
        seg[s0 - lo:s1 - lo, :] = x[b, s0:s1, :]
        u_fm = np.ascontiguousarray(seg.T)
        u_t = _tile_rows(u_fm, NG)
        m = dict(u_t=u_t, smalls=make_smalls(params, core), **w_shared)
        in_maps.append(m)
    return in_maps


_CACHE = {}


def kernel(**inputs):
    key = "full"
    if key not in _CACHE:
        _CACHE[key] = build_program()
    nc, meta = _CACHE[key]
    in_maps = make_in_maps(inputs)
    res = run_bass_kernel_spmd(nc, in_maps, core_ids=list(range(8)))
    out = np.empty((2, 8192, 1024), np.float32)
    for core in range(8):
        b, chunk = core // 4, core % 4
        o = res.results[core]["out_t"]
        o = o.reshape(P, NG, T).transpose(1, 0, 2).reshape(C, T)
        out[b, chunk * T:(chunk + 1) * T, :] = o.T
    return out
